# version 02
# baseline (speedup 1.0000x reference)
from contextlib import ExitStack, contextmanager

import numpy as np
import concourse.bass as bass
import concourse.mybir as mybir

F32 = mybir.dt.float32
BF16 = mybir.dt.bfloat16
AF = mybir.ActivationFunctionType
ALU = mybir.AluOpType
AX = mybir.AxisListType

ENGINES = ("sync", "tensor", "vector", "scalar", "gpsimd")
CHAINED = ("vector", "scalar", "gpsimd")

T = 4096
E = 1024
NTB = T // 128


class SemCtr:
    def __init__(self, h):
        self.h = h
        self.v = 0


class Prog:
    def __init__(self):
        self.nc = bass.Bass("TRN2", target_bir_lowering=False)
        self.es = ExitStack()
        self._scopes = [self.es]
        self.sems = {}
        self._uid = 0
        self.bbs = []
        self._consts = {}
        self._const_inits = []
        self._carena = self.es.enter_context(self.nc.sbuf_tensor("cst_arena", [128, 64], F32))
        self.chain = {e: self.sem(f"ch_{e}") for e in ("tensor", "vector", "scalar", "gpsimd")}
        self.qrr = {q: [self.sem(f"dq_{q}{i}") for i in range(4)]
                    for q in ("sync", "gpsimd", "scalar", "vector")}
        self.qrr_i = {q: 0 for q in self.qrr}

    def uid(self):
        self._uid += 1
        return self._uid

    def sem(self, name) -> SemCtr:
        if name not in self.sems:
            h = self.es.enter_context(self.nc.semaphore(name))
            self.sems[name] = SemCtr(h)
        return self.sems[name]

    def constf(self, value, dtype=F32):
        assert dtype == F32
        key = float(value)
        if key not in self._consts:
            i = len(self._consts)
            assert i < 64, "const arena full"
            self._consts[key] = i
            self._const_inits.append((i, float(value)))
        return self._carena[:, self._consts[key]:self._consts[key] + 1]

    def constf_row(self, value, rows=1):
        self.constf(value)
        i = self._consts[float(value)]
        return self._carena[0:rows, i:i + 1]

    def sbuf(self, name, shape, dtype):
        return self._scopes[-1].enter_context(
            self.nc.sbuf_tensor(f"{name}_u{self.uid()}", shape, dtype))

    def psum(self, name, shape, dtype=F32):
        return self._scopes[-1].enter_context(
            self.nc.psum_tensor(f"{name}_u{self.uid()}", shape, dtype))

    @contextmanager
    def scope(self):
        es = ExitStack()
        self._scopes.append(es)
        try:
            yield
        finally:
            self._scopes.pop()
            es.close()

    def dram(self, name, shape, dtype, kind="Internal"):
        return self.nc.dram_tensor(name, shape, dtype, kind=kind)

    def block(self):
        return BB(self)

    def emit_all(self):
        nc = self.nc
        with nc.Block() as blk:
            inits = list(self._const_inits)
            sems = [s.h for s in self.sems.values()]

            def clear(e, sems=sems):
                for h in sems:
                    e.sem_clear(h)
            blk.sync(clear)

            car = self._carena

            def run(e, inits=inits, car=car):
                for i, val in inits:
                    e.memset(car[:, i:i + 1], val)
            blk.vector(run)
        for bb in self.bbs:
            bb.emit()

    def finish(self):
        self.es.close()


class BB:
    def __init__(self, prog: Prog):
        self.prog = prog
        self.ops = {e: [] for e in ENGINES}
        self.used_queues = set()

    def __enter__(self):
        return self

    def __exit__(self, et, ev, tb):
        if et is None:
            self._quiesce = {q: [(c.h, c.v) for c in self.prog.qrr[q] if c.v > 0]
                             for q in self.used_queues}
            self.prog.bbs.append(self)
        return False

    def emit(self):
        nc = self.prog.nc
        for q in sorted(getattr(self, "_quiesce", {})):
            for h, v in self._quiesce[q]:
                self.ops[q].append(lambda e, h=h, v=v: e.wait_ge(h, v))
        with nc.Block() as blk:
            for eng in ENGINES:
                fns = self.ops[eng]
                if fns:
                    def run(e, fns=fns):
                        for f in fns:
                            f(e)
                    getattr(blk, eng)(run)

    def _waits(self, eng, waits):
        for ctr, tgt in waits:
            if tgt is None or tgt <= 0:
                continue
            self.ops[eng].append(lambda e, h=ctr.h, tgt=tgt: e.wait_ge(h, tgt))

    def op(self, eng, fn, waits=(), mark=True, chain=True):
        c = self.prog.chain[eng]
        self._waits(eng, waits)
        if eng in CHAINED:
            selfwait = c.v if chain else 0
            c.v += 1
            my = c.v

            def f(e, fn=fn, selfwait=selfwait, my=my, h=c.h):
                ins = fn(e)
                if selfwait > 0:
                    ins._wait_ge(h, selfwait)
                ins.then_inc(h, 1)
            self.ops[eng].append(f)
            return my
        else:
            if eng == "tensor" and mark:
                c.v += 1
                my = c.v

                def f(e, fn=fn, h=c.h):
                    fn(e).then_inc(h, 1)
                self.ops[eng].append(f)
                return my
            else:
                self.ops[eng].append(lambda e, fn=fn: fn(e))
                return c.v

    def dma(self, eng, out, in_, waits=(), slot=None, **kw):
        prog = self.prog
        if slot is None:
            slot = prog.qrr_i[eng] % 4
            prog.qrr_i[eng] += 1
        c = prog.qrr[eng][slot]
        self._waits(eng, waits)
        prev = c.v
        c.v += 16
        my = c.v
        self.used_queues.add(eng)

        def f(e, out=out, in_=in_, kw=kw, h=c.h, prev=prev):
            ins = e.dma_start(out, in_, **kw)
            if prev > 0:
                ins._wait_ge(h, prev)
            ins.then_inc(h, 16)
        self.ops[eng].append(f)
        return (c, my)

    def wait(self, eng, ctr: SemCtr, tgt):
        self._waits(eng, [(ctr, tgt)])

    def raw(self, eng, fn):
        self.ops[eng].append(fn)


class PsumPool:

    def __init__(self, prog: Prog, name, shape, n, eng, dtype=F32):
        self.prog = prog
        self.eng = eng
        self.tiles = [prog.psum(f"{name}{i}", shape, dtype) for i in range(n)]
        self.n = n
        self.rel = [0] * n
        self.k = 0

    def acquire(self, bb: BB):
        idx = self.k % self.n
        if self.rel[idx] > 0:
            bb.wait("tensor", self.prog.chain[self.eng], self.rel[idx])
        self.k += 1
        return self.tiles[idx], idx

    def produce(self, bb: BB, fn, waits=()):
        return bb.op("tensor", fn, waits=waits, mark=True)

    def consume(self, bb: BB, idx, pe_tgt, fns, waits=()):
        allw = [(self.prog.chain["tensor"], pe_tgt)] + list(waits)
        last = None
        for i, f in enumerate(fns):
            last = bb.op(self.eng, f, waits=allw if i == 0 else ())
        self.rel[idx] = last
        return last


def ceil_div(a, b):
    return -(-a // b)


def wmax(tuples):
    d = {}
    for ctr, tgt in tuples:
        if ctr in d:
            d[ctr] = max(d[ctr], tgt)
        else:
            d[ctr] = tgt
    return [(c, t) for c, t in d.items()]


def load_w_bf16(bb: BB, prog: Prog, hbm, name="w", queue="gpsimd", tiles=None, waits=()):
    K, M = hbm.shape
    kt = ceil_div(K, 128)
    if tiles is None:
        tiles = [prog.sbuf(f"{name}{i}", [128, M], BF16) for i in range(kt)]
    tgts = []
    for i in range(kt):
        p = min(128, K - i * 128)
        tgts.append(bb.dma(queue, tiles[i][:p, :], hbm[i * 128:i * 128 + p, :],
                           waits=waits if i == 0 else ()))
    return tiles, wmax(tgts)


def ln_to_xnT(bb: BB, prog: Prog, x_hbm, xnT, ident,
              eps=1e-5, pad_left=0,
              xn_f32_hbm=None, xn_bf16_hbm=None, tok_cb=None, second_norm=None,
              n_tb=NTB, x_row_base=0, rinv_out=None, pre_f32_hbm=None):
    u = prog.uid()
    ve = prog.chain["vector"]
    act = prog.chain["scalar"]
    pe = prog.chain["tensor"]
    xt = [prog.sbuf(f"lnx{u}_{i}", [128, E], F32) for i in range(2)]
    sq = prog.sbuf(f"lnsq{u}", [128, E], F32)
    xnf = [prog.sbuf(f"lnxn{u}_{i}", [128, E], F32) for i in range(2)]
    xnb = [prog.sbuf(f"lnxb{u}_{i}", [128, E], BF16) for i in range(2)]
    stat = prog.sbuf(f"lnstat{u}", [128, 8], F32)
    tpA = PsumPool(prog, f"lntpA{u}", [128, 128], 2, "scalar", BF16)
    tpB = PsumPool(prog, f"lntpB{u}", [128, 128], 2, "vector", BF16)

    info = {}
    hist = {}

    for tb in range(n_tb):
        buf = tb % 2
        w = []
        if tb >= 2:
            w.append((ve, hist[tb - 2]["ve"]))
            w.append((act, hist[tb - 2]["sa_sq"]))
        ldt = bb.dma("sync", xt[buf][:, :],
                     x_hbm[x_row_base + tb * 128: x_row_base + (tb + 1) * 128, :], waits=w)

        sa_sq = bb.op("scalar", lambda e, b=buf: e.activation(
            sq[:, :], xt[b][:, :], AF.Square, accum_out=stat[:, 1:2]),
            waits=[ldt])
        bb.op("vector", lambda e, b=buf: e.reduce_sum(stat[:, 0:1], xt[b][:, :], axis=AX.X),
              waits=[ldt])
        bb.op("vector", lambda e: e.tensor_scalar_mul(stat[:, 2:3], stat[:, 0:1], 1.0 / E))
        bb.op("vector", lambda e: e.tensor_scalar_mul(stat[:, 3:4], stat[:, 1:2], 1.0 / E),
              waits=[(act, sa_sq)])
        bb.op("vector", lambda e: e.tensor_tensor(stat[:, 6:7], stat[:, 2:3], stat[:, 2:3], ALU.mult))
        va1 = bb.op("vector", lambda e: e.tensor_tensor(stat[:, 3:4], stat[:, 3:4], stat[:, 6:7], ALU.subtract))
        bb.op("scalar", lambda e, _b=prog.constf(eps): e.activation(
            stat[:, 4:5], stat[:, 3:4], AF.Ln, bias=_b), waits=[(ve, va1)])
        sa2 = bb.op("scalar", lambda e: e.activation(stat[:, 4:5], stat[:, 4:5], AF.Exp, scale=-0.5))
        w = [(act, sa2)]
        if xn_f32_hbm is not None and tb >= 2:
            w.append(hist[tb - 2]["st_f32"])
        va2 = bb.op("vector", lambda e, b=buf: e.tensor_scalar(
            xnf[b][:, :], xt[b][:, :], stat[:, 2:3], stat[:, 4:5], ALU.subtract, ALU.mult),
            waits=w)

        if pre_f32_hbm is not None:
            hist_pre = hist.get(tb - 2, {}).get("st_pre")
            if hist_pre is not None:
                bb.wait("sync", *hist_pre)
            st_pre = bb.dma("sync", pre_f32_hbm[tb * 128:(tb + 1) * 128, :], xnf[buf][:, :],
                            waits=[(ve, va2)])
        else:
            st_pre = None

        if second_norm is not None:
            kind, eps2 = second_norm
            sa3 = bb.op("scalar", lambda e, b=buf: e.activation(
                sq[:, :], xnf[b][:, :], AF.Square, accum_out=stat[:, 5:6]),
                waits=[(ve, va2)])
            if pre_f32_hbm is not None:
                bb.wait("vector", *st_pre)
            bb.op("vector", lambda e: e.tensor_scalar_mul(stat[:, 5:6], stat[:, 5:6], 1.0 / E),
                  waits=[(act, sa3)])
            va3 = bb.op("vector", lambda e: e.tensor_copy(stat[:, 6:7], stat[:, 5:6]))
            if kind == "rms_plain":
                bb.op("scalar", lambda e: e.activation(stat[:, 6:7], stat[:, 6:7], AF.Ln),
                      waits=[(ve, va3)])
                sa4 = bb.op("scalar", lambda e: e.activation(stat[:, 6:7], stat[:, 6:7], AF.Exp, scale=0.5))
                bb.op("vector", lambda e, z=float(eps2): e.tensor_scalar_add(stat[:, 6:7], stat[:, 6:7], z),
                      waits=[(act, sa4)])
                bb.op("vector", lambda e: e.reciprocal(stat[:, 6:7], stat[:, 6:7]))
            elif kind == "rms_eps_inside":
                bb.op("scalar", lambda e, z=prog.constf(eps2): e.activation(
                    stat[:, 7:8], stat[:, 6:7], AF.Ln, bias=z), waits=[(ve, va3)])
                if rinv_out is not None:
                    bb.op("scalar", lambda e, tb=tb: e.activation(
                        rinv_out[:, tb:tb + 1], stat[:, 7:8], AF.Exp, scale=0.5))
                sa4 = bb.op("scalar", lambda e: e.activation(stat[:, 6:7], stat[:, 7:8], AF.Exp, scale=-0.5))
                bb.wait("vector", act, sa4)
            else:
                raise ValueError(kind)
            va2 = bb.op("vector", lambda e, b=buf: e.tensor_scalar_mul(
                xnf[b][:, :], xnf[b][:, :], stat[:, 6:7]))

        st_f32 = None
        if xn_f32_hbm is not None:
            st_f32 = bb.dma("sync", xn_f32_hbm[tb * 128:(tb + 1) * 128, :], xnf[buf][:, :],
                            waits=[(ve, va2)])
        if tok_cb is not None:
            tok_cb(bb, tb, xnf[buf], va2)

        w = [(pe, hist[tb - 2]["pe_tr"])] if tb >= 2 else []
        if xn_bf16_hbm is not None and tb >= 2:
            w.append(hist[tb - 2]["st_bf"])
        nb1 = bb.op("vector", lambda e, b=buf: e.tensor_copy(xnb[b][:, :], xnf[b][:, :]), waits=w)
        st_bf = None
        if xn_bf16_hbm is not None:
            st_bf = bb.dma("sync", xn_bf16_hbm[tb * 128:(tb + 1) * 128, :], xnb[buf][:, :],
                           waits=[(ve, nb1)])

        pe_tr = None
        for et in range(8):
            pool = tpA if et % 2 == 0 else tpB
            eng = "scalar" if et % 2 == 0 else "vector"
            ps, idx = pool.acquire(bb)
            pe_tr = pool.produce(bb, lambda e, b=buf, et=et, ps=ps: e.transpose(
                ps[:, :], xnb[b][:, et * 128:(et + 1) * 128], ident[:, :]),
                waits=[(ve, nb1)] if et == 0 else ())

            def evac(e, et=et, tb=tb, ps=ps, eng=eng):
                dst = xnT[et][:, pad_left + tb * 128: pad_left + (tb + 1) * 128]
                return e.copy(dst, ps[:, :]) if eng == "scalar" else e.tensor_copy(dst, ps[:, :])
            pool.consume(bb, idx, pe_tr, [evac])

        hist[tb] = dict(ve=va2, sa_sq=sa_sq, pe_tr=pe_tr, st_f32=st_f32, st_bf=st_bf,
                        st_pre=st_pre)
        info[tb] = hist[tb]

    info["pe_done"] = prog.chain["tensor"].v
    info["ve_done"] = prog.chain["vector"].v
    info["act_done"] = prog.chain["scalar"].v
    return info


def mm_feature_major(bb: BB, prog: Prog, w_tiles, rhs_tiles, n_rhs_cols,
                     pool: PsumPool, writer, m_blocks=None, rhs_col_base=0,
                     chunk=512, extra_mm=None, waits_first=()):
    kt = len(w_tiles)
    M = w_tiles[0].shape[1]
    if m_blocks is None:
        m_blocks = ceil_div(M, 128)
    first = True
    for mb in range(m_blocks):
        mw = min(128, M - mb * 128)
        for c0 in range(0, n_rhs_cols, chunk):
            c1 = min(c0 + chunk, n_rhs_cols)
            ps, idx = pool.acquire(bb)
            w = list(waits_first) if first else []
            first = False
            for k in range(kt):
                last = (k == kt - 1) and extra_mm is None
                fn = lambda e, k=k, mb=mb, c0=c0, c1=c1, ps=ps, last=last, mw=mw: e.matmul(
                    ps[:mw, :c1 - c0],
                    w_tiles[k][:, mb * 128:mb * 128 + mw],
                    rhs_tiles[k][:, rhs_col_base + c0: rhs_col_base + c1],
                    start=(k == 0), stop=last)
                if last:
                    tgt = pool.produce(bb, fn, waits=w)
                else:
                    bb.op("tensor", fn, waits=w, mark=False)
                    w = []
            if extra_mm is not None:
                tgt = extra_mm(bb, mb, c0, c1, ps)
            writer(bb, mb, c0, c1, ps, idx, tgt)


def mm_token_major(bb: BB, prog: Prog, actT_tiles, w_tiles, F_cols,
                   pool: PsumPool, writer, n_tb=NTB, act_col_base=0, chunk=512,
                   waits_first=(), extra_pairs=None):
    kt = len(actT_tiles)
    first = True
    for tb in range(n_tb):
        ex = extra_pairs(tb) if extra_pairs is not None else []
        for c0 in range(0, F_cols, chunk):
            c1 = min(c0 + chunk, F_cols)
            ps, idx = pool.acquire(bb)
            w = list(waits_first) if first else []
            w += [t for _, _, t in ex if t is not None]
            first = False
            net = len(ex)
            for k in range(kt + net):
                last = (k == kt + net - 1)
                if k < kt:
                    fn = lambda e, k=k, tb=tb, c0=c0, c1=c1, ps=ps, last=last: e.matmul(
                        ps[:, :c1 - c0],
                        actT_tiles[k][:, act_col_base + tb * 128: act_col_base + (tb + 1) * 128],
                        w_tiles[k][:, c0:c1],
                        start=(k == 0), stop=last)
                else:
                    fn = lambda e, k=k, c0=c0, c1=c1, ps=ps, last=last,                         ap=ex[k - kt][0], wt=ex[k - kt][1]: e.matmul(
                        ps[:, :c1 - c0], ap, wt[:, c0:c1], start=(k == 0), stop=last)
                if last:
                    tgt = pool.produce(bb, fn, waits=wmax(w))
                else:
                    bb.op("tensor", fn, waits=wmax(w), mark=False)
                    w = []
            writer(bb, tb, c0, c1, ps, idx, tgt)


def run_sim(prog: Prog, inputs: dict):
    from concourse.bass_interp import CoreSim
    sim = CoreSim(prog.nc, require_finite=False, require_nnan=False)
    for k, v in inputs.items():
        sim.tensor(k)[:] = v
    sim.simulate()
    return sim


def host_consts():
    import ml_dtypes
    c = {}
    c["cst_ident"] = np.eye(128, dtype=ml_dtypes.bfloat16)
    u = np.arange(1536)[None, :] - 640
    p = np.arange(128)[:, None]
    c["cst_tric"] = (p <= u).astype(ml_dtypes.bfloat16)
    c["cst_band"] = ((u - p > 0) & (u - p <= 128)).astype(ml_dtypes.bfloat16)
    c["cst_iotaq"] = np.broadcast_to(np.arange(512, dtype=np.float32)[None, :],
                                     (64, 512)).copy()
    c["cst_latb"] = (64.0 * (np.arange(64, dtype=np.float32)[:, None] + 1))
    c["cst_kpos32"] = (np.arange(128)[:, None] + 128 * np.arange(32)[None, :]).astype(np.float32)
    mA = np.zeros((1, 128), np.float32); mA[0, :64] = 1
    mB = np.zeros((1, 128), np.float32); mB[0, 64:] = 1
    c["cst_maskA"] = mA
    c["cst_maskB"] = mB
    c["cst_onesrow"] = np.ones((1, 128), np.float32)
    c["cst_ones128f"] = np.ones((128, 1), np.float32)
    c["cst_identf"] = np.eye(128, dtype=np.float32)
    c["cst_ones128"] = np.ones((128, 1), dtype=ml_dtypes.bfloat16)
    c["cst_iota128"] = np.arange(128, dtype=np.float32)[:, None]
    return c


def load_consts(bb: BB, prog: Prog):
    shapes = {"cst_ident": ([128, 128], BF16), "cst_tric": ([128, 1536], BF16),
              "cst_kpos32": ([128, 32], F32), "cst_maskA": ([1, 128], F32),
              "cst_maskB": ([1, 128], F32), "cst_ones128": ([128, 1], BF16),
              "cst_iota128": ([128, 1], F32), "cst_onesrow": ([1, 128], F32),
              "cst_ones128f": ([128, 1], F32), "cst_identf": ([128, 128], F32),
              "cst_band": ([128, 1536], BF16), "cst_iotaq": ([64, 512], F32),
              "cst_latb": ([64, 1], F32)}
    out = {}
    for name, (shape, dt) in shapes.items():
        t = prog.es.enter_context(prog.nc.sbuf_tensor(f"{name}_u{prog.uid()}", shape, dt))
        hbm = prog.dram(name, shape, dt, kind="ExternalInput")
        bb.dma("sync", t[:, :], hbm[:, :])
        out[name] = t
    prog.consts_tiles = out
    return out


def attention_heads(prog: Prog, *, q_stage, kT_ap, vaug, oT_stage, H, dh, dv,
                    score_scale, cconst=20.0, causal=False,
                    neg_slope_col=None, wmult_ap=None, mask_ap=None,
                    kb_range=None, denom_eps=0.0, n_tok=T, qchunk=512,
                    head_pre=None):
    assert dv + 1 <= 128
    u = prog.uid()
    ve, act, pe = prog.chain["vector"], prog.chain["scalar"], prog.chain["tensor"]
    nkb_all = n_tok // 128
    nqc = ceil_div(n_tok, qchunk)
    csts = prog.consts_tiles
    negc = prog.constf(-cconst)
    if kb_range is None:
        kbs_of = (lambda q0, qcw: range(ceil_div(q0 + qcw, 128))) if causal else (
            lambda q0, qcw: range(nkb_all))
    else:
        kbs_of = kb_range

    with prog.scope():
        qh = prog.sbuf(f"ah_q{u}", [128, n_tok], BF16)
        oh = prog.sbuf(f"ah_o{u}", [max(dv, 64), n_tok], BF16)
        expT = [prog.sbuf(f"ah_e{u}_{i}", [128, qchunk], BF16) for i in range(3)]
        drow = prog.sbuf(f"ah_d{u}", [1, n_tok], F32)
        bias_h = prog.sbuf(f"ah_b{u}", [128, 32], F32)
        spool = PsumPool(prog, f"ah_s{u}", [128, qchunk], 2, "scalar")
        avpool = PsumPool(prog, f"ah_av{u}", [dv + 1, qchunk], 2, "vector")
        rdpool = PsumPool(prog, f"ah_rd{u}", [dv, qchunk], 2, "vector")

        for h in range(H):
            qb0 = (h * dh) % 128
            with prog.block() as bb:
                ldq = bb.dma("sync", qh[qb0:qb0 + dh, :], q_stage[h * dh:(h + 1) * dh, :])
                if head_pre is not None:
                    head_pre(bb, h)
                if neg_slope_col is not None:
                    bias_ready = bb.op("vector", lambda e, h=h, _s=neg_slope_col(h), _c=negc:
                                       e.tensor_scalar(bias_h[:, :], csts["cst_kpos32"][:, :],
                                                       _s, _c, ALU.mult, ALU.add))
                else:
                    bias_ready = None
                eidx = 0
                ebuf_tgt = {}
                for qc in range(nqc):
                    q0 = qc * qchunk
                    qcw = min(qchunk, n_tok - q0)
                    kbs = list(kbs_of(q0, qcw))
                    ps_av, av_idx = avpool.acquire(bb)
                    for j, kb in enumerate(kbs):
                        ps_s, s_idx = spool.acquire(bb)
                        w = [ldq] if (qc == 0 and j == 0) else []
                        s_tgt = spool.produce(bb, lambda e, kb=kb, q0=q0, qcw=qcw, ps_s=ps_s, h=h, qb0=qb0:
                                              e.matmul(ps_s[:, :qcw],
                                                       kT_ap(h)[:, kb * 128:(kb + 1) * 128],
                                                       qh[qb0:qb0 + dh, q0:q0 + qcw],
                                                       start=True, stop=True),
                                              waits=w)
                        eb = expT[eidx % 3]
                        ew = []
                        if eidx >= 3:
                            ew.append((pe, ebuf_tgt[eidx - 3]))
                        if bias_ready is not None:
                            bias_ap = bias_h[:, kb:kb + 1]
                            if eidx == 0:
                                ew.append((ve, bias_ready))
                        else:
                            bias_ap = negc
                        e_tgt = spool.consume(bb, s_idx, s_tgt, [
                            lambda e, ps_s=ps_s, eb=eb, qcw=qcw, bias_ap=bias_ap: e.activation(
                                eb[:, :qcw], ps_s[:, :qcw], AF.Exp, bias=bias_ap,
                                scale=float(score_scale))], waits=ew)
                        av_w = (act, e_tgt)
                        m = mask_ap(h, kb, q0, qcw) if mask_ap is not None else None
                        if m is not None:
                            mv = bb.op("vector", lambda e, eb=eb, qcw=qcw, m=m: e.tensor_tensor(
                                eb[:, :qcw], eb[:, :qcw], m, ALU.mult), waits=[(act, e_tgt)])
                            av_w = (ve, mv)
                        wm = wmult_ap(h, kb, q0, qcw) if wmult_ap is not None else None
                        if wm is not None:
                            mv = bb.op("vector", lambda e, eb=eb, qcw=qcw, wm=wm: e.tensor_tensor(
                                eb[:, :qcw], eb[:, :qcw], wm, ALU.mult),
                                waits=[av_w] if av_w[0] is act else [])
                            av_w = (ve, mv)
                        av_tgt = bb.op("tensor", lambda e, h=h, kb=kb, eb=eb, qcw=qcw, ps_av=ps_av,
                                       j=j, last=(j == len(kbs) - 1): e.matmul(
                            ps_av[:, :qcw], vaug[kb][:, h * (dv + 1): (h + 1) * (dv + 1)],
                            eb[:, :qcw], start=(j == 0), stop=last),
                            waits=[av_w], mark=True)
                        ebuf_tgt[eidx] = av_tgt
                        eidx += 1
                    last_evac = avpool.consume(bb, av_idx, av_tgt, [
                        lambda e, q0=q0, qcw=qcw, ps_av=ps_av: e.tensor_copy(
                            oh[0:dv, q0:q0 + qcw], ps_av[0:dv, :qcw]),
                        lambda e, q0=q0, qcw=qcw, ps_av=ps_av: e.tensor_copy(
                            drow[:, q0:q0 + qcw], ps_av[dv:dv + 1, :qcw]),
                    ])
                bb.op("scalar", lambda e, z=prog.constf_row(float(denom_eps)): e.activation(
                    drow[:, :], drow[:, :], AF.Ln, bias=z), waits=[(ve, last_evac)])
                rcp = bb.op("scalar", lambda e: e.activation(
                    drow[:, :], drow[:, :], AF.Exp, scale=-1.0))
                for qc in range(nqc):
                    q0 = qc * qchunk
                    qcw = min(qchunk, n_tok - q0)
                    ps_rd, rd_idx = rdpool.acquire(bb)
                    rd_tgt = rdpool.produce(bb, lambda e, q0=q0, qcw=qcw, ps_rd=ps_rd: e.matmul(
                        ps_rd[:, :qcw], csts["cst_onesrow"][:, 0:dv], drow[:, q0:q0 + qcw],
                        start=True, stop=True), waits=[(act, rcp)] if qc == 0 else ())
                    rdpool.consume(bb, rd_idx, rd_tgt, [
                        lambda e, q0=q0, qcw=qcw, ps_rd=ps_rd: e.tensor_tensor(
                            oh[0:dv, q0:q0 + qcw], oh[0:dv, q0:q0 + qcw], ps_rd[:, :qcw], ALU.mult)])
                bb.dma("sync", oT_stage[h * dv:(h + 1) * dv, :], oh[0:dv, :],
                       waits=[(ve, prog.chain["vector"].v)])


def out_proj_residual(prog: Prog, *, act_stage, n_act, wo_tiles, x_hbm, xout_hbm,
                      row_bias_rep=None, alpha_col=None, extra_add_hbm=None,
                      n_tok=T, tb_per_blk=4, xout_fn=None, blk_pre=None,
                      extra_pairs=None):
    u = prog.uid()
    ve = prog.chain["vector"]
    EA = n_act // 128
    with prog.scope():
        a_ch = [prog.sbuf(f"op_a{u}_{i}", [128, tb_per_blk * 128], BF16) for i in range(EA)]
        xold = [prog.sbuf(f"op_x{u}_{i}", [128, E], F32) for i in range(tb_per_blk)]
        xex = [prog.sbuf(f"op_e{u}_{i}", [128, E], F32) for i in range(tb_per_blk)] if extra_add_hbm is not None else None
        ytile = [prog.sbuf(f"op_y{u}_{i}", [128, 512], F32) for i in range(4)]
        pool = PsumPool(prog, f"op_p{u}", [128, 512], 2, "vector")

        for blk0 in range(0, n_tok // 128, tb_per_blk):
            with prog.block() as bb:
                lda = []
                for i in range(EA):
                    lda.append(bb.dma("sync", a_ch[i][:, :],
                                      act_stage[i * 128:(i + 1) * 128,
                                                blk0 * 128:(blk0 + tb_per_blk) * 128]))
                ldr = []
                for j in range(tb_per_blk):
                    tb = blk0 + j
                    ldr.append(bb.dma("sync", xold[j][:, :], x_hbm[tb * 128:(tb + 1) * 128, :]))
                    if extra_add_hbm is not None:
                        ldr.append(bb.dma("sync", xex[j][:, :],
                                          extra_add_hbm[tb * 128:(tb + 1) * 128, :]))
                cnt = [0]
                st_hist = {}

                def owriter(bb, tb_loc, c0, c1, ps, idx, tgt):
                    k = cnt[0] % 4
                    tb = blk0 + tb_loc
                    fns = []
                    if xout_fn is not None:
                        xout_fn(bb, pool, tb, tb_loc, c0, c1, ps, idx, tgt, xold, ytile, k, ldr)
                        cnt[0] += 1
                        return
                    if alpha_col is not None:
                        fns.append(lambda e, ps=ps, k=k, c0=c0, c1=c1, tb_loc=tb_loc:
                                   e.scalar_tensor_tensor(ytile[k][:, :c1 - c0], ps[:, :c1 - c0],
                                                          alpha_col, xold[tb_loc][:, c0:c1],
                                                          ALU.mult, ALU.add))
                    elif row_bias_rep is not None:
                        fns.append(lambda e, ps=ps, k=k, c0=c0, c1=c1: e.tensor_tensor(
                            ytile[k][:, :c1 - c0], ps[:, :c1 - c0], row_bias_rep[:, c0:c1], ALU.add))
                        fns.append(lambda e, k=k, c0=c0, c1=c1, tb_loc=tb_loc: e.tensor_tensor(
                            ytile[k][:, :c1 - c0], ytile[k][:, :c1 - c0], xold[tb_loc][:, c0:c1], ALU.add))
                    else:
                        fns.append(lambda e, ps=ps, k=k, c0=c0, c1=c1, tb_loc=tb_loc: e.tensor_tensor(
                            ytile[k][:, :c1 - c0], ps[:, :c1 - c0], xold[tb_loc][:, c0:c1], ALU.add))
                    if extra_add_hbm is not None:
                        fns.append(lambda e, k=k, c0=c0, c1=c1, tb_loc=tb_loc: e.tensor_tensor(
                            ytile[k][:, :c1 - c0], ytile[k][:, :c1 - c0], xex[tb_loc][:, c0:c1], ALU.add))
                    wl_ = wmax(ldr)
                    if cnt[0] >= 4:
                        wl_ = wl_ + [st_hist[cnt[0] - 4]]
                    last = pool.consume(bb, idx, tgt, fns, waits=wl_)
                    st_hist[cnt[0]] = bb.dma("sync", xout_hbm[tb * 128:(tb + 1) * 128, c0:c1],
                           ytile[k][:, :c1 - c0], waits=[(ve, last)])
                    cnt[0] += 1

                if blk_pre is not None:
                    blk_pre(bb, blk0, a_ch, wmax(lda))
                mm_token_major(bb, prog, a_ch, wo_tiles, E, pool, owriter,
                               n_tb=tb_per_blk, waits_first=wmax(lda),
                               extra_pairs=extra_pairs)


def stage_fm_writer(prog, pool, stage_hbm, tmp_tiles, bias_tile=None, scale=None):
    n_tmp = len(tmp_tiles)
    st_hist = {}

    def writer(bb, mb, c0, c1, ps, idx, tgt):
        tt = tmp_tiles[mb % n_tmp]
        w = []
        if c0 == 0 and mb >= n_tmp:
            w.append(st_hist[mb - n_tmp])
        if bias_tile is not None:
            fn = lambda e, mb=mb, c0=c0, c1=c1, ps=ps, tt=tt: e.tensor_scalar(
                tt[:, c0:c1], ps[:, :c1 - c0], bias_tile[:, mb:mb + 1], None, ALU.add)
        elif scale is not None:
            fn = lambda e, mb=mb, c0=c0, c1=c1, ps=ps, tt=tt, s=float(scale): e.tensor_scalar_mul(
                tt[:, c0:c1], ps[:, :c1 - c0], s)
        else:
            fn = lambda e, mb=mb, c0=c0, c1=c1, ps=ps, tt=tt: e.tensor_copy(
                tt[:, c0:c1], ps[:, :c1 - c0])
        last = pool.consume(bb, idx, tgt, fns=[fn], waits=w)
        if c1 == tt.shape[1] or c1 == stage_hbm.shape[1]:
            st_hist[mb] = bb.dma("sync", stage_hbm[mb * 128:(mb + 1) * 128, :c1],
                                 tt[:, :c1], waits=[(prog.chain[pool.eng], last)])
    return writer


def causal_mask_ap(prog, qchunk=512):
    tric = prog.consts_tiles["cst_tric"]

    def f(h, kb, q0, qcw):
        if kb * 128 + 127 <= q0:
            return None
        off = 640 + q0 - kb * 128
        return tric[:, off:off + qcw]
    return f


def mm_feature_major_streamed(bb, prog, w_tiles, xn_stage, n_feat, n_tok,
                              pool, writer, m_blocks=None, chunk=512,
                              extra_rhs=None, extra_w=None, waits_first=(),
                              ld_bufs=2, chunk_cb=None, xb=None, pre_chunk=None):
    u = prog.uid()
    EQ = n_feat // 128
    if m_blocks is None:
        m_blocks = w_tiles[0].shape[1] // 128
    if xb is None:
        xb = [[prog.sbuf(f"fms{u}_{b}_{i}", [128, chunk], BF16) for i in range(EQ)]
              for b in range(ld_bufs)]
    ld_bufs = len(xb)
    first = True
    ld_hist = {}
    for ci, c0 in enumerate(range(0, n_tok, chunk)):
        c1 = min(c0 + chunk, n_tok)
        b = ci % ld_bufs
        w = []
        if ci >= ld_bufs:
            w = [(prog.chain["tensor"], ld_hist[ci - ld_bufs])]
        lds = []
        for i in range(EQ):
            lds.append(bb.dma("sync", xb[b][i][:, :c1 - c0],
                              xn_stage[i * 128:(i + 1) * 128, c0:c1],
                              waits=w if i == 0 else ()))
        if pre_chunk is not None:
            pre_chunk(bb, ci, c0, c1, xb[b], lds)
        ex = extra_rhs(c0, c1) if extra_rhs is not None else []
        for mb in range(m_blocks):
            pl = pool(mb) if callable(pool) else pool
            ps, idx = pl.acquire(bb)
            wf = wmax(list(lds) + [t for _, t in ex if t is not None])
            if first:
                wf += list(waits_first)
                first = False
            kt = len(w_tiles)
            net = len(ex)
            for k in range(kt + net):
                last = (k == kt + net - 1)
                if k < kt:
                    fn = lambda e, k=k, mb=mb, ps=ps, cw=c1 - c0, b=b, last=last: e.matmul(
                        ps[:, :cw], w_tiles[k][:, mb * 128:(mb + 1) * 128],
                        xb[b][k][:, :cw], start=(k == 0), stop=last)
                else:
                    fn = lambda e, k=k, mb=mb, ps=ps, cw=c1 - c0, last=last, ap=ex[k - kt][0]: e.matmul(
                        ps[:, :cw], extra_w[k - kt][:, mb * 128:(mb + 1) * 128],
                        ap, start=(k == 0), stop=last)
                if last:
                    tgt = bb.op("tensor", fn, waits=wf if k == 0 else (), mark=True)
                else:
                    bb.op("tensor", fn, waits=wf if k == 0 else (), mark=False)
                wf = ()
            ld_hist[ci] = tgt
            writer(bb, mb, c0, c1, ps, idx, tgt)
        if chunk_cb is not None:
            t2 = chunk_cb(bb, ci, c0, c1)
            if t2 is not None:
                ld_hist[ci] = t2


def mm_token_major_streamed(bb, prog, xn_stage, n_feat, w_tiles, F_cols,
                            pool, writer, n_tb=NTB, chunk=512, ld_bufs=2,
                            extra_act=None, extra_w=None, waits_first=(),
                            pre_tb=None):
    u = prog.uid()
    EA = n_feat // 128
    ab = [prog.sbuf(f"tms{u}_{b}", [128, EA * 128], BF16) for b in range(ld_bufs)]
    first = True
    ld_hist = {}
    for tb in range(n_tb):
        b = tb % ld_bufs
        w = []
        if tb >= ld_bufs:
            w = [(prog.chain["tensor"], ld_hist[tb - ld_bufs])]
        src = xn_stage.rearrange("(e p) t -> p e t", p=128)[:, :, tb * 128:(tb + 1) * 128]
        ld = bb.dma("sync", ab[b][:, :].rearrange("p (e c) -> p e c", c=128), src, waits=w)
        if pre_tb is not None:
            pre_tb(bb, tb, ab[b], ld)
        ex = extra_act(tb) if extra_act is not None else []
        for c0 in range(0, F_cols, chunk):
            c1 = min(c0 + chunk, F_cols)
            ps, idx = pool.acquire(bb)
            wf = [ld] + [t for _, t in ex if t is not None]
            if first:
                wf += list(waits_first)
                first = False
            kt = EA
            net = len(ex)
            for k in range(kt + net):
                last = (k == kt + net - 1)
                if k < kt:
                    fn = lambda e, k=k, ps=ps, c0=c0, c1=c1, b=b: e.matmul(
                        ps[:, :c1 - c0], ab[b][:, k * 128:(k + 1) * 128],
                        w_tiles[k][:, c0:c1], start=(k == 0), stop=False if k < kt + net - 1 else True)
                else:
                    fn = lambda e, k=k, ps=ps, c0=c0, c1=c1, ap=ex[k - kt][0], last=last: e.matmul(
                        ps[:, :c1 - c0], ap, extra_w[k - kt][:, c0:c1],
                        start=(k == 0), stop=last)
                if last:
                    tgt = bb.op("tensor", fn, waits=wmax(wf) if k == 0 else (), mark=True)
                else:
                    bb.op("tensor", fn, waits=wmax(wf) if k == 0 else (), mark=False)
                wf = []
            ld_hist[tb] = tgt
            writer(bb, tb, c0, c1, ps, idx, tgt)


def stage_fm_chunk_writer(prog, pool, stage_hbm, bias_tile=None, scale=None, n_tmp=4):
    u = prog.uid()
    tmp = [prog.sbuf(f"sfc{u}_{i}", [128, 512], BF16) for i in range(n_tmp)]
    cnt = [0]
    st_hist = {}

    def writer(bb, mb, c0, c1, ps, idx, tgt):
        k = cnt[0] % n_tmp
        w = [st_hist[cnt[0] - n_tmp]] if cnt[0] >= n_tmp else []
        if bias_tile is not None:
            fn = lambda e, mb=mb, c0=c0, c1=c1, ps=ps, k=k: e.tensor_scalar(
                tmp[k][:, :c1 - c0], ps[:, :c1 - c0], bias_tile[:, mb:mb + 1], None, ALU.add)
        elif scale is not None:
            fn = lambda e, c0=c0, c1=c1, ps=ps, k=k, sc=float(scale): e.tensor_scalar_mul(
                tmp[k][:, :c1 - c0], ps[:, :c1 - c0], sc)
        else:
            fn = lambda e, c0=c0, c1=c1, ps=ps, k=k: e.tensor_copy(
                tmp[k][:, :c1 - c0], ps[:, :c1 - c0])
        last = pool.consume(bb, idx, tgt, [fn], waits=w)
        st_hist[cnt[0]] = bb.dma("sync", stage_hbm[mb * 128:(mb + 1) * 128, c0:c1],
                                 tmp[k][:, :c1 - c0], waits=[(prog.chain[pool.eng], last)])
        cnt[0] += 1
    return writer


def add_residual_pass(prog: Prog, x_hbm, y_hbm, xout_hbm, n_tok=T, y_dtype=F32):
    u = prog.uid()
    ve = prog.chain["vector"]
    with prog.scope():
        xa = [prog.sbuf(f"arp_x{u}_{i}", [128, E], F32) for i in range(2)]
        ya = [prog.sbuf(f"arp_y{u}_{i}", [128, E], y_dtype) for i in range(2)]
        with prog.block() as bb:
            hist = {}
            for tb in range(n_tok // 128):
                b = tb % 2
                w = [(ve, hist[tb - 2][0])] if tb >= 2 else []
                if tb >= 2:
                    w.append(hist[tb - 2][1])
                l1 = bb.dma("sync", xa[b][:, :], x_hbm[tb * 128:(tb + 1) * 128, :], waits=w)
                l2 = bb.dma("sync", ya[b][:, :], y_hbm[tb * 128:(tb + 1) * 128, :])
                a = bb.op("vector", lambda e, b=b: e.tensor_add(xa[b][:, :], xa[b][:, :], ya[b][:, :]),
                          waits=wmax([l1, l2]))
                st = bb.dma("sync", xout_hbm[tb * 128:(tb + 1) * 128, :], xa[b][:, :],
                            waits=[(ve, a)])
                hist[tb] = (a, st)


def ffn_expert_pass(bb, prog, ctx, *, xn_stage, w1_tiles, w2_tiles, swiglu,
                    wcomb_col, y_hbm, first_expert, inter=2048, n_tok=T,
                    tc_chunk=512):
    act, ve, pe = prog.chain["scalar"], prog.chain["vector"], prog.chain["tensor"]
    IMB = inter // 128
    pA, pB, ypool = ctx["pA"], ctx["pB"], ctx["ypool"]
    h1sb, hT, ytile = ctx["h1sb"], ctx["hT"], ctx["ytile"]
    h1_tgt = {}
    hT_tgt = {}
    y_done = ctx.setdefault("y_done", [0])
    ycnt = ctx.setdefault("ycnt", [0])
    yst = ctx.setdefault("yst", {})

    def hwriter(bb, mb, c0, c1, ps, idx, tgt):
        w = [(pe, y_done[0])] if (mb == 0 or mb == IMB) and y_done[0] else []
        if swiglu and mb >= IMB:
            m2 = mb - IMB
            hT_tgt[m2] = pB.consume(bb, idx, tgt, [
                lambda e, m2=m2, ps=ps, cw=c1 - c0: e.tensor_tensor(
                    hT[m2][:, :cw], ps[:, :cw], h1sb[m2][:, :cw], ALU.mult)],
                waits=w + [(act, h1_tgt[m2])])
        elif swiglu:
            h1_tgt[mb] = pA.consume(bb, idx, tgt, [
                lambda e, mb=mb, ps=ps, cw=c1 - c0: e.activation(
                    h1sb[mb][:, :cw], ps[:, :cw], AF.Silu)], waits=w)
        else:
            hT_tgt[mb] = pA.consume(bb, idx, tgt, [
                lambda e, mb=mb, ps=ps, cw=c1 - c0: e.activation(
                    hT[mb][:, :cw], ps[:, :cw], AF.Silu)], waits=w)

    def chunk_cb(bb, ci, c0, c1):
        tbs = (c1 - c0) // 128
        hready = [(pB if swiglu else pA).prog.chain[(pB if swiglu else pA).eng],
                  max(hT_tgt.values())] if hT_tgt else None

        def ywriter(bb, tb_loc, y0, y1, ps, idx, tgt):
            tbg = (c0 + tb_loc * 128) // 128
            k = ycnt[0] % 4
            w = [yst[ycnt[0] - 4]] if ycnt[0] >= 4 else []
            col = wcomb_col(tbg) if wcomb_col is not None else None
            if col is not None:
                fn = lambda e, ps=ps, k=k, y0=y0, y1=y1, col=col: e.tensor_scalar_mul(
                    ytile[k][:, :y1 - y0], ps[:, :y1 - y0], col)
            else:
                fn = lambda e, ps=ps, k=k, y0=y0, y1=y1: e.tensor_copy(
                    ytile[k][:, :y1 - y0], ps[:, :y1 - y0])
            last = ypool.consume(bb, idx, tgt, [fn], waits=w)
            yst[ycnt[0]] = bb.dma(
                "gpsimd", y_hbm[tbg * 128:(tbg + 1) * 128, y0:y1], ytile[k][:, :y1 - y0],
                waits=[(ve, last)],
                accum_op=(ALU.bypass if first_expert else ALU.add))
            ycnt[0] += 1

        mm_token_major(bb, prog, hT, w2_tiles, E, ypool, ywriter, n_tb=tbs,
                       waits_first=[hready] if hready else ())
        y_done[0] = prog.chain["tensor"].v
        hT_tgt.clear()
        h1_tgt.clear()
        return y_done[0]

    mm_feature_major_streamed(
        bb, prog, w1_tiles, xn_stage, E, n_tok, (lambda mb: pB if (swiglu and mb >= IMB) else pA),
        hwriter, m_blocks=(2 * IMB if swiglu else IMB), chunk=tc_chunk,
        chunk_cb=chunk_cb, xb=ctx["xb"])


def ffn_make_ctx(prog, inter=2048, tc_chunk=512):
    IMB = inter // 128
    u = prog.uid()
    return dict(
        pA=PsumPool(prog, f"ffA{u}", [128, tc_chunk], 2, "scalar"),
        pB=PsumPool(prog, f"ffB{u}", [128, tc_chunk], 2, "vector"),
        ypool=PsumPool(prog, f"ffY{u}", [128, 512], 2, "vector"),
        h1sb=[prog.sbuf(f"ffh1{u}_{i}", [128, tc_chunk], BF16) for i in range(IMB)],
        hT=[prog.sbuf(f"ffh{u}_{i}", [128, tc_chunk], BF16) for i in range(IMB)],
        ytile=[prog.sbuf(f"ffy{u}_{i}", [128, 512], F32) for i in range(4)],
        xb=[[prog.sbuf(f"ffx{u}_{b}_{i}", [128, tc_chunk], BF16) for i in range(E // 128)]
            for b in range(2)],
    )

PP = PsumPool
"""Per-sublayer program fragments. Each sub_* authors blocks computing
x_out = x + f(LN(x)) for its sublayer; x/x_out are HBM [T, E] f32."""
import numpy as np
import concourse.bass as bass


def bias_fm_writer(prog, pool, outT, bias_tile):
    def writer(bb, mb, c0, c1, ps, idx, tgt):
        pool.consume(bb, idx, tgt, [
            lambda e, mb=mb, c0=c0, c1=c1, ps=ps: e.tensor_scalar(
                outT[mb][:, c0:c1], ps[:, :c1 - c0], bias_tile[:, mb:mb + 1], None, ALU.add)])
    return writer


def copy_fm_writer(prog, pool, outT):
    def writer(bb, mb, c0, c1, ps, idx, tgt):
        pool.consume(bb, idx, tgt, [
            lambda e, mb=mb, c0=c0, c1=c1, ps=ps: e.tensor_copy(outT[mb][:, c0:c1], ps[:, :c1 - c0])])
    return writer


def broadcast_row(bb, prog, row_ap, width, out_tile, pool):
    onesrow = prog.consts_tiles["cst_onesrow"]
    for c0 in range(0, width, 512):
        c1 = min(c0 + 512, width)
        ps, idx = pool.acquire(bb)
        tgt = pool.produce(bb, lambda e, c0=c0, c1=c1, ps=ps: e.matmul(
            ps[:, :c1 - c0], onesrow[:, :], row_ap[:, c0:c1], start=True, stop=True))
        pool.consume(bb, idx, tgt, [
            lambda e, c0=c0, c1=c1, ps=ps: e.tensor_copy(out_tile[:, c0:c1], ps[:, :c1 - c0])])


def vaug_tm_writer(prog, pool, vaug, dv, bias_rep=None):
    def writer(bb, tb, c0, c1, ps, idx, tgt):
        base0 = (c0 // dv) * (dv + 1)
        nh = (c1 - c0) // dv
        dst = vaug[tb][:, base0: base0 + nh * (dv + 1)].rearrange(
            "p (h d) -> p h d", d=dv + 1)[:, :, 0:dv]
        src = ps[:, :c1 - c0].rearrange("p (h d) -> p h d", d=dv)
        if bias_rep is not None:
            fn = lambda e, dst=dst, src=src, c0=c0, c1=c1: e.tensor_tensor(
                dst, src, bias_rep[:, c0:c1].rearrange("p (h d) -> p h d", d=dv), ALU.add)
        else:
            fn = lambda e, dst=dst, src=src: e.tensor_copy(dst, src)
        pool.consume(bb, idx, tgt, [fn])
    return writer


def vaug_memset_ones(bb, prog, vaug, dv):
    for tb in range(len(vaug)):
        bb.op("vector", lambda e, tb=tb: e.memset(
            vaug[tb][:, :].rearrange("p (h d) -> p h d", d=dv + 1)[:, :, dv:dv + 1], 1.0))


def sub_l3_attn_b(prog: Prog, x_hbm, xout_hbm, w, uid):
    H, dh, dv = 16, 64, 64
    csts = prog.consts_tiles
    ident = csts["cst_ident"]
    q_stage = prog.dram(f"stg_q_{uid}", [H * dh, T], BF16)
    o_stage = prog.dram(f"stg_o_{uid}", [H * dv, T], BF16)

    with prog.scope():
        with prog.block() as bb:
            bq = prog.sbuf("bq", [128, 8], F32)
            bl = prog.sbuf("bl", [128, 2], F32)
            bk = prog.sbuf("bk", [128, 8], F32)
            borow = prog.sbuf("borow", [1, E], F32)
            bb.dma("sync", bq[:, :], w["bq"][:, :])
            bb.dma("sync", bl[:, :], w["bl"][:, :])
            bb.dma("sync", bk[:, :], w["bk"][:, :])
            bb.dma("sync", borow[:, :], w["bo"][:, :])
        borep = prog.sbuf("borep", [128, E], BF16)
        _l3b_scope = prog.scope(); _l3b_scope.__enter__()
        latT = [prog.sbuf(f"latT{i}", [128, T], BF16) for i in range(2)]

        with prog.scope():
            xnT = [prog.sbuf(f"xnT{i}", [128, T], BF16) for i in range(8)]
            with prog.block() as bb:
                wq, _ = load_w_bf16(bb, prog, w["Wq"], name="wq")
                wl, _ = load_w_bf16(bb, prog, w["Wl"], name="wl")
                ln_to_xnT(bb, prog, x_hbm, xnT, ident)
            with prog.block() as bb:
                pl = PsumPool(prog, "pl", [128, 512], 2, "scalar")

                def lat_writer(bb, mb, c0, c1, ps, idx, tgt):
                    pl.consume(bb, idx, tgt, [
                        lambda e, mb=mb, c0=c0, c1=c1, ps=ps: e.activation(
                            latT[mb][:, c0:c1], ps[:, :c1 - c0], AF.Identity,
                            bias=bl[:, mb:mb + 1])])
                mm_feature_major(bb, prog, wl, xnT, T, pl, lat_writer, m_blocks=2)
                pq = PsumPool(prog, "pq", [128, 512], 2, "vector")
                qtmp = [prog.sbuf(f"qtmp{i}", [128, T], BF16) for i in range(2)]
                mm_feature_major(bb, prog, wq, xnT, T, pq,
                                 stage_fm_writer(prog, pq, q_stage, qtmp, bias_tile=bq))

        kT = [prog.sbuf(f"kT{i}", [128, T], BF16) for i in range(8)]
        vaug = [prog.sbuf(f"va{i}", [128, H * (dv + 1)], BF16) for i in range(NTB)]
        with prog.scope():
            with prog.block() as bb:
                wk, _ = load_w_bf16(bb, prog, w["Wk"], name="wk")
                wv, _ = load_w_bf16(bb, prog, w["Wv"], name="wv")
                bvrow = prog.sbuf("bvrow", [1, E], F32)
                bb.dma("sync", bvrow[:, :], w["bv"][:, :])
            bvrep = prog.sbuf("bvrep", [128, E], F32)
            with prog.block() as bb:
                bp = PsumPool(prog, "brc", [128, 512], 2, "vector")
                broadcast_row(bb, prog, bvrow, E, bvrep, bp)
                broadcast_row(bb, prog, borow, E, borep, bp)
            with prog.block() as bb:
                pk = PsumPool(prog, "pk", [128, 512], 2, "vector")
                mm_feature_major(bb, prog, wk, latT, T, pk, bias_fm_writer(prog, pk, kT, bk))
                pv = PsumPool(prog, "pv", [128, 512], 2, "vector")
                vaug_memset_ones(bb, prog, vaug, dv)
                mm_token_major(bb, prog, latT, wv, E, pv,
                               vaug_tm_writer(prog, pv, vaug, dv, bias_rep=bvrep))

        attention_heads(prog, q_stage=q_stage,
                        kT_ap=lambda h: kT[h // 2][(h % 2) * 64:(h % 2) * 64 + dh, :],
                        vaug=vaug, oT_stage=o_stage, H=H, dh=dh, dv=dv,
                        score_scale=dh ** -0.5, cconst=20.0)
        _l3b_scope.__exit__(None, None, None)

        with prog.block() as bb:
            wo, _ = load_w_bf16(bb, prog, w["Wo"], name="wo")
        out_proj_residual(prog, act_stage=o_stage, n_act=H * dv, wo_tiles=wo,
                          x_hbm=x_hbm, xout_hbm=xout_hbm, row_bias_rep=borep)


def sub_l0_attn_a(prog: Prog, x_hbm, xout_hbm, w, uid):
    H, dh, dv = 16, 64, 64
    csts = prog.consts_tiles
    ident = csts["cst_ident"]
    q_stage = prog.dram(f"stg_q_{uid}", [H * dh, T], BF16)
    o_stage = prog.dram(f"stg_o_{uid}", [H * dv, T], BF16)
    xn_stage = prog.dram(f"stg_xn_{uid}", [E, T], BF16)

    with prog.scope():
        with prog.block() as bb:
            slopes = prog.sbuf("slopes", [1, 16], F32)
            bb.dma("sync", slopes[:, :], w["slopes"][:, :])
            bqT, _ = load_w_bf16(bb, prog, w["BqT"], name="bqT")
            bkT, _ = load_w_bf16(bb, prog, w["BkT"], name="bkT")
            bvT, _ = load_w_bf16(bb, prog, w["BvT"], name="bvT")
            aqT, _ = load_w_bf16(bb, prog, w["AqT"], name="aqT")
            akT, _ = load_w_bf16(bb, prog, w["AkT"], name="akT")
            avT, _ = load_w_bf16(bb, prog, w["AvT"], name="avT")
        nslopes = prog.sbuf("nslopes", [128, 16], F32)
        with prog.block() as bb:
            bp = PsumPool(prog, "brc", [128, 512], 1, "vector")
            ps, idx = bp.acquire(bb)
            tgt = bp.produce(bb, lambda e, ps=ps: e.matmul(
                ps[:, :16], csts["cst_onesrow"][:, :], slopes[:, :], start=True, stop=True))
            bp.consume(bb, idx, tgt, [
                lambda e, ps=ps: e.tensor_scalar_mul(nslopes[:, :], ps[:, :16], -1.0)])

        t1q = prog.sbuf("t1q", [8, T], BF16)
        t1k = prog.sbuf("t1k", [8, T], BF16)
        t1v = prog.sbuf("t1v", [8, T], BF16)

        with prog.scope():
            xnT = [prog.sbuf(f"xnT{i}", [128, T], BF16) for i in range(8)]
            with prog.block() as bb:
                ln_to_xnT(bb, prog, x_hbm, xnT, ident)
            with prog.block() as bb:
                pt = PsumPool(prog, "pt", [128, 512], 2, "vector")

                def t1_writer(dst):
                    def wr(bb, mb, c0, c1, ps, idx, tgt):
                        pt.consume(bb, idx, tgt, [
                            lambda e, c0=c0, c1=c1, ps=ps: e.tensor_scalar_mul(
                                dst[:, c0:c1], ps[0:8, :c1 - c0], 2.0)])
                    return wr
                mm_feature_major(bb, prog, bqT, xnT, T, pt, t1_writer(t1q), m_blocks=1)
                mm_feature_major(bb, prog, bkT, xnT, T, pt, t1_writer(t1k), m_blocks=1)
                mm_feature_major(bb, prog, bvT, xnT, T, pt, t1_writer(t1v), m_blocks=1)
                for i in range(8):
                    bb.dma("sync", xn_stage[i * 128:(i + 1) * 128, :], xnT[i][:, :])

        with prog.scope():
            kT = [prog.sbuf(f"kT{i}", [128, T], BF16) for i in range(8)]
            with prog.scope():
                with prog.block() as bb:
                    wkT, _ = load_w_bf16(bb, prog, w["WkT"], name="wkT")
                    wqT, _ = load_w_bf16(bb, prog, w["WqT"], name="wqT")
                with prog.block() as bb:
                    pk = PsumPool(prog, "pk", [128, 512], 2, "vector")
                    mm_feature_major_streamed(
                        bb, prog, wkT, xn_stage, E, T, pk, copy_fm_writer(prog, pk, kT),
                        extra_rhs=lambda c0, c1: [(t1k[:, c0:c1], None)],
                        extra_w=[akT[0][0:8, :]])
                with prog.block() as bb:
                    pq = PsumPool(prog, "pq", [128, 512], 2, "vector")
                    mm_feature_major_streamed(
                        bb, prog, wqT, xn_stage, E, T, pq,
                        stage_fm_chunk_writer(prog, pq, q_stage),
                        extra_rhs=lambda c0, c1: [(t1q[:, c0:c1], None)],
                        extra_w=[aqT[0][0:8, :]])

            vaug = [prog.sbuf(f"va{i}", [128, H * (dv + 1)], BF16) for i in range(NTB)]
            with prog.scope():
                with prog.block() as bb:
                    wvT, _ = load_w_bf16(bb, prog, w["WvT"], name="wvT")
                with prog.block() as bb:
                    pv = PsumPool(prog, "pv", [128, 512], 2, "vector")
                    vaug_memset_ones(bb, prog, vaug, dv)
                    mm_token_major_streamed(
                        bb, prog, xn_stage, E, wvT, E, pv,
                        vaug_tm_writer(prog, pv, vaug, dv),
                        extra_act=lambda tb: [(t1v[:, tb * 128:(tb + 1) * 128], None)],
                        extra_w=[avT[0][0:8, :]])

            attention_heads(prog, q_stage=q_stage,
                            kT_ap=lambda h: kT[h // 2][(h % 2) * 64:(h % 2) * 64 + dh, :],
                            vaug=vaug, oT_stage=o_stage, H=H, dh=dh, dv=dv,
                            score_scale=dh ** -0.5, cconst=20.0, causal=True,
                            neg_slope_col=lambda h: nslopes[:, h:h + 1],
                            mask_ap=causal_mask_ap(prog))

        with prog.block() as bb:
            wpT, _ = load_w_bf16(bb, prog, w["WpT"], name="wpT")
        out_proj_residual(prog, act_stage=o_stage, n_act=H * dv, wo_tiles=wpT,
                          x_hbm=x_hbm, xout_hbm=xout_hbm)


def sub_l3_ffn(prog: Prog, x_hbm, xout_hbm, w, uid):
    csts = prog.consts_tiles
    ident = csts["cst_ident"]
    RMS_EPS = 1.1920929e-07
    NE = 14
    xn_stage = prog.dram(f"stg_xn_{uid}", [E, T], BF16)
    y_hbm = prog.dram(f"stg_y_{uid}", [T, E], F32)

    with prog.scope():
        rinv = prog.sbuf("rinv", [128, NTB], F32)
        wcomb = prog.sbuf("wcomb", [128, NTB * NE], F32)
        rms_cols = prog.sbuf("rmsc", [128, 16 * 8], F32)
        with prog.block() as bb:
            bb.dma("sync", rms_cols[:, :], w["rms_cols"][:, :])
            rw, _ = load_w_bf16(bb, prog, w["router_w"], name="rw")

        with prog.scope():
            xnT = [prog.sbuf(f"xnT{i}", [128, T], BF16) for i in range(8)]
            with prog.block() as bb:
                info = ln_to_xnT(bb, prog, x_hbm, xnT, ident,
                                 second_norm=("rms_eps_inside", RMS_EPS), rinv_out=rinv)
                for i in range(8):
                    bb.dma("sync", xn_stage[i * 128:(i + 1) * 128, :], xnT[i][:, :],
                           waits=[(prog.chain["vector"], info["ve_done"]),
                                  (prog.chain["scalar"], info["act_done"])] if i == 0 else ())

        with prog.scope():
            with prog.block() as bb:
                rp = PsumPool(prog, "rp", [128, 32], 2, "vector")
                g = prog.sbuf("rg", [128, NE], F32)
                st = prog.sbuf("rst", [128, 8], F32)
                act = prog.chain["scalar"]
                ve = prog.chain["vector"]

                def rwriter(bb, tb, c0, c1, ps, idx, tgt):
                    rp.consume(bb, idx, tgt, [
                        lambda e, ps=ps, tb=tb: e.tensor_scalar_mul(
                            g[:, :], ps[:, :NE], rinv[:, tb:tb + 1]),
                        lambda e: e.reduce_max(st[:, 0:1], g[:, :], axis=AX.X),
                        lambda e: e.tensor_scalar_mul(st[:, 1:2], st[:, 0:1], -1.0),
                    ])
                    a1 = bb.op("scalar", lambda e: e.activation(
                        g[:, :], g[:, :], AF.Exp, bias=st[:, 1:2]),
                        waits=[(ve, prog.chain["vector"].v)])
                    bb.op("vector", lambda e: e.reduce_sum(st[:, 2:3], g[:, :], axis=AX.X),
                          waits=[(act, a1)])
                    bb.op("vector", lambda e: e.reciprocal(st[:, 2:3], st[:, 2:3]))
                    bb.op("vector", lambda e: e.tensor_scalar_mul(g[:, :], g[:, :], st[:, 2:3]))
                    bb.op("vector", lambda e: e.reduce_max(st[:, 3:4], g[:, :], axis=AX.X))
                    bb.op("vector", lambda e, tb=tb: e.tensor_scalar(
                        wcomb[:, tb * NE:(tb + 1) * NE], g[:, :], st[:, 3:4], None, ALU.is_ge))
                    bb.op("vector", lambda e, tb=tb: e.scalar_tensor_tensor(
                        wcomb[:, tb * NE:(tb + 1) * NE], wcomb[:, tb * NE:(tb + 1) * NE],
                        -1e30, g[:, :], ALU.mult, ALU.add))
                    bb.op("vector", lambda e, tb=tb: e.reduce_max(
                        st[:, 6:7], wcomb[:, tb * NE:(tb + 1) * NE], axis=AX.X))
                    bb.op("vector", lambda e, tb=tb: e.tensor_scalar(
                        wcomb[:, tb * NE:(tb + 1) * NE], g[:, :], st[:, 6:7], None, ALU.is_ge))
                    bb.op("vector", lambda e, tb=tb: e.tensor_tensor(
                        wcomb[:, tb * NE:(tb + 1) * NE], wcomb[:, tb * NE:(tb + 1) * NE],
                        g[:, :], ALU.mult))
                    bb.op("vector", lambda e, tb=tb: e.reduce_sum(
                        st[:, 7:8], wcomb[:, tb * NE:(tb + 1) * NE], axis=AX.X))
                    bb.op("vector", lambda e: e.tensor_scalar_add(st[:, 7:8], st[:, 7:8], 1e-9))
                    bb.op("vector", lambda e: e.reciprocal(st[:, 7:8], st[:, 7:8]))
                    bb.op("vector", lambda e, tb=tb: e.tensor_scalar_mul(
                        wcomb[:, tb * NE:(tb + 1) * NE], wcomb[:, tb * NE:(tb + 1) * NE],
                        st[:, 7:8]))

                mm_token_major_streamed(bb, prog, xn_stage, E, rw, NE, rp, rwriter)

        with prog.scope():
            ctx = ffn_make_ctx(prog, inter=2048)
            w1t = [prog.sbuf(f"ew1_{i}", [128, 2048], BF16) for i in range(8)]
            w2t = [prog.sbuf(f"ew2_{i}", [128, 1024], BF16) for i in range(16)]
            for e in range(16):
                routed = e < NE
                nm = f"w1_{e}" if routed else f"w1s_{e - NE}"
                nm2 = f"w2_{e}" if routed else f"w2s_{e - NE}"
                with prog.block() as bb:
                    _, t1 = load_w_bf16(bb, prog, w[nm], name=f"lw1{e}", tiles=w1t)
                    _, t2 = load_w_bf16(bb, prog, w[nm2], name=f"lw2{e}", tiles=w2t)
                    for k in range(8):
                        bb.op("vector", lambda e_, k=k, e=e: e_.tensor_scalar_mul(
                            w1t[k][:, :], w1t[k][:, :], rms_cols[:, e * 8 + k:e * 8 + k + 1]),
                            waits=wmax(t1 + t2) if k == 0 else ())
                with prog.block() as bb:
                    ffn_expert_pass(bb, prog, ctx, xn_stage=xn_stage,
                                    w1_tiles=w1t, w2_tiles=w2t, swiglu=False,
                                    wcomb_col=(lambda tbg, e=e: wcomb[:, tbg * NE + e: tbg * NE + e + 1]) if routed else None,
                                    y_hbm=y_hbm, first_expert=(e == 0), inter=2048)

        add_residual_pass(prog, x_hbm, y_hbm, xout_hbm)


def sub_l0_ffn(prog: Prog, x_hbm, xout_hbm, w, uid):
    csts = prog.consts_tiles
    ident = csts["cst_ident"]
    NE = 8
    xn_stage = prog.dram(f"stg_xn_{uid}", [E, T], BF16)
    y_hbm = prog.dram(f"stg_y_{uid}", [T, E], F32)

    with prog.scope():
        wcomb = prog.sbuf("wcomb", [128, NTB * NE], F32)
        gbias_rep = prog.sbuf("gbrep", [128, NE], F32)
        with prog.block() as bb:
            gw, _ = load_w_bf16(bb, prog, w["gate_wT"], name="gw")
            gbrow = prog.sbuf("gbrow", [1, NE], F32)
            bb.dma("sync", gbrow[:, :], w["gate_bias"][:, :])
        with prog.block() as bb:
            bp = PsumPool(prog, "brc", [128, 32], 1, "vector")
            ps, idx = bp.acquire(bb)
            tgt = bp.produce(bb, lambda e, ps=ps: e.matmul(
                ps[:, :NE], csts["cst_onesrow"][:, :], gbrow[:, :], start=True, stop=True))
            bp.consume(bb, idx, tgt, [
                lambda e, ps=ps: e.tensor_copy(gbias_rep[:, :], ps[:, :NE])])

        with prog.scope():
            xnT = [prog.sbuf(f"xnT{i}", [128, T], BF16) for i in range(8)]
            with prog.block() as bb:
                info = ln_to_xnT(bb, prog, x_hbm, xnT, ident)
                for i in range(8):
                    bb.dma("sync", xn_stage[i * 128:(i + 1) * 128, :], xnT[i][:, :],
                           waits=[(prog.chain["vector"], info["ve_done"]),
                                  (prog.chain["scalar"], info["act_done"])] if i == 0 else ())

        with prog.scope():
            with prog.block() as bb:
                rp = PsumPool(prog, "rp", [128, 32], 2, "vector")
                g = prog.sbuf("rg", [128, NE], F32)
                st = prog.sbuf("rst", [128, 8], F32)
                act = prog.chain["scalar"]
                ve = prog.chain["vector"]

                def rwriter(bb, tb, c0, c1, ps, idx, tgt):
                    rp.consume(bb, idx, tgt, [
                        lambda e, ps=ps: e.tensor_copy(g[:, :], ps[:, :NE]),
                        lambda e: e.reduce_max(st[:, 0:1], g[:, :], axis=AX.X),
                        lambda e: e.tensor_scalar_mul(st[:, 1:2], st[:, 0:1], -1.0),
                    ])
                    a1 = bb.op("scalar", lambda e: e.activation(
                        g[:, :], g[:, :], AF.Exp, bias=st[:, 1:2]),
                        waits=[(ve, prog.chain["vector"].v)])
                    bb.op("vector", lambda e: e.reduce_sum(st[:, 2:3], g[:, :], axis=AX.X),
                          waits=[(act, a1)])
                    bb.op("vector", lambda e: e.reciprocal(st[:, 2:3], st[:, 2:3]))
                    bb.op("vector", lambda e: e.tensor_scalar_mul(g[:, :], g[:, :], st[:, 2:3]))
                    wsl = lambda tb=tb: wcomb[:, tb * NE:(tb + 1) * NE]
                    bb.op("vector", lambda e, tb=tb: e.tensor_tensor(
                        wsl(tb), g[:, :], gbias_rep[:, :], ALU.add))
                    bb.op("vector", lambda e, tb=tb: e.reduce_max(st[:, 3:4], wsl(tb), axis=AX.X))
                    bb.op("vector", lambda e, tb=tb: e.tensor_scalar(
                        st[:, 4:5], st[:, 3:4], 1.0, None, ALU.mult))
                    bb.op("vector", lambda e, tb=tb: e.tensor_scalar(
                        g2[:, :], wsl(tb), st[:, 3:4], None, ALU.is_ge))
                    bb.op("vector", lambda e, tb=tb: e.scalar_tensor_tensor(
                        g2[:, :], g2[:, :], -1e30, wsl(tb), ALU.mult, ALU.add))
                    bb.op("vector", lambda e: e.reduce_max(st[:, 6:7], g2[:, :], axis=AX.X))
                    bb.op("vector", lambda e, tb=tb: e.tensor_scalar(
                        g2[:, :], wsl(tb), st[:, 6:7], None, ALU.is_ge))
                    bb.op("vector", lambda e, tb=tb: e.tensor_tensor(
                        wsl(tb), g2[:, :], g[:, :], ALU.mult))

                g2 = prog.sbuf("rg2", [128, NE], F32)
                mm_token_major_streamed(bb, prog, xn_stage, E, gw, NE, rp, rwriter)

        with prog.scope():
            ctx = ffn_make_ctx(prog, inter=2048)
            w13t = [prog.sbuf(f"ew13_{i}", [128, 4096], BF16) for i in range(8)]
            w2t = [prog.sbuf(f"ew2_{i}", [128, 1024], BF16) for i in range(16)]
            for e in range(NE + 1):
                routed = e < NE
                n1 = f"w1T_{e}" if routed else "sw1T"
                n3 = f"w3T_{e}" if routed else "sw3T"
                n2 = f"w2T_{e}" if routed else "sw2T"
                with prog.block() as bb:
                    for i in range(8):
                        bb.dma("gpsimd", w13t[i][:, 0:2048], w[n1][i * 128:(i + 1) * 128, :])
                        bb.dma("gpsimd", w13t[i][:, 2048:4096], w[n3][i * 128:(i + 1) * 128, :])
                    load_w_bf16(bb, prog, w[n2], name=f"lw2{e}", tiles=w2t)
                with prog.block() as bb:
                    ffn_expert_pass(bb, prog, ctx, xn_stage=xn_stage,
                                    w1_tiles=w13t, w2_tiles=w2t, swiglu=True,
                                    wcomb_col=(lambda tbg, e=e: wcomb[:, tbg * NE + e: tbg * NE + e + 1]) if routed else None,
                                    y_hbm=y_hbm, first_expert=(e == 0), inter=2048)

        add_residual_pass(prog, x_hbm, y_hbm, xout_hbm)


def ternarize(prog, w_hbm, out_tiles, name):
    K, M = w_hbm.shape
    kt = K // 128
    u = prog.uid()
    act, ve, pe = prog.chain["scalar"], prog.chain["vector"], prog.chain["tensor"]
    csts = prog.consts_tiles
    with prog.scope():
        wf = [prog.sbuf(f"tz_w{u}_{i}", [128, M], F32) for i in range(2)]
        scr = prog.sbuf(f"tz_s{u}", [128, M], F32)
        part = prog.sbuf(f"tz_p{u}", [128, kt], F32)
        thr = prog.sbuf(f"tz_t{u}", [128, 1], F32)
        with prog.block() as bb:
            hist = {}
            for i in range(kt):
                b = i % 2
                w = [(ve, hist[i - 2])] if i >= 2 else []
                ld = bb.dma("sync", wf[b][:, :], w_hbm[i * 128:(i + 1) * 128, :], waits=w)
                hist[i] = bb.op("vector", lambda e, b=b, i=i: e.reduce_sum(
                    part[:, i:i + 1], wf[b][:, :], axis=AX.X, apply_absolute_value=True),
                    waits=[ld])
            bp = PsumPool(prog, f"tzp{u}", [128, 64], 1, "vector")
            ps, idx = bp.acquire(bb)
            tgt = bp.produce(bb, lambda e, ps=ps: e.matmul(
                ps[0:1, :kt], csts["cst_ones128f"][:, :], part[:, :],
                start=True, stop=True), waits=[(ve, hist[kt - 1])])
            bp.consume(bb, idx, tgt, [
                lambda e, ps=ps: e.reduce_sum(thr[0:1, :], ps[0:1, :kt], axis=AX.X),
                lambda e: e.tensor_scalar_mul(thr[0:1, :], thr[0:1, :], 0.5 / (K * M)),
            ])
            ps2, idx2 = bp.acquire(bb)
            tgt2 = bp.produce(bb, lambda e, ps2=ps2: e.matmul(
                ps2[:, 0:1], csts["cst_onesrow"][:, :], thr[0:1, 0:1], start=True, stop=True),
                waits=[(ve, prog.chain["vector"].v)])
            bp.consume(bb, idx2, tgt2, [
                lambda e, ps2=ps2: e.tensor_copy(thr[:, :], ps2[:, 0:1])])
        with prog.block() as bb:
            hist = {}
            for i in range(kt):
                b = i % 2
                w = [(ve, hist[i - 2])] if i >= 2 else []
                ld = bb.dma("sync", wf[b][:, :], w_hbm[i * 128:(i + 1) * 128, :], waits=w)
                wv = [ld] + ([(ve, hist[i - 1])] if i >= 1 else [])
                a1 = bb.op("scalar", lambda e, b=b: e.activation(
                    scr[:, :], wf[b][:, :], AF.Abs), waits=wv)
                m1 = bb.op("vector", lambda e: e.tensor_scalar(
                    scr[:, :], scr[:, :], thr[:, 0:1], None, ALU.is_gt),
                    waits=[(act, a1)])
                a2 = bb.op("scalar", lambda e, b=b: e.sign(wf[b][:, :], wf[b][:, :]),
                           waits=[(ve, m1)])
                hist[i] = bb.op("vector", lambda e, b=b, i=i: e.tensor_tensor(
                    out_tiles[i][:, :], wf[b][:, :], scr[:, :], ALU.mult),
                    waits=[(act, a2)])


def sub_l2_ffn(prog: Prog, x_hbm, xout_hbm, w, uid):
    csts = prog.consts_tiles
    ident = csts["cst_ident"]
    DF = 4096
    xn_stage = prog.dram(f"stg_xn_{uid}", [E, T], BF16)
    h_stage = prog.dram(f"stg_h_{uid}", [DF, T], BF16)
    act, ve, pe = prog.chain["scalar"], prog.chain["vector"], prog.chain["tensor"]

    with prog.scope():
        with prog.scope():
            xnT = [prog.sbuf(f"xnT{i}", [128, T], BF16) for i in range(8)]
            with prog.block() as bb:
                info = ln_to_xnT(bb, prog, x_hbm, xnT, ident,
                                 second_norm=("rms_plain", 1e-8))
                for i in range(8):
                    bb.dma("sync", xn_stage[i * 128:(i + 1) * 128, :], xnT[i][:, :],
                           waits=[(ve, info["ve_done"]), (act, info["act_done"])] if i == 0 else ())

        ssq_row = prog.sbuf("ssq", [1, T], F32)
        rcol = prog.sbuf("rcol", [128, NTB], F32)
        with prog.scope():
            wtup = [prog.sbuf(f"wtu{i}", [128, DF], BF16) for i in range(8)]
            ternarize(prog, w["w_upT"], wtup, "up")
            with prog.block() as bb:
                hp = PsumPool(prog, "hp", [128, 512], 2, "scalar")
                sqp = PsumPool(prog, "sqp", [1, 512], 1, "vector")
                hsb = [prog.sbuf(f"hsb{i}", [128, 512], BF16) for i in range(4)]
                hsq = prog.sbuf("hsq", [128, 512], BF16)
                st_h = {}
                cnt = [0]
                ssq_ps = [None]

                def hwriter(bb, mb, c0, c1, ps, idx, tgt):
                    k = cnt[0] % 4
                    wv = [st_h[cnt[0] - 4]] if cnt[0] >= 4 else []
                    if mb == 0:
                        ssq_ps[0] = sqp.acquire(bb)
                    a1 = hp.consume(bb, idx, tgt, [
                        lambda e, ps=ps, k=k, cw=c1 - c0: e.activation(
                            hsb[k][:, :cw], ps[:, :cw], AF.Gelu)], waits=wv)
                    a2 = bb.op("scalar", lambda e, k=k, cw=c1 - c0: e.activation(
                        hsq[:, :cw], hsb[k][:, :cw], AF.Square))
                    pssq, issq = ssq_ps[0]
                    t2 = bb.op("tensor", lambda e, cw=c1 - c0, mb=mb, pssq=pssq: e.matmul(
                        pssq[0:1, :cw], csts["cst_ones128"][:, :], hsq[:, :cw],
                        start=(mb == 0), stop=(mb == 31)),
                        waits=[(act, a2)], mark=True)
                    st_h[cnt[0]] = bb.dma("sync", h_stage[mb * 128:(mb + 1) * 128, c0:c1],
                                          hsb[k][:, :cw if False else c1 - c0],
                                          waits=[(act, a1)])
                    cnt[0] += 1
                    if mb == 31:
                        sqp.consume(bb, issq, t2, [
                            lambda e, c0=c0, c1=c1, pssq=pssq: e.tensor_copy(
                                ssq_row[:, c0:c1], pssq[0:1, :c1 - c0])])

                mm_feature_major_streamed(bb, prog, wtup, xn_stage, E, T, hp,
                                          hwriter, m_blocks=32)
            with prog.block() as bb:
                r_row = prog.sbuf("rrow", [1, T], F32)
                bb.op("scalar", lambda e: e.activation(r_row[:, :], ssq_row[:, :], AF.Ln,
                                                       scale=1.0 / DF))
                a = bb.op("scalar", lambda e: e.activation(r_row[:, :], r_row[:, :], AF.Exp, scale=0.5))
                v1 = bb.op("vector", lambda e: e.tensor_scalar_add(r_row[:, :], r_row[:, :], 1e-8),
                           waits=[(act, a)])
                bb.op("scalar", lambda e: e.activation(r_row[:, :], r_row[:, :], AF.Ln),
                      waits=[(ve, v1)])
                a2 = bb.op("scalar", lambda e: e.activation(r_row[:, :], r_row[:, :], AF.Exp, scale=-1.0))
                tp = PsumPool(prog, "tpr", [128, NTB], 1, "vector")
                psr, idxr = tp.acquire(bb)
                for c in range(NTB):
                    tt = bb.op("tensor", lambda e, c=c, psr=psr: e.transpose(
                        psr[:, c:c + 1], r_row[0:1, c * 128:(c + 1) * 128],
                        csts["cst_identf"][0:1, 0:1]),
                        waits=[(act, a2)] if c == 0 else (), mark=(c == NTB - 1))
                tp.consume(bb, idxr, tt, [
                    lambda e, psr=psr: e.tensor_copy(rcol[:, :], psr[:, :])])

        with prog.scope():
            wtdn = [prog.sbuf(f"wtd{i}", [128, E], BF16) for i in range(32)]
            ternarize(prog, w["w_downT"], wtdn, "dn")
            with prog.block() as bb:
                yp = PsumPool(prog, "yp", [128, 512], 2, "vector")
                xold = [prog.sbuf(f"xo{i}", [128, E], F32) for i in range(2)]
                ytl = [prog.sbuf(f"ytl{i}", [128, 512], F32) for i in range(4)]
                ld_hist = {}
                st_hist = {}
                cnt = [0]

                def ywriter(bb, tb, c0, c1, ps, idx, tgt):
                    b = tb % 2
                    if c0 == 0:
                        wv = [(ve, ld_hist[tb - 2][1])] if tb >= 2 else []
                        ld = bb.dma("sync", xold[b][:, :], x_hbm[tb * 128:(tb + 1) * 128, :], waits=wv)
                        ld_hist[tb] = (ld, 0)
                    k = cnt[0] % 4
                    wv = [st_hist[cnt[0] - 4]] if cnt[0] >= 4 else []
                    wv.append(ld_hist[tb][0])
                    last = yp.consume(bb, idx, tgt, [
                        lambda e, ps=ps, k=k, c0=c0, c1=c1, b=b, tb=tb: e.scalar_tensor_tensor(
                            ytl[k][:, :c1 - c0], ps[:, :c1 - c0], rcol[:, tb:tb + 1],
                            xold[b][:, c0:c1], ALU.mult, ALU.add)], waits=wmax(wv))
                    ld_hist[tb] = (ld_hist[tb][0], last)
                    st_hist[cnt[0]] = bb.dma("sync", xout_hbm[tb * 128:(tb + 1) * 128, c0:c1],
                                             ytl[k][:, :c1 - c0], waits=[(ve, last)])
                    cnt[0] += 1

                mm_token_major_streamed(bb, prog, h_stage, DF, wtdn, E, yp, ywriter)


def sub_l1_attn_b(prog: Prog, x_hbm, xout_hbm, w, uid):
    _br = broadcast_row
    H, dh, dv = 16, 64, 64
    csts = prog.consts_tiles
    ident = csts["cst_ident"]
    q_stage = prog.dram(f"stg_q_{uid}", [H * dh, T], BF16)
    o_stage = prog.dram(f"stg_o_{uid}", [H * dv, T], BF16)
    xn_stage = prog.dram(f"stg_xn_{uid}", [E, T], BF16)
    rep_hbm = prog.dram(f"stg_rep_{uid}", [H, 128, 8448], F32)
    SW = 8448

    with prog.scope():
        borep = prog.sbuf("borep", [128, E], BF16)
        band_hbm = prog.dram(f"stg_band_{uid}", [16, SW], F32)
        with prog.scope():
            with prog.block() as bb:
                table = prog.sbuf("rtab", [32, 16], F32)
                oht = prog.sbuf("oht", [32, SW], F32)
                borow = prog.sbuf("borow", [1, E], F32)
                bb.dma("sync", table[:, :], w["rel_table"][:, :])
                bb.dma("sync", oht[:, :], w["OHT"][:, :])
                bb.dma("sync", borow[:, :], w["b_proj"][:, :])
            bandT = prog.sbuf("bandT", [16, SW], F32)
            with prog.block() as bb:
                bp = PsumPool(prog, "bp", [128, 512], 2, "scalar")
                for c0 in range(0, SW, 512):
                    c1 = min(c0 + 512, SW)
                    ps, idx = bp.acquire(bb)
                    tgt = bp.produce(bb, lambda e, c0=c0, c1=c1, ps=ps: e.matmul(
                        ps[0:16, :c1 - c0], table[:, :], oht[:, c0:c1], start=True, stop=True))
                    bp.consume(bb, idx, tgt, [
                        lambda e, c0=c0, c1=c1, ps=ps: e.activation(
                            bandT[:, c0:c1], ps[0:16, :c1 - c0], AF.Exp)])
                bp2 = PsumPool(prog, "bp2", [128, 512], 2, "vector")
                _br(bb, prog, borow, E, borep, bp2)
            with prog.block() as bb:
                bb.dma("sync", band_hbm[:, :], bandT[:, :],
                       waits=[(prog.chain["scalar"], prog.chain["scalar"].v)])
        with prog.scope():
            brow = prog.sbuf("brow", [1, SW], F32)
            brep = prog.sbuf("brep", [128, SW], F32)
            with prog.block() as bb:
                rp = PsumPool(prog, "rp", [128, 512], 2, "vector")
                st = None
                for h in range(H):
                    wv = [(prog.chain["tensor"], prog.chain["tensor"].v)]
                    ldb = bb.dma("sync", brow[:, :], band_hbm[h:h + 1, :], waits=wv)
                    bb.wait("tensor", *ldb)
                    if st is not None:
                        bb.wait("vector", *st)
                    _br(bb, prog, brow, SW, brep, rp)
                    st = bb.dma("sync", rep_hbm[h, :, :], brep[:, :],
                                waits=[(prog.chain["vector"], prog.chain["vector"].v)])

        with prog.scope():
            xnT = [prog.sbuf(f"xnT{i}", [128, T], BF16) for i in range(8)]
            with prog.block() as bb:
                info = ln_to_xnT(bb, prog, x_hbm, xnT, ident)
                for i in range(8):
                    bb.dma("sync", xn_stage[i * 128:(i + 1) * 128, :], xnT[i][:, :],
                           waits=[(prog.chain["vector"], info["ve_done"]),
                                  (prog.chain["scalar"], info["act_done"])] if i == 0 else ())

        kT = [prog.sbuf(f"kT{i}", [128, T], BF16) for i in range(8)]
        with prog.scope():
            with prog.scope():
                with prog.block() as bb:
                    wkT, _ = load_w_bf16(bb, prog, w["w_kT"], name="wkT")
                    wqT, _ = load_w_bf16(bb, prog, w["w_qT"], name="wqT")
                with prog.block() as bb:
                    pk = PsumPool(prog, "pk", [128, 512], 2, "vector")
                    mm_feature_major_streamed(bb, prog, wkT, xn_stage, E, T, pk,
                                              copy_fm_writer(prog, pk, kT))
                    pq = PsumPool(prog, "pq", [128, 512], 2, "vector")
                    mm_feature_major_streamed(bb, prog, wqT, xn_stage, E, T, pq,
                                              stage_fm_chunk_writer(prog, pq, q_stage))
            vaug = [prog.sbuf(f"va{i}", [128, H * (dv + 1)], BF16) for i in range(NTB)]
            with prog.scope():
                with prog.block() as bb:
                    wvT, _ = load_w_bf16(bb, prog, w["w_vT"], name="wvT")
                with prog.block() as bb:
                    pv = PsumPool(prog, "pv", [128, 512], 2, "vector")
                    vaug_memset_ones(bb, prog, vaug, dv)
                    mm_token_major_streamed(bb, prog, xn_stage, E, wvT, E, pv,
                                            vaug_tm_writer(prog, pv, vaug, dv))

            wrep = prog.sbuf("wrep", [128, 8192], BF16)
            wrep_ld = [None]

            def head_pre(bb, h):
                wrep_ld[0] = bb.dma("gpsimd", wrep[:, :],
                                    bass.AP(rep_hbm, h * 128 * SW + 127,
                                            [[SW - 1, 128], [1, 8192]]))
                bb.wait("vector", *wrep_ld[0])

            def wmult_ap(h, kb, q0, qcw):
                off = 4095 + q0 - kb * 128
                return wrep[:, off:off + qcw]


            attention_heads(prog, q_stage=q_stage,
                            kT_ap=lambda h: kT[h // 2][(h % 2) * 64:(h % 2) * 64 + dh, :],
                            vaug=vaug, oT_stage=o_stage, H=H, dh=dh, dv=dv,
                            score_scale=dh ** -0.5, cconst=20.0,
                            wmult_ap=wmult_ap, head_pre=head_pre)

        with prog.block() as bb:
            wpT, _ = load_w_bf16(bb, prog, w["w_projT"], name="wpT")
        out_proj_residual(prog, act_stage=o_stage, n_act=H * dv, wo_tiles=wpT,
                          x_hbm=x_hbm, xout_hbm=xout_hbm, row_bias_rep=borep)


def sub_l1_ffn(prog: Prog, x_hbm, xout_hbm, w, uid):
    H, dh, dv = 16, 64, 64
    csts = prog.consts_tiles
    ident = csts["cst_ident"]
    q_stage = prog.dram(f"stg_q_{uid}", [H * dh, T], BF16)
    o_stage = prog.dram(f"stg_o_{uid}", [H * dv, T], BF16)
    xn_stage = prog.dram(f"stg_xn_{uid}", [E, T], BF16)
    act, ve, pe = prog.chain["scalar"], prog.chain["vector"], prog.chain["tensor"]

    with prog.scope():
        gcol = prog.sbuf("gcol", [128, 1], F32)
        pooledT = prog.sbuf("pooledT", [128, 8], F32)
        with prog.scope():
            xnT = [prog.sbuf(f"xnT{i}", [128, T], BF16) for i in range(8)]
            with prog.block() as bb:
                info = ln_to_xnT(bb, prog, x_hbm, xnT, ident)
                for i in range(8):
                    bb.dma("sync", xn_stage[i * 128:(i + 1) * 128, :], xnT[i][:, :],
                           waits=[(ve, info["ve_done"]), (act, info["act_done"])] if i == 0 else ())
                for i in range(8):
                    bb.op("vector", lambda e, i=i: e.reduce_sum(
                        pooledT[:, i:i + 1], xnT[i][:, :], axis=AX.X),
                        waits=[(act, info["act_done"])] if i == 0 else ())
                bb.op("vector", lambda e: e.tensor_scalar_mul(pooledT[:, :], pooledT[:, :], 1.0 / T))

        with prog.scope():
            with prog.block() as bb:
                gw1, _ = load_w_bf16(bb, prog, w["gw1T"], name="gw1")
                gw2, _ = load_w_bf16(bb, prog, w["gw2T"], name="gw2")
                gb1 = prog.sbuf("gb1", [1, 256], F32)
                gb2 = prog.sbuf("gb2", [1, 8], F32)
                exp8 = prog.sbuf("exp8", [8, 128], F32)
                bb.dma("sync", gb1[:, :], w["gb1"][:, :])
                bb.dma("sync", gb2[:, :], w["gb2"][:, :])
                bb.dma("sync", exp8[:, :], w["exp8"][:, :])
            with prog.block() as bb:
                pb = prog.sbuf("poolb", [128, 8], BF16)
                c1v = bb.op("vector", lambda e: e.tensor_copy(pb[:, :], pooledT[:, :]))
                gp = PP(prog, "gp", [128, 512], 2, "vector")
                hrow = prog.sbuf("hrow", [1, 256], F32)
                hT = prog.sbuf("hT", [128, 2], BF16)
                st8 = prog.sbuf("st8", [1, 8], F32)
                grow = prog.sbuf("grow", [1, 8], F32)
                ps, idx = gp.acquire(bb)
                for k in range(8):
                    fn = lambda e, k=k, ps=ps: e.matmul(
                        ps[0:1, 0:256], pb[:, k:k + 1], gw1[k][:, :],
                        start=(k == 0), stop=(k == 7))
                    if k == 7:
                        tgt = gp.produce(bb, fn)
                    else:
                        bb.op("tensor", fn, waits=[(ve, c1v)] if k == 0 else (), mark=False)
                gp.consume(bb, idx, tgt, [
                    lambda e, ps=ps: e.tensor_tensor(hrow[:, :], ps[0:1, 0:256], gb1[:, :], ALU.add),
                    lambda e: e.tensor_relu(hrow[:, :], hrow[:, :]),
                ])
                hrb = prog.sbuf("hrb", [1, 256], BF16)
                hv = bb.op("vector", lambda e: e.tensor_copy(hrb[:, :], hrow[:, :]))
                tpp = PP(prog, "tpp", [128, 2], 2, "vector", dtype=BF16)
                tppf = PP(prog, "tppf", [128, 2], 1, "vector")
                for c in range(2):
                    ps2, idx2 = tpp.acquire(bb)
                    tt = bb.op("tensor", lambda e, c=c, ps2=ps2: e.transpose(
                        ps2[:, 0:1], hrb[0:1, c * 128:(c + 1) * 128], ident[0:1, 0:1]),
                        waits=[(ve, hv)] if c == 0 else (), mark=True)
                    tpp.consume(bb, idx2, tt, [
                        lambda e, ps2=ps2, c=c: e.tensor_copy(hT[:, c:c + 1], ps2[:, 0:1])])
                ps3, idx3 = gp.acquire(bb)
                for k in range(2):
                    fn = lambda e, k=k, ps3=ps3: e.matmul(
                        ps3[0:1, 0:8], hT[:, k:k + 1], gw2[k][:, :],
                        start=(k == 0), stop=(k == 1))
                    if k == 1:
                        tgt3 = gp.produce(bb, fn, waits=[(ve, prog.chain["vector"].v)])
                    else:
                        bb.op("tensor", fn, waits=[(ve, prog.chain["vector"].v)], mark=False)
                gp.consume(bb, idx3, tgt3, [
                    lambda e, ps3=ps3: e.tensor_tensor(grow[:, :], ps3[0:1, 0:8], gb2[:, :], ALU.add),
                    lambda e: e.reduce_max(st8[:, 0:1], grow[:, :], axis=AX.X),
                    lambda e: e.tensor_scalar(st8[:, 1:2], grow[:, :], st8[:, 0:1], None, ALU.is_ge) if False else e.tensor_scalar_mul(st8[:, 1:2], st8[:, 0:1], -1.0),
                ])
                m8 = prog.sbuf("m8", [1, 8], F32)
                m8b = prog.sbuf("m8b", [1, 8], F32)
                bb.op("vector", lambda e: e.tensor_scalar(
                    m8[:, :], grow[:, :], st8[:, 0:1], None, ALU.is_ge))
                bb.op("vector", lambda e: e.scalar_tensor_tensor(
                    m8b[:, :], m8[:, :], -1e30, grow[:, :], ALU.mult, ALU.add))
                bb.op("vector", lambda e: e.reduce_max(st8[:, 2:3], m8b[:, :], axis=AX.X))
                a1 = bb.op("scalar", lambda e: e.activation(
                    st8[:, 3:4], st8[:, 2:3], AF.Exp, bias=st8[:, 1:2]),
                    waits=[(ve, prog.chain["vector"].v)])
                bb.op("vector", lambda e: e.tensor_scalar_add(st8[:, 4:5], st8[:, 3:4], 1.0),
                      waits=[(act, a1)])
                bb.op("vector", lambda e: e.reciprocal(st8[:, 4:5], st8[:, 4:5]))
                bb.op("vector", lambda e: e.tensor_tensor(st8[:, 5:6], st8[:, 3:4], st8[:, 4:5], ALU.mult))
                bb.op("vector", lambda e: e.tensor_scalar(
                    m8b[:, :], m8b[:, :], st8[:, 2:3], None, ALU.is_ge))
                bb.op("vector", lambda e: e.tensor_scalar_mul(m8[:, :], m8[:, :], st8[:, 4:5]))
                bb.op("vector", lambda e: e.tensor_scalar(
                    m8b[:, :], m8b[:, :], st8[:, 5:6], None, ALU.mult))
                gv = bb.op("vector", lambda e: e.tensor_tensor(grow[:, :], m8[:, :], m8b[:, :], ALU.add))
                g8 = prog.sbuf("g8", [8, 1], F32)
                ps4, idx4 = tppf.acquire(bb)
                t4 = bb.op("tensor", lambda e, ps4=ps4: e.transpose(
                    ps4[0:8, 0:1], grow[0:1, 0:8], csts["cst_identf"][0:1, 0:1]),
                    waits=[(ve, gv)], mark=True)
                tppf.consume(bb, idx4, t4, [
                    lambda e, ps4=ps4: e.tensor_copy(g8[:, :], ps4[0:8, 0:1])])
                ps5, idx5 = gp.acquire(bb)
                t5 = gp.produce(bb, lambda e, ps5=ps5: e.matmul(
                    ps5[:, 0:1], exp8[:, :], g8[:, :], start=True, stop=True),
                    waits=[(ve, prog.chain["vector"].v)])
                gp.consume(bb, idx5, t5, [
                    lambda e, ps5=ps5: e.tensor_copy(gcol[:, :], ps5[:, 0:1])])

        kT = [prog.sbuf(f"kT{i}", [128, T], BF16) for i in range(8)]

        def svd_fm(bb, prog, p, writer, pool):
            wm = wdict[p + "Wm"]
            vall = wdict[p + "Vall"]
            urt = wdict[p + "UrT"]
            scol = wdict[p + "S"]
            xvs = prog.sbuf(f"xvs{p}", [128, 512], BF16)
            xvp = PP(prog, f"xvp{p}", [128, 512], 1, "vector")
            xv_tgt = [0]

            def pre_chunk(bb, ci, c0, c1, xbt, lds):
                ps, idx = xvp.acquire(bb)
                for k in range(8):
                    fn = lambda e, k=k, ps=ps, cw=c1 - c0: e.matmul(
                        ps[:, :cw], vall[k][:, :], xbt[k][:, :cw],
                        start=(k == 0), stop=(k == 7))
                    if k == 7:
                        tgt = xvp.produce(bb, fn)
                    else:
                        bb.op("tensor", fn, waits=wmax(lds) if k == 0 else (), mark=False)
                xv_tgt[0] = xvp.consume(bb, idx, tgt, [
                    lambda e, ps=ps, cw=c1 - c0: e.tensor_scalar(
                        xvs[:, :cw], ps[:, :cw], scol[:, 0:1], gcol[:, 0:1], ALU.mult, ALU.mult)],
                    waits=[(prog.chain["tensor"], prog.chain["tensor"].v)])

            mm_feature_major_streamed(
                bb, prog, wm, xn_stage, E, T, pool, writer, chunk=512,
                extra_rhs=lambda c0, c1: [(xvs[:, :c1 - c0], (ve, xv_tgt[0]))],
                extra_w=[urt[0]], pre_chunk=pre_chunk)

        wdict = {}
        with prog.scope():
            with prog.block() as bb:
                for p in ("q", "k"):
                    wdict[p + "Wm"], _ = load_w_bf16(bb, prog, w[p + "WmT"], name=p + "wm")
                    wdict[p + "Vall"], _ = load_w_bf16(bb, prog, w[p + "VallT"], name=p + "va")
                    wdict[p + "UrT"], _ = load_w_bf16(bb, prog, w[p + "UrT"], name=p + "ur")
                    wdict[p + "S"] = prog.sbuf(p + "S", [128, 1], F32)
                    bb.dma("sync", wdict[p + "S"][:, :], w[p + "S"][:, :])
                    wdict[p + "b"] = prog.sbuf(p + "b", [128, 8], F32)
                    bb.dma("sync", wdict[p + "b"][:, :], w[p + "b"][:, :])
            with prog.block() as bb:
                pk = PP(prog, "pk", [128, 512], 2, "vector")
                svd_fm(bb, prog, "k", bias_fm_writer(prog, pk, kT, wdict["kb"]), pk)
                pq = PP(prog, "pq", [128, 512], 2, "vector")
                svd_fm(bb, prog, "q", stage_fm_chunk_writer(prog, pq, q_stage,
                                                            bias_tile=wdict["qb"]), pq)

        vaug = [prog.sbuf(f"va{i}", [128, H * (dv + 1)], BF16) for i in range(NTB)]
        with prog.scope():
            with prog.block() as bb:
                for p in ("v",):
                    wdict[p + "Wm"], _ = load_w_bf16(bb, prog, w[p + "WmT"], name=p + "wm")
                    wdict[p + "Vall"], _ = load_w_bf16(bb, prog, w[p + "VallT"], name=p + "va")
                    wdict[p + "UrT"], _ = load_w_bf16(bb, prog, w[p + "UrT"], name=p + "ur")
                    wdict[p + "S"] = prog.sbuf(p + "S", [128, 1], F32)
                    bb.dma("sync", wdict[p + "S"][:, :], w[p + "S"][:, :])
                vbrow = prog.sbuf("vbrow", [1, E], F32)
                bb.dma("sync", vbrow[:, :], w["vbrow"][:, :])
            vbrep = prog.sbuf("vbrep", [128, E], F32)
            with prog.block() as bb:
                bp = PP(prog, "bp", [128, 512], 2, "vector")
                broadcast_row(bb, prog, vbrow, E, vbrep, bp)
            with prog.block() as bb:
                pv = PP(prog, "pv", [128, 512], 2, "vector")
                vaug_memset_ones(bb, prog, vaug, dv)
                xvsv = prog.sbuf("xvsv", [128, 128], BF16)
                xvp = PP(prog, "xvpv", [128, 128], 1, "vector")
                xw = [0]

                def extra_act(tb):
                    return [(xvsv[:, :], (ve, xw[0]))]

                def pre_tb(bb, tb, ab_tile, ld):
                    ps, idx = xvp.acquire(bb)
                    for k in range(8):
                        fn = lambda e, k=k, ps=ps: e.matmul(
                            ps[:, :], wdict["vVall"][k][:, :], ab_tile[:, k * 128:(k + 1) * 128],
                            start=(k == 0), stop=(k == 7))
                        if k == 7:
                            tgt = xvp.produce(bb, fn)
                        else:
                            bb.op("tensor", fn, waits=[ld] if k == 0 else (), mark=False)
                    xw[0] = xvp.consume(bb, idx, tgt, [
                        lambda e, ps=ps: e.tensor_scalar(
                            xvsv[:, :], ps[:, :], wdict["vS"][:, 0:1], gcol[:, 0:1],
                            ALU.mult, ALU.mult)],
                        waits=[(prog.chain["tensor"], prog.chain["tensor"].v)])

                mm_token_major_streamed(bb, prog, xn_stage, E, wdict["vWm"], E, pv,
                                        vaug_tm_writer(prog, pv, vaug, dv, bias_rep=vbrep),
                                        extra_act=extra_act, extra_w=[wdict["vUrT"][0]],
                                        pre_tb=pre_tb)

        attention_heads(prog, q_stage=q_stage,
                        kT_ap=lambda h: kT[h // 2][(h % 2) * 64:(h % 2) * 64 + dh, :],
                        vaug=vaug, oT_stage=o_stage, H=H, dh=dh, dv=dv,
                        score_scale=0.125, cconst=20.0)

        with prog.scope():
            with prog.block() as bb:
                for p in ("o",):
                    wdict[p + "Wm"], _ = load_w_bf16(bb, prog, w[p + "WmT"], name=p + "wm")
                    wdict[p + "Vall"], _ = load_w_bf16(bb, prog, w[p + "VallT"], name=p + "va")
                    wdict[p + "UrT"], _ = load_w_bf16(bb, prog, w[p + "UrT"], name=p + "ur")
                    wdict[p + "S"] = prog.sbuf(p + "S", [128, 1], F32)
                    bb.dma("sync", wdict[p + "S"][:, :], w[p + "S"][:, :])
                obrow = prog.sbuf("obrow", [1, E], F32)
                bb.dma("sync", obrow[:, :], w["obrow"][:, :])
            obrep = prog.sbuf("obrep", [128, E], F32)
            with prog.block() as bb:
                bp2 = PP(prog, "bp2", [128, 512], 2, "vector")
                broadcast_row(bb, prog, obrow, E, obrep, bp2)

            xvso = prog.sbuf("xvso", [128, 512], BF16)
            xvpo = PP(prog, "xvpo", [128, 512], 1, "vector")
            xwo = [0]

            def blk_pre(bb, blk0, a_ch, ldw):
                ps, idx = xvpo.acquire(bb)
                for k in range(8):
                    fn = lambda e, k=k, ps=ps: e.matmul(
                        ps[:, :], a_ch[k][:, :], wdict["oVallT_rhs"][k][:, :] if False else None,
                        start=(k == 0), stop=(k == 7))
                for k in range(8):
                    fn = lambda e, k=k, ps=ps: e.matmul(
                        ps[:, :], wdict["oVall"][k][:, :], a_ch[k][:, :],
                        start=(k == 0), stop=(k == 7))
                    if k == 7:
                        tgt = xvpo.produce(bb, fn)
                    else:
                        bb.op("tensor", fn, waits=ldw if k == 0 else (), mark=False)
                xwo[0] = xvpo.consume(bb, idx, tgt, [
                    lambda e, ps=ps: e.tensor_scalar(
                        xvso[:, :], ps[:, :], wdict["oS"][:, 0:1], gcol[:, 0:1],
                        ALU.mult, ALU.mult)],
                    waits=[(prog.chain["tensor"], prog.chain["tensor"].v)])

            def extra_pairs(tb_loc):
                return [(xvso[:, tb_loc * 128:(tb_loc + 1) * 128],
                         wdict["oUrT"][0], (ve, xwo[0]))]

            out_proj_residual(prog, act_stage=o_stage, n_act=H * dv,
                              wo_tiles=wdict["oWm"], x_hbm=x_hbm, xout_hbm=xout_hbm,
                              row_bias_rep=obrep, blk_pre=blk_pre, extra_pairs=extra_pairs)


def sub_l2_attn_a(prog: Prog, x_hbm, xout_hbm, w, uid):
    csts = prog.consts_tiles
    ident = csts["cst_ident"]
    identf = csts["cst_identf"]
    NEH = 16
    s_qk = float(1.0 / np.sqrt(E))
    q_tok = prog.dram(f"stg_q_{uid}", [T, E], BF16)
    k_tok = prog.dram(f"stg_k_{uid}", [T, E], BF16)
    xn_stage = prog.dram(f"stg_xn_{uid}", [E, T], BF16)
    rv_stage = prog.dram(f"stg_rv_{uid}", [E, T], BF16)
    R_hbm = prog.dram(f"stg_R_{uid}", [16, T], F32)
    act, ve, pe = prog.chain["scalar"], prog.chain["vector"], prog.chain["tensor"]

    with prog.scope():
        with prog.block() as bb:
            bqrow = prog.sbuf("bqrow", [1, E], F32)
            bkrow = prog.sbuf("bkrow", [1, E], F32)
            borow = prog.sbuf("borow", [1, E], F32)
            bvcol = prog.sbuf("bvcol", [128, 8], F32)
            bb.dma("sync", bqrow[:, :], w["bq_row"][:, :])
            bb.dma("sync", bkrow[:, :], w["bk_row"][:, :])
            bb.dma("sync", borow[:, :], w["bo_row"][:, :])
            bb.dma("sync", bvcol[:, :], w["bv_col"][:, :])
        borep = prog.sbuf("borep", [128, E], BF16)
        with prog.block() as bb:
            bp = PP(prog, "brc", [128, 512], 1, "vector")
            broadcast_row(bb, prog, borow, E, borep, bp)

        with prog.scope():
            xnT = [prog.sbuf(f"xnT{i}", [128, T], BF16) for i in range(8)]
            bqrep = prog.sbuf("bqrep", [128, E], F32)
            bkrep = prog.sbuf("bkrep", [128, E], F32)
            with prog.block() as bb:
                bp2 = PP(prog, "brc2", [128, 512], 1, "vector")
                broadcast_row(bb, prog, bqrow, E, bqrep, bp2)
                broadcast_row(bb, prog, bkrow, E, bkrep, bp2)
                bb.op("vector", lambda e: e.tensor_scalar_mul(bqrep[:, :], bqrep[:, :], s_qk))
                bb.op("vector", lambda e: e.tensor_scalar_mul(bkrep[:, :], bkrep[:, :], s_qk))
            with prog.block() as bb:
                wq, _ = load_w_bf16(bb, prog, w["W_q"], name="wq")
                wk, _ = load_w_bf16(bb, prog, w["W_k"], name="wk")
                ln_to_xnT(bb, prog, x_hbm, xnT, ident)
            with prog.block() as bb:
                pq = PP(prog, "pq", [128, 512], 2, "vector")
                qt = [prog.sbuf(f"qt{i}", [128, 512], BF16) for i in range(4)]
                cnt = [0]
                sth = {}

                def tok_stage_writer(stage, brep):
                    def wr(bb, tb, c0, c1, ps, idx, tgt):
                        k = cnt[0] % 4
                        wv = [sth[cnt[0] - 4]] if cnt[0] >= 4 else []
                        last = pq.consume(bb, idx, tgt, [
                            lambda e, ps=ps, k=k, c0=c0, c1=c1, brep=brep: e.scalar_tensor_tensor(
                                qt[k][:, :c1 - c0], ps[:, :c1 - c0], s_qk, brep[:, c0:c1],
                                ALU.mult, ALU.add)], waits=wv)
                        sth[cnt[0]] = bb.dma("sync", stage[tb * 128:(tb + 1) * 128, c0:c1],
                                             qt[k][:, :c1 - c0], waits=[(ve, last)])
                        cnt[0] += 1
                    return wr
                mm_token_major(bb, prog, xnT, wq, E, pq, tok_stage_writer(q_tok, bqrep))
                mm_token_major(bb, prog, xnT, wk, E, pq, tok_stage_writer(k_tok, bkrep))
                for i in range(8):
                    bb.dma("sync", xn_stage[i * 128:(i + 1) * 128, :], xnT[i][:, :])

        vT = [prog.sbuf(f"vT{i}", [128, T], BF16) for i in range(8)]
        with prog.scope():
            with prog.block() as bb:
                wv_, _ = load_w_bf16(bb, prog, w["W_v"], name="wv")
            with prog.block() as bb:
                pv = PP(prog, "pv", [128, 512], 2, "vector")
                mm_feature_major_streamed(bb, prog, wv_, xn_stage, E, T, pv,
                                          bias_fm_writer(prog, pv, vT, bvcol))

        q_flat = q_tok.rearrange("t (f g) -> (t f) g", g=64)
        k_flat = k_tok.rearrange("t (f g) -> (t f) g", g=64)
        with prog.scope():
            qe = prog.sbuf("qe", [128, 32 * 64], BF16)
            ke = prog.sbuf("ke", [128, 32 * 64], BF16)
            qeT = prog.sbuf("qeT", [64, T], BF16)
            keT = prog.sbuf("keT", [64, T], BF16)
            prodqk = prog.sbuf("prodqk", [64, T], BF16)
            diag_row = prog.sbuf("diagr", [1, T], F32)
            r_row = prog.sbuf("rrow", [1, T], F32)
            colp = prog.sbuf("colp", [128, 8], F32)
            cs_col = prog.sbuf("cscol", [128, 32], F32)
            dg_col = prog.sbuf("dgcol", [128, 32], F32)
            scr = prog.sbuf("scr", [128, 512], F32)
            rg = prog.sbuf("rg", [1, 256], F32)
            tp = PP(prog, "tpe", [64, 512], 2, "vector", dtype=BF16)
            dp = PP(prog, "dpe", [1, 512], 1, "vector")
            sp = PP(prog, "spe", [128, 512], 2, "scalar")
            tcp = PP(prog, "tcp", [128, 32], 1, "vector")
            trp = PP(prog, "trp", [1, 512], 1, "vector")
            for e_ in range(NEH):
                with prog.block() as bb:
                    ld1 = bb.dma("sync", qe[:, :].rearrange("p (b d) -> p b d", d=64),
                                 q_flat.rearrange("(e b p) d -> e p b d", e=16, p=128)[e_])
                    ld2 = bb.dma("sync", ke[:, :].rearrange("p (b d) -> p b d", d=64),
                                 k_flat.rearrange("(e b p) d -> e p b d", e=16, p=128)[e_])
                    for which, (tile, dst, ldw) in enumerate(
                            ((qe, qeT, ld1), (ke, keT, ld2))):
                        for cc in range(8):
                            ps, idx = tp.acquire(bb)
                            for j in range(4):
                                blk = cc * 4 + j
                                tt = bb.op("tensor", lambda e, tile=tile, blk=blk, j=j, ps=ps:
                                           e.transpose(ps[:, j * 128:(j + 1) * 128],
                                                       tile[:, blk * 64:(blk + 1) * 64],
                                                       ident[:, :]),
                                           waits=[ldw] if (cc == 0 and j == 0) else (),
                                           mark=(j == 3))
                            tp.consume(bb, idx, tt, [
                                lambda e, dst=dst, cc=cc, ps=ps: e.tensor_copy(
                                    dst[:, cc * 512:(cc + 1) * 512], ps[:, :])])
                    dv = bb.op("vector", lambda e: e.tensor_tensor(
                        prodqk[:, :], qeT[:, :], keT[:, :], ALU.mult))
                    for cc in range(8):
                        ps, idx = dp.acquire(bb)
                        tgt = dp.produce(bb, lambda e, cc=cc, ps=ps: e.matmul(
                            ps[0:1, :], csts["cst_ones128"][0:64, :],
                            prodqk[:, cc * 512:(cc + 1) * 512], start=True, stop=True),
                            waits=[(ve, dv)] if cc == 0 else ())
                        dp.consume(bb, idx, tgt, [
                            lambda e, cc=cc, ps=ps: e.tensor_copy(
                                diag_row[:, cc * 512:(cc + 1) * 512], ps[0:1, :])])
                    rs_prev = 0
                    for bblk in range(32):
                        for ac in range(8):
                            ps, idx = sp.acquire(bb)
                            tgt = sp.produce(bb, lambda e, bblk=bblk, ac=ac, ps=ps: e.matmul(
                                ps[:, :], keT[:, bblk * 128:(bblk + 1) * 128],
                                qeT[:, ac * 512:(ac + 1) * 512], start=True, stop=True))
                            sp.consume(bb, idx, tgt, [
                                lambda e, ac=ac, ps=ps: e.activation(
                                    scr[:, :], ps[:, :], AF.Exp, accum_out=colp[:, ac:ac + 1])],
                                waits=[(ve, rs_prev)] if (ac == 0 and rs_prev) else ())
                        rs_prev = bb.op("vector", lambda e, bblk=bblk: e.reduce_sum(
                            cs_col[:, bblk:bblk + 1], colp[:, :], axis=AX.X),
                            waits=[(act, prog.chain["scalar"].v)])
                    psd, idxd = tcp.acquire(bb)
                    for cc in range(32):
                        td = bb.op("tensor", lambda e, cc=cc, psd=psd: e.transpose(
                            psd[:, cc:cc + 1], diag_row[0:1, cc * 128:(cc + 1) * 128],
                            identf[0:1, 0:1]),
                            waits=[(ve, prog.chain["vector"].v)] if cc == 0 else (),
                            mark=(cc == 31))
                    tcp.consume(bb, idxd, td, [
                        lambda e, psd=psd: e.tensor_copy(dg_col[:, :], psd[:, :])])
                    ae = bb.op("scalar", lambda e: e.activation(dg_col[:, :], dg_col[:, :], AF.Exp),
                               waits=[(ve, prog.chain["vector"].v)])
                    rv1 = bb.op("vector", lambda e: e.reciprocal(cs_col[:, :], cs_col[:, :]),
                                waits=[(act, ae)])
                    rv2 = bb.op("vector", lambda e: e.tensor_tensor(
                        dg_col[:, :], dg_col[:, :], cs_col[:, :], ALU.mult))
                    for cc in range(8):
                        ps, idx = trp.acquire(bb)
                        for j in range(4):
                            blk = cc * 4 + j
                            tr = bb.op("tensor", lambda e, blk=blk, j=j, ps=ps: e.transpose(
                                ps[0:1, j * 128:(j + 1) * 128], dg_col[:, blk:blk + 1],
                                identf[:, :]),
                                waits=[(ve, rv2)] if (cc == 0 and j == 0) else (),
                                mark=(j == 3))
                        trp.consume(bb, idx, tr, [
                            lambda e, cc=cc, ps=ps: e.tensor_copy(
                                r_row[:, cc * 512:(cc + 1) * 512], ps[0:1, :])])
                    st_rg = None
                    for g in range(16):
                        cv = bb.op("vector", lambda e, g=g: e.tensor_copy(
                            rg[:, :], r_row.rearrange("a (u g) -> a g u", g=16)[:, g, :]),
                            waits=[st_rg] if st_rg else ())
                        st_rg = bb.dma("sync", R_hbm[g:g + 1, e_ * 256:(e_ + 1) * 256], rg[:, :],
                                       waits=[(ve, cv)])

        with prog.scope():
            r2a = prog.sbuf("r2a", [1, T], F32)
            r2b = prog.sbuf("r2b", [1, T], F32)
            with prog.block() as bb:
                rp = PP(prog, "rvp", [128, 512], 2, "vector")
                for kidx in range(8):
                    l1 = bb.dma("sync", r2a[:, :], R_hbm[2 * kidx:2 * kidx + 1, :],
                                waits=[(pe, prog.chain["tensor"].v)])
                    l2 = bb.dma("sync", r2b[:, :], R_hbm[2 * kidx + 1:2 * kidx + 2, :])
                    for c0 in range(0, T, 512):
                        ps, idx = rp.acquire(bb)
                        bb.op("tensor", lambda e, c0=c0, ps=ps: e.matmul(
                            ps[:, :], csts["cst_maskA"][:, :], r2a[:, c0:c0 + 512],
                            start=True, stop=False),
                            waits=wmax([l1, l2]) if c0 == 0 else (), mark=False)
                        tgt = rp.produce(bb, lambda e, c0=c0, ps=ps: e.matmul(
                            ps[:, :], csts["cst_maskB"][:, :], r2b[:, c0:c0 + 512],
                            start=False, stop=True))
                        rp.consume(bb, idx, tgt, [
                            lambda e, kidx=kidx, c0=c0, ps=ps: e.tensor_tensor(
                                vT[kidx][:, c0:c0 + 512], vT[kidx][:, c0:c0 + 512],
                                ps[:, :], ALU.mult)])
                    bb.dma("sync", rv_stage[kidx * 128:(kidx + 1) * 128, :], vT[kidx][:, :],
                           waits=[(ve, prog.chain["vector"].v)])

        with prog.block() as bb:
            wo, _ = load_w_bf16(bb, prog, w["W_o"], name="wo")
        out_proj_residual(prog, act_stage=rv_stage, n_act=E, wo_tiles=wo,
                          x_hbm=x_hbm, xout_hbm=xout_hbm, row_bias_rep=borep)


def sub_l0_attn_b(prog: Prog, x_hbm, xout_hbm, w, uid):
    csts = prog.consts_tiles
    ident = csts["cst_ident"]
    NEH, dh, dv, LP = 8, 64, 128, 4096
    SC = 0.125
    CC = 4.0
    q_tok = prog.dram(f"stg_q_{uid}", [T, 512], BF16)
    k_tok = prog.dram(f"stg_k_{uid}", [T, 512], BF16)
    v_tok = prog.dram(f"stg_v_{uid}", [T, E], BF16)
    e_stage = prog.dram(f"stg_e_{uid}", [LP, LP], BF16)
    o_tok = prog.dram(f"stg_o_{uid}", [T, E], BF16)
    act, ve, pe = prog.chain["scalar"], prog.chain["vector"], prog.chain["tensor"]

    with prog.scope():
        with prog.scope():
            xnT = [prog.sbuf(f"xnT{i}", [128, T], BF16) for i in range(8)]
            with prog.block() as bb:
                mq, _ = load_w_bf16(bb, prog, w["Mq"], name="mq")
                mk, _ = load_w_bf16(bb, prog, w["Mk"], name="mk")
                mv, _ = load_w_bf16(bb, prog, w["Mv"], name="mv")
                ln_to_xnT(bb, prog, x_hbm, xnT, ident)
            with prog.block() as bb:
                pq = PP(prog, "pq", [128, 512], 2, "vector")
                qt = [prog.sbuf(f"qt{i}", [128, 512], BF16) for i in range(4)]
                cnt = [0]
                sth = {}

                def stg_writer(stage):
                    def wr(bb, tb, c0, c1, ps, idx, tgt):
                        k = cnt[0] % 4
                        wv = [sth[cnt[0] - 4]] if cnt[0] >= 4 else []
                        last = pq.consume(bb, idx, tgt, [
                            lambda e, ps=ps, k=k, c0=c0, c1=c1: e.tensor_copy(
                                qt[k][:, :c1 - c0], ps[:, :c1 - c0])], waits=wv)
                        sth[cnt[0]] = bb.dma("sync", stage[tb * 128:(tb + 1) * 128, c0:c1],
                                             qt[k][:, :c1 - c0], waits=[(ve, last)])
                        cnt[0] += 1
                    return wr
                mm_token_major(bb, prog, xnT, mq, 512, pq, stg_writer(q_tok))
                mm_token_major(bb, prog, xnT, mk, 512, pq, stg_writer(k_tok))
                mm_token_major(bb, prog, xnT, mv, E, pq, stg_writer(v_tok))

        q_flat = q_tok.rearrange("t (a g) -> (t a) g", g=64)
        k_flat = k_tok.rearrange("t (a g) -> (t a) g", g=64)
        v_flat = v_tok.rearrange("t (a g) -> (t a) g", g=128)
        o_flat = o_tok.rearrange("t (a g) -> (t a) g", g=128)

        with prog.scope():
            qe = prog.sbuf("qe", [128, 32 * 64], BF16)
            ke = prog.sbuf("ke", [128, 32 * 64], BF16)
            qeT = prog.sbuf("qeT", [64, LP], BF16)
            keT = prog.sbuf("keT", [64, LP], BF16)
            colp = prog.sbuf("colp", [128, 8], F32)
            cs_col = prog.sbuf("cscol", [128, 32], F32)
            scr = prog.sbuf("scr", [128, 512], F32)
            etile = [prog.sbuf(f"et{i}", [128, 512], BF16) for i in range(3)]
            vs = prog.sbuf("vs", [128, 32 * 128], BF16)
            otT = prog.sbuf("otT", [128, LP], BF16)
            tp = PP(prog, "tpe", [64, 512], 2, "vector", dtype=BF16)
            sp = PP(prog, "spe", [128, 512], 2, "scalar")
            ap_ = PP(prog, "ape", [128, 512], 2, "vector")
            otp = PP(prog, "otp", [128, 128], 2, "vector", dtype=BF16)
            for e_ in range(NEH):
                with prog.block() as bb:
                    ld1 = bb.dma("sync", qe[:, :].rearrange("p (b d) -> p b d", d=64),
                                 q_flat.rearrange("(e b p) d -> e p b d", e=NEH, p=128)[e_])
                    ld2 = bb.dma("sync", ke[:, :].rearrange("p (b d) -> p b d", d=64),
                                 k_flat.rearrange("(e b p) d -> e p b d", e=NEH, p=128)[e_])
                    for which, (tile, dst, ldw) in enumerate(((qe, qeT, ld1), (ke, keT, ld2))):
                        for cc in range(8):
                            ps, idx = tp.acquire(bb)
                            for j in range(4):
                                blk = cc * 4 + j
                                tt = bb.op("tensor", lambda e, tile=tile, blk=blk, j=j, ps=ps:
                                           e.transpose(ps[:, j * 128:(j + 1) * 128],
                                                       tile[:, blk * 64:(blk + 1) * 64],
                                                       ident[:, :]),
                                           waits=[ldw] if (cc == 0 and j == 0) else (),
                                           mark=(j == 3))
                            tp.consume(bb, idx, tt, [
                                lambda e, dst=dst, cc=cc, ps=ps: e.tensor_copy(
                                    dst[:, cc * 512:(cc + 1) * 512], ps[:, :])])
                    rs_prev = 0
                    ecnt = 0
                    esth = {}
                    for jb in range(32):
                        for ic in range(8):
                            ps, idx = sp.acquire(bb)
                            tgt = sp.produce(bb, lambda e, jb=jb, ic=ic, ps=ps: e.matmul(
                                ps[:, :], keT[:, jb * 128:(jb + 1) * 128],
                                qeT[:, ic * 512:(ic + 1) * 512], start=True, stop=True))
                            eb = etile[ecnt % 3]
                            wv = [(ve, rs_prev)] if (ic == 0 and rs_prev) else []
                            if ecnt >= 3:
                                wv.append(esth[ecnt - 3])
                            a1 = sp.consume(bb, idx, tgt, [
                                lambda e, ps=ps, eb=eb, ic=ic, _b=prog.constf(-CC): e.activation(
                                    eb[:, :], ps[:, :], AF.Exp, bias=_b,
                                    scale=SC, accum_out=colp[:, ic:ic + 1])], waits=wv)
                            esth[ecnt] = bb.dma("sync",
                                                e_stage[jb * 128:(jb + 1) * 128,
                                                        ic * 512:(ic + 1) * 512],
                                                eb[:, :], waits=[(act, a1)])
                            ecnt += 1
                        rs_prev = bb.op("vector", lambda e, jb=jb: e.reduce_sum(
                            cs_col[:, jb:jb + 1], colp[:, :], axis=AX.X),
                            waits=[(act, prog.chain["scalar"].v)])
                    bb.op("vector", lambda e: e.reciprocal(cs_col[:, :], cs_col[:, :]))
                    ldv = bb.dma("sync", vs[:, :].rearrange("p (b d) -> p b d", d=128),
                                 v_flat.rearrange("(e b p) d -> e p b d", e=NEH, p=128)[e_])
                    sv = bb.op("vector", lambda e: e.tensor_copy(vs[:, :], vs[:, :]) if False
                               else e.engine_nop(), waits=[ldv])
                    for jb in range(32):
                        sv = bb.op("vector", lambda e, jb=jb: e.tensor_scalar_mul(
                            vs[:, jb * 128:(jb + 1) * 128], vs[:, jb * 128:(jb + 1) * 128],
                            cs_col[:, jb:jb + 1]))
                with prog.block() as bb:
                    ecnt = 0
                    ldh = {}
                    for ic in range(8):
                        ps, idx = ap_.acquire(bb)
                        for jb in range(32):
                            eb = etile[ecnt % 3]
                            wv = []
                            if ecnt >= 3:
                                wv = [(pe, ldh[ecnt - 3])]
                            ldi = bb.dma("sync", eb[:, :],
                                         e_stage[jb * 128:(jb + 1) * 128,
                                                 ic * 512:(ic + 1) * 512], waits=wv)
                            mmt = bb.op("tensor", lambda e, jb=jb, eb=eb, ps=ps, last=(jb == 31):
                                        e.matmul(ps[:, :], vs[:, jb * 128:(jb + 1) * 128],
                                                 eb[:, :], start=(jb == 0), stop=last),
                                        waits=[ldi], mark=True)
                            ldh[ecnt] = mmt
                            ecnt += 1
                        ap_.consume(bb, idx, mmt, [
                            lambda e, ic=ic, ps=ps: e.tensor_copy(
                                otT[:, ic * 512:(ic + 1) * 512], ps[:, :])])
                    ost = None
                    for blk in range(32):
                        ps, idx = otp.acquire(bb)
                        tt = otp.produce(bb, lambda e, blk=blk, ps=ps: e.transpose(
                            ps[:, :], otT[:, blk * 128:(blk + 1) * 128], ident[:, :]),
                            waits=[(ve, prog.chain["vector"].v)] if blk == 0 else ())
                        ob = etile[blk % 3]
                        last = otp.consume(bb, idx, tt, [
                            lambda e, ps=ps, ob=ob: e.tensor_copy(ob[:, 0:128], ps[:, :])],
                            waits=[ost] if ost else ())
                        ost = bb.dma("sync",
                                     o_flat.rearrange("(e b p) d -> e p b d", e=NEH, p=128)[e_][:, blk, :],
                                     ob[:, 0:128], waits=[(ve, last)])
        add_residual_pass(prog, x_hbm, o_tok, xout_hbm, y_dtype=BF16)


def sub_l2_attn_b(prog: Prog, x_hbm, xout_hbm, w, uid):
    H, dh, dv = 16, 64, 64
    LH, LHD, NLAT = 4, 64, 64
    csts = prog.consts_tiles
    ident = csts["cst_ident"]
    q_stage = prog.dram(f"stg_q_{uid}", [H * dh, T], BF16)
    o_stage = prog.dram(f"stg_o_{uid}", [H * dv, T], BF16)
    xn_stage = prog.dram(f"stg_xn_{uid}", [E, T], BF16)
    rem_stage = prog.dram(f"stg_rem_{uid}", [E, T], BF16)
    mix_stage = prog.dram(f"stg_mix_{uid}", [E, T], BF16)
    act, ve, pe = prog.chain["scalar"], prog.chain["vector"], prog.chain["tensor"]

    with prog.scope():
        pooledT = [prog.sbuf(f"plT{i}", [128, NLAT], F32) for i in range(8)]
        rqT = [prog.sbuf(f"rqT{i}", [128, T], BF16) for i in range(2)]
        rkvT = [prog.sbuf(f"rkvT{i}", [128, NLAT], F32) for i in range(4)]
        with prog.scope():
            xnT = [prog.sbuf(f"xnT{i}", [128, T], BF16) for i in range(8)]
            with prog.block() as bb:
                rqw, _ = load_w_bf16(bb, prog, w["rq_w"], name="rqw")
                rqb = prog.sbuf("rqb", [128, 2], F32)
                bb.dma("sync", rqb[:, :], w["rq_b_col"][:, :])
                info = ln_to_xnT(bb, prog, x_hbm, xnT, ident)
                lnw = [(ve, info["ve_done"]), (act, info["act_done"])]
                for i in range(8):
                    bb.dma("sync", xn_stage[i * 128:(i + 1) * 128, :], xnT[i][:, :],
                           waits=lnw if i == 0 else ())
            with prog.block() as bb:
                for i in range(8):
                    for c in range(NLAT):
                        bb.op("vector", lambda e, i=i, c=c: e.reduce_sum(
                            pooledT[i][:, c:c + 1], xnT[i][:, c * 64:(c + 1) * 64], axis=AX.X))
                    bb.op("vector", lambda e, i=i: e.tensor_scalar_mul(
                        pooledT[i][:, :], pooledT[i][:, :], 1.0 / 64))
                pq = PP(prog, "prq", [128, 512], 2, "vector")
                mm_feature_major(bb, prog, rqw, xnT, T, pq,
                                 bias_fm_writer(prog, pq, rqT, rqb), m_blocks=2)

        with prog.scope():
            with prog.block() as bb:
                rkvw, rkvw_t = load_w_bf16(bb, prog, w["rkv_w"], name="rkvw")
                rkvb = prog.sbuf("rkvb", [128, 4], F32)
                rkvb_ld = bb.dma("sync", rkvb[:, :], w["rkv_b_col"][:, :])
                plb = [prog.sbuf(f"plb{i}", [128, NLAT], BF16) for i in range(8)]
                for i in range(8):
                    bb.op("vector", lambda e, i=i: e.tensor_copy(plb[i][:, :], pooledT[i][:, :]))
                prk = PP(prog, "prk", [128, 64], 2, "vector")
                for mb in range(4):
                    ps, idx = prk.acquire(bb)
                    for k in range(8):
                        fn = lambda e, k=k, mb=mb, ps=ps: e.matmul(
                            ps[:, :NLAT], rkvw[k][:, mb * 128:(mb + 1) * 128],
                            plb[k][:, :], start=(k == 0), stop=(k == 7))
                        if k == 7:
                            tgt = prk.produce(bb, fn)
                        else:
                            bb.op("tensor", fn,
                                  waits=([(ve, prog.chain["vector"].v)] + rkvw_t
                                         if (mb == 0 and k == 0) else ()),
                                  mark=False)
                    prk.consume(bb, idx, tgt, [
                        lambda e, mb=mb, ps=ps: e.tensor_scalar(
                            rkvT[mb][:, :], ps[:, :NLAT], rkvb[:, mb:mb + 1], None, ALU.add)],
                        waits=[rkvb_ld] if mb == 0 else ())

        kT = [prog.sbuf(f"kT{i}", [128, T], BF16) for i in range(8)]
        with prog.scope():
            with prog.scope():
                with prog.block() as bb:
                    wkT, _ = load_w_bf16(bb, prog, w["wk"], name="wk")
                    wqT, _ = load_w_bf16(bb, prog, w["wq"], name="wq")
                    bkc = prog.sbuf("bkc", [128, 8], F32)
                    bqc = prog.sbuf("bqc", [128, 8], F32)
                    bb.dma("sync", bkc[:, :], w["bk_col"][:, :])
                    bb.dma("sync", bqc[:, :], w["bq_col"][:, :])
                with prog.block() as bb:
                    pk = PP(prog, "pk", [128, 512], 2, "vector")
                    mm_feature_major_streamed(bb, prog, wkT, xn_stage, E, T, pk,
                                              bias_fm_writer(prog, pk, kT, bkc))
                    pq2 = PP(prog, "pq2", [128, 512], 2, "vector")
                    mm_feature_major_streamed(bb, prog, wqT, xn_stage, E, T, pq2,
                                              stage_fm_chunk_writer(prog, pq2, q_stage,
                                                                    bias_tile=bqc))
            vaug = [prog.sbuf(f"va{i}", [128, H * (dv + 1)], BF16) for i in range(NTB)]
            with prog.scope():
                with prog.block() as bb:
                    wvT, _ = load_w_bf16(bb, prog, w["wv"], name="wv")
                    bvrow = prog.sbuf("bvrow", [1, E], F32)
                    bb.dma("sync", bvrow[:, :], w["bv_row"][:, :])
                bvrep = prog.sbuf("bvrep", [128, E], F32)
                with prog.block() as bb:
                    bp = PP(prog, "brc", [128, 512], 1, "vector")
                    broadcast_row(bb, prog, bvrow, E, bvrep, bp)
                with prog.block() as bb:
                    pv = PP(prog, "pv", [128, 512], 2, "vector")
                    vaug_memset_ones(bb, prog, vaug, dv)
                    mm_token_major_streamed(bb, prog, xn_stage, E, wvT, E, pv,
                                            vaug_tm_writer(prog, pv, vaug, dv, bias_rep=bvrep))

            band = csts["cst_band"]

            def kb_range(q0, qcw):
                lo = max(0, (q0 - 128) // 128)
                hi = min(T // 128, ceil_div(q0 + qcw, 128))
                return range(lo, hi)

            def mask_ap(h, kb, q0, qcw):
                off = 640 + q0 - kb * 128
                return band[:, off:off + qcw]

            attention_heads(prog, q_stage=q_stage,
                            kT_ap=lambda h: kT[h // 2][(h % 2) * 64:(h % 2) * 64 + dh, :],
                            vaug=vaug, oT_stage=o_stage, H=H, dh=dh, dv=dv,
                            score_scale=0.125, cconst=10.0, kb_range=kb_range,
                            mask_ap=mask_ap, denom_eps=1e-30)

        latoutT = [prog.sbuf(f"loT{i}", [128, T], BF16) for i in range(2)]
        with prog.scope():
            with prog.block() as bb:
                rva = prog.sbuf("rva", [64, LH * (LHD + 1)], BF16)
                rkb16 = [prog.sbuf(f"rkb16{i}", [128, NLAT], BF16) for i in range(2)]
                rvb16 = [prog.sbuf(f"rvb16{i}", [128, NLAT], BF16) for i in range(2)]
                for i in range(2):
                    bb.op("vector", lambda e, i=i: e.tensor_copy(
                        rkb16[i][:, :], rkvT[i][:, :]))
                    bb.op("vector", lambda e, i=i: e.tensor_copy(
                        rvb16[i][:, :], rkvT[2 + i][:, :]))
                bb.op("vector", lambda e: e.memset(
                    rva[:, :].rearrange("p (h d) -> p h d", d=LHD + 1)[:, :, LHD:LHD + 1], 1.0))
                tpv = PP(prog, "tpv", [64, 128], 2, "vector", dtype=BF16)
                for i in range(2):
                    for half in range(2):
                        ps, idx = tpv.acquire(bb)
                        tt = tpv.produce(bb, lambda e, i=i, half=half, ps=ps: e.transpose(
                            ps[:, 0:64], rvb16[i][half * 64:(half + 1) * 64, :],
                            ident[half * 64:(half + 1) * 64, half * 64:(half + 1) * 64]),
                            waits=[(ve, prog.chain["vector"].v)] if (i == 0 and half == 0) else ())
                        lh = i * 2 + half
                        tpv.consume(bb, idx, tt, [
                            lambda e, lh=lh, ps=ps: e.tensor_copy(
                                rva[:, :].rearrange("p (h d) -> p h d", d=LHD + 1)[:, lh, 0:LHD],
                                ps[:, 0:64])])
                lsp = PP(prog, "lsp", [64, 512], 2, "scalar")
                lav = PP(prog, "lav", [LHD + 1, 512], 2, "vector")
                lrd = PP(prog, "lrd", [64, 512], 1, "vector")
                lexp = prog.sbuf("lexp", [64, 512], BF16)
                lmask = prog.sbuf("lmask", [64, 512], BF16)
                ldrow = prog.sbuf("ldrow", [1, T], F32)
                for lh in range(LH):
                    for qc in range(8):
                        q0 = qc * 512
                        mv = bb.op("vector", lambda e, q0=q0: e.tensor_scalar(
                            lmask[:, :], csts["cst_iotaq"][:, :], float(q0),
                            csts["cst_latb"][0:64, 0:1], ALU.add, ALU.is_ge))
                        ti, r0 = lh // 2, (lh % 2) * 64
                        ps, idx = lsp.acquire(bb)
                        tgt = lsp.produce(bb, lambda e, ti=ti, r0=r0, q0=q0, ps=ps: e.matmul(
                            ps[:, :], rkb16[ti][r0:r0 + 64, :],
                            rqT[ti][r0:r0 + 64, q0:q0 + 512], start=True, stop=True))
                        a1 = lsp.consume(bb, idx, tgt, [
                            lambda e, ps=ps, _b=prog.constf_row(-10.0, rows=64): e.activation(
                                lexp[:, :], ps[:, :], AF.Exp, bias=_b, scale=0.125)])
                        m2 = bb.op("vector", lambda e: e.tensor_tensor(
                            lexp[:, :], lexp[:, :], lmask[:, :], ALU.mult),
                            waits=[(act, a1)])
                        ps2, idx2 = lav.acquire(bb)
                        t2 = bb.op("tensor", lambda e, lh=lh, ps2=ps2: e.matmul(
                            ps2[:, :], rva[:, lh * (LHD + 1):(lh + 1) * (LHD + 1)],
                            lexp[:, :], start=True, stop=True),
                            waits=[(ve, m2)], mark=True)
                        lav.consume(bb, idx2, t2, [
                            lambda e, lh=lh, q0=q0, ps2=ps2: e.tensor_copy(
                                latoutT[lh // 2][(lh % 2) * 64:(lh % 2) * 64 + LHD, q0:q0 + 512],
                                ps2[0:LHD, :]),
                            lambda e, q0=q0, ps2=ps2: e.tensor_copy(
                                ldrow[:, q0:q0 + 512], ps2[LHD:LHD + 1, :]),
                        ])
                    le = bb.op("scalar", lambda e, _b=prog.constf_row(1e-30): e.activation(
                        ldrow[:, :], ldrow[:, :], AF.Ln, bias=_b),
                        waits=[(ve, prog.chain["vector"].v)])
                    le = bb.op("scalar", lambda e: e.activation(
                        ldrow[:, :], ldrow[:, :], AF.Exp, scale=-1.0))
                    for qc in range(8):
                        q0 = qc * 512
                        ps3, idx3 = lrd.acquire(bb)
                        t3 = lrd.produce(bb, lambda e, q0=q0, ps3=ps3: e.matmul(
                            ps3[:, :], csts["cst_onesrow"][:, 0:64], ldrow[:, q0:q0 + 512],
                            start=True, stop=True), waits=[(act, le)] if qc == 0 else ())
                        lrd.consume(bb, idx3, t3, [
                            lambda e, lh=lh, q0=q0, ps3=ps3: e.tensor_tensor(
                                latoutT[lh // 2][(lh % 2) * 64:(lh % 2) * 64 + LHD, q0:q0 + 512],
                                latoutT[lh // 2][(lh % 2) * 64:(lh % 2) * 64 + LHD, q0:q0 + 512],
                                ps3[:, :], ALU.mult)])

        with prog.scope():
            with prog.block() as bb:
                rro, _ = load_w_bf16(bb, prog, w["rout_w"], name="rro")
                rob = prog.sbuf("rob", [128, 8], F32)
                bb.dma("sync", rob[:, :], w["rout_b_col"][:, :])
            with prog.block() as bb:
                pr = PP(prog, "pr", [128, 512], 2, "vector")
                mm_feature_major(bb, prog, rro, latoutT, T, pr,
                                 stage_fm_chunk_writer(prog, pr, rem_stage, bias_tile=rob))

        with prog.scope():
            with prog.block() as bb:
                glc = prog.sbuf("glc", [64, 1], BF16)
                grc = prog.sbuf("grc", [64, 1], BF16)
                glf = prog.sbuf("glf", [64, 1], F32)
                grf = prog.sbuf("grf", [64, 1], F32)
                lg1 = bb.dma("sync", glf[:, :], w["gl"][:, :])
                lg2 = bb.dma("sync", grf[:, :], w["gr"][:, :])
                bb.op("vector", lambda e: e.tensor_copy(glc[:, :], glf[:, :]),
                      waits=wmax([lg1, lg2]))
                bb.op("vector", lambda e: e.tensor_copy(grc[:, :], grf[:, :]))
            loc = prog.sbuf("loc", [64, T], BF16)
            rem = prog.sbuf("rem", [64, T], BF16)
            mix = prog.sbuf("mix", [64, T], BF16)
            grow = prog.sbuf("grow_g", [1, T], F32)
            gp = PP(prog, "gp", [1, 512], 2, "vector")
            gb = PP(prog, "gb", [64, 512], 2, "vector")
            for h in range(H):
                with prog.block() as bb:
                    l1 = bb.dma("sync", loc[:, :], o_stage[h * 64:(h + 1) * 64, :])
                    l2 = bb.dma("sync", rem[:, :], rem_stage[h * 64:(h + 1) * 64, :])
                    for qc in range(8):
                        q0 = qc * 512
                        ps, idx = gp.acquire(bb)
                        bb.op("tensor", lambda e, q0=q0, ps=ps: e.matmul(
                            ps[0:1, :], glc[:, :], loc[:, q0:q0 + 512], start=True, stop=False),
                            waits=wmax([l1, l2]) if qc == 0 else (), mark=False)
                        tg = gp.produce(bb, lambda e, q0=q0, ps=ps: e.matmul(
                            ps[0:1, :], grc[:, :], rem[:, q0:q0 + 512], start=False, stop=True))
                        gp.consume(bb, idx, tg, [
                            lambda e, q0=q0, ps=ps: e.tensor_copy(grow[:, q0:q0 + 512], ps[0:1, :])])
                    sg = bb.op("scalar", lambda e: e.activation(
                        grow[:, :], grow[:, :], AF.Sigmoid),
                        waits=[(ve, prog.chain["vector"].v)])
                    for qc in range(8):
                        q0 = qc * 512
                        ps2, idx2 = gb.acquire(bb)
                        t2 = gb.produce(bb, lambda e, q0=q0, ps2=ps2: e.matmul(
                            ps2[:, :], csts["cst_onesrow"][:, 0:64], grow[:, q0:q0 + 512],
                            start=True, stop=True), waits=[(act, sg)] if qc == 0 else ())
                        gb.consume(bb, idx2, t2, [
                            lambda e, q0=q0: e.tensor_tensor(
                                mix[:, q0:q0 + 512], loc[:, q0:q0 + 512],
                                rem[:, q0:q0 + 512], ALU.subtract),
                            lambda e, q0=q0, ps2=ps2: e.tensor_tensor(
                                mix[:, q0:q0 + 512], mix[:, q0:q0 + 512], ps2[:, :], ALU.mult),
                            lambda e, q0=q0: e.tensor_tensor(
                                mix[:, q0:q0 + 512], mix[:, q0:q0 + 512],
                                rem[:, q0:q0 + 512], ALU.add),
                        ])
                    bb.dma("sync", mix_stage[h * 64:(h + 1) * 64, :], mix[:, :],
                           waits=[(ve, prog.chain["vector"].v)])

        with prog.scope():
            with prog.block() as bb:
                wo, _ = load_w_bf16(bb, prog, w["out_w"], name="wo")
                obrow = prog.sbuf("obrow", [1, E], F32)
                bb.dma("sync", obrow[:, :], w["out_b_row"][:, :])
            obrep = prog.sbuf("obrep", [128, E], BF16)
            with prog.block() as bb:
                bp = PP(prog, "brc3", [128, 512], 1, "vector")
                broadcast_row(bb, prog, obrow, E, obrep, bp)
            out_proj_residual(prog, act_stage=mix_stage, n_act=E, wo_tiles=wo,
                              x_hbm=x_hbm, xout_hbm=xout_hbm, row_bias_rep=obrep)


def sub_l1_attn_a(prog: Prog, x_hbm, xout_hbm, w, uid):
    C = 64
    NC = T // C
    csts = prog.consts_tiles
    ident = csts["cst_ident"]
    xn_f32_hbm = prog.dram(f"stg_xnf_{uid}", [T, E], F32)
    xn_bf_hbm = prog.dram(f"stg_xnb_{uid}", [T, E], BF16)
    v_hbm = prog.dram(f"stg_v_{uid}", [T, E], BF16)
    reads_hbm = prog.dram(f"stg_r_{uid}", [T, E], BF16)
    act, ve, pe = prog.chain["scalar"], prog.chain["vector"], prog.chain["tensor"]

    with prog.scope():
        lgcol = prog.sbuf("lgcol", [128, 1], F32)
        deccol = prog.sbuf("deccol", [64, 1], F32)
        gwcol = prog.sbuf("gwcol", [64, 1], F32)
        gccol = prog.sbuf("gccol", [128, 1], F32)
        alcol = prog.sbuf("alcol", [128, 1], F32)
        mtb = prog.sbuf("mtb", [64, 64], BF16)
        with prog.scope():
         with prog.block() as bb:
            dsc = prog.sbuf("dsc", [1, 2], F32)
            l1 = bb.dma("sync", dsc[:, 0:1], w["decay"][:, :])
            l2 = bb.dma("sync", dsc[:, 1:2], w["log_alpha"][:, :])
            diffT = prog.sbuf("diffT", [64, 64], F32)
            causT = prog.sbuf("causT", [64, 64], F32)
            l3 = bb.dma("sync", diffT[:, :], w["diffT"][:, :])
            l4 = bb.dma("sync", causT[:, :], w["causT"][:, :])
            rev64 = prog.sbuf("rev64", [64, 1], F32)
            l5 = bb.dma("sync", rev64[:, :], w["rev64"][:, :])
            a1 = bb.op("scalar", lambda e: e.activation(dsc[:, 0:1], dsc[:, 0:1], AF.Sigmoid),
                       waits=wmax([l1, l2, l3, l4, l5]))
            a2 = bb.op("scalar", lambda e: e.activation(dsc[:, 0:1], dsc[:, 0:1], AF.Ln))
            bp = PP(prog, "l1abp", [128, 64], 2, "vector")
            ps, idx = bp.acquire(bb)
            t1 = bp.produce(bb, lambda e, ps=ps: e.matmul(
                ps[:, 0:1], csts["cst_onesrow"][:, :], dsc[:, 0:1],
                start=True, stop=True), waits=[(act, a2)])
            bp.consume(bb, idx, t1, [
                lambda e, ps=ps: e.tensor_copy(lgcol[:, :], ps[:, 0:1])])
            a3 = bb.op("scalar", lambda e: e.activation(dsc[:, 1:2], dsc[:, 1:2], AF.Exp))
            ps2, idx2 = bp.acquire(bb)
            t2 = bp.produce(bb, lambda e, ps2=ps2: e.matmul(
                ps2[:, 0:1], csts["cst_onesrow"][:, :], dsc[:, 1:2],
                start=True, stop=True), waits=[(act, a3)])
            bp.consume(bb, idx2, t2, [
                lambda e, ps2=ps2: e.tensor_copy(alcol[:, :], ps2[:, 0:1])])
            a4 = bb.op("scalar", lambda e: e.activation(
                mtb[:, :], diffT[:, :], AF.Exp, scale=lgcol[0:64, 0:1]),
                waits=[(ve, prog.chain["vector"].v)])
            bb.op("vector", lambda e: e.tensor_tensor(mtb[:, :], mtb[:, :], causT[:, :], ALU.mult),
                  waits=[(act, a4)])
            bb.op("scalar", lambda e: e.activation(
                deccol[:, :], csts["cst_iota128"][0:64, :], AF.Exp, scale=lgcol[0:64, 0:1]))
            bb.op("scalar", lambda e: e.activation(
                gwcol[:, :], rev64[:, :], AF.Exp, scale=lgcol[0:64, 0:1]))
            a5 = bb.op("vector", lambda e: e.tensor_scalar_mul(gccol[:, :], lgcol[:, :], 64.0))
            bb.op("scalar", lambda e: e.activation(gccol[:, :], gccol[:, :], AF.Exp),
                  waits=[(ve, a5)])

        xnT = [prog.sbuf(f"xnTp{i}", [128, T + 4], BF16) for i in range(8)]
        with prog.scope():
            with prog.block() as bb:
                for i in range(8):
                    bb.op("vector", lambda e, i=i: e.memset(xnT[i][:, 0:1], 0.0))
                ln_to_xnT(bb, prog, x_hbm, xnT, ident, pad_left=1,
                          xn_f32_hbm=xn_f32_hbm, xn_bf16_hbm=xn_bf_hbm)

        with prog.scope():
            with prog.block() as bb:
                wwT, _ = load_w_bf16(bb, prog, w["W_writeT"], name="wwT")
            with prog.block() as bb:
                pv = PP(prog, "pv", [128, 512], 2, "vector")
                vt = [prog.sbuf(f"vt{i}", [128, 512], BF16) for i in range(4)]
                cnt = [0]
                sth = {}

                def v_writer(bb, tb, c0, c1, ps, idx, tgt):
                    k = cnt[0] % 4
                    wv = [sth[cnt[0] - 4]] if cnt[0] >= 4 else []
                    last = pv.consume(bb, idx, tgt, [
                        lambda e, ps=ps, k=k, c0=c0, c1=c1: e.tensor_copy(
                            vt[k][:, :c1 - c0], ps[:, :c1 - c0])], waits=wv)
                    sth[cnt[0]] = bb.dma("sync", v_hbm[tb * 128:(tb + 1) * 128, c0:c1],
                                         vt[k][:, :c1 - c0], waits=[(ve, last)])
                    cnt[0] += 1
                mm_token_major(bb, prog, xnT, wwT, E, pv, v_writer, act_col_base=1)

        WT = [prog.sbuf(f"WTs{i}", [128, E], BF16) for i in range(8)]
        with prog.scope():
            wk_tok = [prog.sbuf(f"wkt{i}", [64, E], BF16) for i in range(2)]
            v_tok = [prog.sbuf(f"vtk{i}", [64, E], BF16) for i in range(2)]
            vgw = prog.sbuf("vgw", [64, E], BF16)
            smt = prog.sbuf("smt", [64, 64], BF16)
            intra_sb = prog.sbuf("intra", [64, E], F32)
            reads_sb = prog.sbuf("reads", [64, E], BF16)
            stp = PP(prog, "stp", [64, 64], 1, "vector")
            i1p = PP(prog, "i1p", [64, E], 1, "vector")
            i2p = PP(prog, "i2p", [64, E], 1, "vector")
            upp = PP(prog, "upp", [128, E], 1, "vector")
            with prog.block() as bb:
                for i in range(8):
                    bb.op("vector", lambda e, i=i: e.memset(WT[i][:, :], 0.0))
            ld_hist = {}
            st_hist = {}
            for c in range(NC):
                with prog.block() as bb:
                    buf = c % 2
                    if c == 0:
                        mz = bb.op("vector", lambda e: e.memset(wk_tok[0][0:1, :], 0.0))
                        lw = bb.dma("sync", wk_tok[0][1:64, :], xn_bf_hbm[0:63, :])
                    else:
                        lw = bb.dma("sync", wk_tok[buf][:, :],
                                    xn_bf_hbm[c * C - 1:(c + 1) * C - 1, :])
                    lv = bb.dma("sync", v_tok[buf][:, :], v_hbm[c * C:(c + 1) * C, :])
                    gv = bb.op("vector", lambda e, buf=buf: e.tensor_scalar_mul(
                        vgw[:, :], v_tok[buf][:, :], gwcol[:, 0:1]), waits=[lv])
                    ps_st, i_st = stp.acquire(bb)
                    for et in range(8):
                        fn = lambda e, et=et, c=c, ps_st=ps_st: e.matmul(
                            ps_st[:, :], xnT[et][:, c * C:(c + 1) * C],
                            xnT[et][:, c * C + 1:(c + 1) * C + 1],
                            start=(et == 0), stop=(et == 7))
                        if et == 7:
                            t_st = stp.produce(bb, fn)
                        else:
                            bb.op("tensor", fn, mark=False)
                    stp.consume(bb, i_st, t_st, [
                        lambda e, ps_st=ps_st: e.tensor_tensor(
                            smt[:, :], ps_st[:, :], mtb[:, :], ALU.mult)])
                    ps_i1, i_i1 = i1p.acquire(bb)
                    for et in range(8):
                        for fc in range(2):
                            fn = lambda e, et=et, fc=fc, c=c, ps_i1=ps_i1: e.matmul(
                                ps_i1[:, fc * 512:(fc + 1) * 512],
                                xnT[et][:, c * C + 1:(c + 1) * C + 1],
                                WT[et][:, fc * 512:(fc + 1) * 512],
                                start=(et == 0), stop=(et == 7))
                            if et == 7 and fc == 1:
                                t_i1 = i1p.produce(bb, fn)
                            else:
                                bb.op("tensor", fn, mark=False)
                    ps_i2, i_i2 = i2p.acquire(bb)
                    for fc in range(2):
                        fn = lambda e, fc=fc, buf=buf, ps_i2=ps_i2: e.matmul(
                            ps_i2[:, fc * 512:(fc + 1) * 512], smt[:, :],
                            v_tok[buf][:, fc * 512:(fc + 1) * 512],
                            start=True, stop=True)
                        tloc = bb.op("tensor", fn,
                                     waits=[(ve, prog.chain["vector"].v)] if fc == 0 else (),
                                     mark=(fc == 1))
                    t_i2 = tloc
                    i2p.consume(bb, i_i2, t_i2, [
                        lambda e, ps_i2=ps_i2: e.tensor_copy(intra_sb[:, :], ps_i2[:, :])])
                    rd = i1p.consume(bb, i_i1, t_i1, [
                        lambda e, ps_i1=ps_i1: e.scalar_tensor_tensor(
                            reads_sb[:, :], ps_i1[:, :], deccol[:, 0:1], intra_sb[:, :],
                            ALU.mult, ALU.add)])
                    st_hist[c] = bb.dma("sync", reads_hbm[c * C:(c + 1) * C, :],
                                        reads_sb[:, :], waits=[(ve, rd)])
                    for et in range(8):
                        ps_up, i_up = upp.acquire(bb)
                        for fc in range(2):
                            fn = lambda e, et=et, fc=fc, buf=buf, ps_up=ps_up: e.matmul(
                                ps_up[:, fc * 512:(fc + 1) * 512],
                                wk_tok[buf][:, et * 128:(et + 1) * 128],
                                vgw[:, fc * 512:(fc + 1) * 512], start=True, stop=True)
                            t_up = bb.op("tensor", fn,
                                         waits=[(ve, gv), lw] if (et == 0 and fc == 0) else (),
                                         mark=(fc == 1))
                        upp.consume(bb, i_up, t_up, [
                            lambda e, et=et, ps_up=ps_up: e.scalar_tensor_tensor(
                                WT[et][:, :], WT[et][:, :], gccol[:, 0:1],
                                ps_up[:, :E], ALU.mult, ALU.add)])

        with prog.scope():
            with prog.block() as bb:
                wrT, _ = load_w_bf16(bb, prog, w["W_readT"], name="wrT")
            rT = [prog.sbuf(f"rT{i}", [128, 128], BF16) for i in range(8)]
            rtk = [prog.sbuf(f"rtk{i}", [128, E], BF16) for i in range(2)]
            xo = [prog.sbuf(f"xo{i}", [128, E], F32) for i in range(2)]
            zo = [prog.sbuf(f"zo{i}", [128, E], F32) for i in range(2)]
            yt = [prog.sbuf(f"yt{i}", [128, 512], F32) for i in range(4)]
            with prog.block() as bb:
                fp = PP(prog, "fp", [128, 512], 2, "vector")
                tpr = PP(prog, "tpr", [128, 128], 2, "vector", dtype=BF16)
                cnt = [0]
                sth = {}
                for tb in range(NTB):
                    buf = tb % 2
                    wv = [(ve, ld_hist.get(tb - 2, 0))] if tb >= 2 else []
                    lr = bb.dma("sync", rtk[buf][:, :], reads_hbm[tb * 128:(tb + 1) * 128, :],
                                waits=wv)
                    lx = bb.dma("sync", xo[buf][:, :], x_hbm[tb * 128:(tb + 1) * 128, :])
                    lz = bb.dma("sync", zo[buf][:, :], xn_f32_hbm[tb * 128:(tb + 1) * 128, :])
                    for et in range(8):
                        ps, idx = tpr.acquire(bb)
                        tt = tpr.produce(bb, lambda e, et=et, buf=buf, ps=ps: e.transpose(
                            ps[:, :], rtk[buf][:, et * 128:(et + 1) * 128], ident[:, :]),
                            waits=[lr] if et == 0 else ())
                        tpr.consume(bb, idx, tt, [
                            lambda e, et=et, ps=ps: e.tensor_copy(rT[et][:, :], ps[:, :])])
                    trdone = prog.chain["vector"].v
                    for fc in range(2):
                        ps, idx = fp.acquire(bb)
                        for et in range(8):
                            fn = lambda e, et=et, fc=fc, ps=ps: e.matmul(
                                ps[:, :], rT[et][:, :], wrT[et][:, fc * 512:(fc + 1) * 512],
                                start=(et == 0), stop=(et == 7))
                            if et == 7:
                                tg = fp.produce(bb, fn)
                            else:
                                bb.op("tensor", fn,
                                      waits=[(ve, trdone)] if et == 0 else (), mark=False)
                        k = cnt[0] % 4
                        wv2 = [sth[cnt[0] - 4]] if cnt[0] >= 4 else []
                        wv2 += wmax([lx, lz])
                        last = fp.consume(bb, idx, tg, [
                            lambda e, ps=ps, k=k, fc=fc, buf=buf: e.scalar_tensor_tensor(
                                yt[k][:, :], ps[:, :], alcol[:, 0:1],
                                zo[buf][:, fc * 512:(fc + 1) * 512], ALU.mult, ALU.add),
                            lambda e, k=k, fc=fc, buf=buf: e.tensor_tensor(
                                yt[k][:, :], yt[k][:, :], xo[buf][:, fc * 512:(fc + 1) * 512],
                                ALU.add)], waits=wv2)
                        ld_hist[tb] = last
                        sth[cnt[0]] = bb.dma("sync",
                                             xout_hbm[tb * 128:(tb + 1) * 128,
                                                      fc * 512:(fc + 1) * 512],
                                             yt[k][:, :], waits=[(ve, last)])
                        cnt[0] += 1


def sub_l3_attn_a(prog: Prog, x_hbm, xout_hbm, w, uid):
    DI = 2048
    DS, DTR = 16, 64
    csts = prog.consts_tiles
    ident = csts["cst_ident"]
    xn_stage = prog.dram(f"stg_xn_{uid}", [E, T], BF16)
    pre_stage = prog.dram(f"stg_pre_{uid}", [T, E], F32)
    z_stage = prog.dram(f"stg_z_{uid}", [DI, T], BF16)
    y_stage = prog.dram(f"stg_y_{uid}", [DI, T], BF16)
    bc_stage = prog.dram(f"stg_bc_{uid}", [2, DS, 128, T], BF16)
    bc_rows = [prog.dram(f"stg_bcr{i}_{uid}", [16, T], F32) for i in range(2)]
    act, ve, pe, pl = (prog.chain["scalar"], prog.chain["vector"],
                       prog.chain["tensor"], prog.chain["gpsimd"])

    with prog.scope():
        with prog.scope():
            xnT = [prog.sbuf(f"xnT{i}", [128, T], BF16) for i in range(8)]
            with prog.block() as bb:
                info = ln_to_xnT(bb, prog, x_hbm, xnT, ident,
                                 second_norm=("rms_eps_inside", 1e-5),
                                 pre_f32_hbm=pre_stage)
                for i in range(8):
                    bb.dma("sync", xn_stage[i * 128:(i + 1) * 128, :], xnT[i][:, :],
                           waits=[(ve, info["ve_done"]), (act, info["act_done"])] if i == 0 else ())

        xi_stage = prog.dram(f"stg_xi_{uid}", [DI, T], BF16)
        xc_stage = prog.dram(f"stg_xc_{uid}", [DI, T], BF16)
        with prog.scope():
          with prog.scope():
            with prog.block() as bb:
                win, win_t = load_w_bf16(bb, prog, w["W_in"], name="win")
                gcol = prog.sbuf("gcol", [128, 8], F32)
                bcolb = prog.sbuf("bcolb", [128, 8], BF16)
                lg1 = bb.dma("sync", gcol[:, :], w["lng_col"][:, :])
                lg2 = bb.dma("gpsimd", bcolb[:, :], w["lnb_col"][:, :])
                bias_xz = prog.sbuf("bias_xz", [128, 32], F32)
                bxp = PP(prog, "bxp", [128, 32], 2, "vector")
                for fb in range(32):
                    ps, idx = bxp.acquire(bb)
                    for k in range(8):
                        fn = lambda e, k=k, fb=fb, ps=ps: e.matmul(
                            ps[:, 0:1], win[k][:, fb * 128:(fb + 1) * 128],
                            bcolb[:, k:k + 1], start=(k == 0), stop=(k == 7))
                        if k == 7:
                            tgt = bxp.produce(bb, fn)
                        else:
                            bb.op("tensor", fn,
                                  waits=(win_t + wmax([lg1, lg2])) if (fb == 0 and k == 0) else (),
                                  mark=False)
                    bxp.consume(bb, idx, tgt, [
                        lambda e, fb=fb, ps=ps: e.tensor_copy(bias_xz[:, fb:fb + 1], ps[:, 0:1])])
                for k in range(8):
                    bb.op("vector", lambda e, k=k: e.tensor_scalar_mul(
                        win[k][:, :], win[k][:, :], gcol[:, k:k + 1]),
                        waits=[(pe, prog.chain["tensor"].v)] if k == 0 else ())
            with prog.block() as bb:
                pxz = PP(prog, "pxz", [128, 512], 2, "vector")
                zt = [prog.sbuf(f"zt{i}", [128, 512], BF16) for i in range(4)]
                zcnt = [0]
                zsth = {}

                def xz_writer(bb, mb, c0, c1, ps, idx, tgt):
                    stage = xi_stage if mb < 16 else z_stage
                    mb2 = mb if mb < 16 else mb - 16
                    k = zcnt[0] % 4
                    wv = [zsth[zcnt[0] - 4]] if zcnt[0] >= 4 else []
                    last = pxz.consume(bb, idx, tgt, [
                        lambda e, mb=mb, c0=c0, c1=c1, ps=ps, k=k: e.tensor_scalar(
                            zt[k][:, :c1 - c0], ps[:, :c1 - c0],
                            bias_xz[:, mb:mb + 1], None, ALU.add)], waits=wv)
                    zsth[zcnt[0]] = bb.dma("sync",
                                           stage[mb2 * 128:(mb2 + 1) * 128, c0:c1],
                                           zt[k][:, :c1 - c0], waits=[(ve, last)])
                    zcnt[0] += 1

                mm_feature_major_streamed(bb, prog, win, xn_stage, E, T, pxz,
                                          xz_writer, m_blocks=32)

          with prog.scope():
            with prog.block() as bb:
                ccol = prog.sbuf("ccol", [128, 64], F32)
                cbcol = prog.sbuf("cbcol", [128, 16], F32)
                lc1 = bb.dma("sync", ccol[:, :], w["convw_col"][:, :])
                lc2 = bb.dma("sync", cbcol[:, :], w["convb_col"][:, :])
                acc = prog.sbuf("cacc", [128, T], F32)
                xib = [prog.sbuf(f"xib{i}", [128, 3 + T], BF16) for i in range(2)]
                xcb = [prog.sbuf(f"xcb{i}", [128, T], BF16) for i in range(2)]
                for i in range(2):
                    bb.op("vector", lambda e, i=i: e.memset(xib[i][:, 0:3], 0.0))
                sil_prev = 0
                sth = {}
                for ct in range(16):
                    bi = ct % 2
                    wv = [(ve, sth[ct - 2][0])] if ct >= 2 else []
                    ld = bb.dma("sync", xib[bi][:, 3:3 + T],
                                xi_stage[ct * 128:(ct + 1) * 128, :], waits=wv)
                    wv0 = (wmax([lc1, lc2]) if ct == 0 else [(act, sil_prev)]) + [ld]
                    v0 = bb.op("vector", lambda e, ct=ct, bi=bi: e.tensor_scalar_mul(
                        acc[:, :], xib[bi][:, 0:T], ccol[:, ct * 4:ct * 4 + 1]),
                        waits=wv0)
                    lastv = v0
                    for k in range(1, 4):
                        lastv = bb.op("vector", lambda e, ct=ct, k=k, bi=bi: e.scalar_tensor_tensor(
                            acc[:, :], xib[bi][:, k:k + T], ccol[:, ct * 4 + k:ct * 4 + k + 1],
                            acc[:, :], ALU.mult, ALU.add))
                    wv2 = [(prog.qrr["sync"][0], 0)]
                    if ct >= 2:
                        wv2 = [sth[ct - 2][1]]
                    sil_prev = bb.op("scalar", lambda e, ct=ct, bi=bi: e.activation(
                        xcb[bi][:, :], acc[:, :], AF.Silu, bias=cbcol[:, ct:ct + 1]),
                        waits=[(ve, lastv)] + (wv2 if ct >= 2 else []))
                    stt = bb.dma("sync", xc_stage[ct * 128:(ct + 1) * 128, :], xcb[bi][:, :],
                                 waits=[(act, sil_prev)])
                    sth[ct] = (lastv, stt)

        dtT = prog.sbuf("dtT", [64, T], BF16)
        with prog.scope():
            with prog.block() as bb:
                wdt_, _ = load_w_bf16(bb, prog, w["Wx_dt"], name="wxdt")
                wb_, _ = load_w_bf16(bb, prog, w["Wx_B"], name="wxb")
                wc_, _ = load_w_bf16(bb, prog, w["Wx_C"], name="wxc")
            with prog.block() as bb:
                pd = PP(prog, "pd", [64, 512], 2, "vector")
                pbc = PP(prog, "pbc", [16, 512], 2, "vector")
                bcrow = [prog.sbuf(f"bcrow{i}", [16, T], F32) for i in range(2)]
                xcc = [[prog.sbuf(f"xcc{b}_{i}", [128, 512], BF16) for i in range(16)]
                       for b in range(2)]
                ldh = {}
                for ci, c0 in enumerate(range(0, T, 512)):
                    b = ci % 2
                    wv = [(pe, ldh[ci - 2])] if ci >= 2 else []
                    lds = []
                    for k in range(16):
                        lds.append(bb.dma("sync", xcc[b][k][:, :],
                                          xc_stage[k * 128:(k + 1) * 128, c0:c0 + 512],
                                          waits=wv if k == 0 else ()))
                    ps, idx = pd.acquire(bb)
                    for k in range(16):
                        fn = lambda e, k=k, c0=c0, ps=ps, b=b: e.matmul(
                            ps[:, :], wdt_[k][:, :], xcc[b][k][:, :],
                            start=(k == 0), stop=(k == 15))
                        if k == 15:
                            tgt = pd.produce(bb, fn)
                        else:
                            bb.op("tensor", fn, waits=wmax(lds) if k == 0 else (), mark=False)
                    pd.consume(bb, idx, tgt, [
                        lambda e, c0=c0, ps=ps: e.tensor_copy(dtT[:, c0:c0 + 512], ps[:, :])])
                    for which, wt_ in enumerate((wb_, wc_)):
                        ps2, idx2 = pbc.acquire(bb)
                        for k in range(16):
                            fn = lambda e, k=k, ps2=ps2, wt_=wt_, b=b: e.matmul(
                                ps2[0:16, :], wt_[k][:, :], xcc[b][k][:, :],
                                start=(k == 0), stop=(k == 15))
                            if k == 15:
                                tgt2 = pbc.produce(bb, fn)
                            else:
                                bb.op("tensor", fn, mark=False)
                        pbc.consume(bb, idx2, tgt2, [
                            lambda e, which=which, c0=c0, ps2=ps2: e.tensor_copy(
                                bcrow[which][:, c0:c0 + 512], ps2[0:16, :])])
                    ldh[ci] = prog.chain["tensor"].v
                for which in range(2):
                    bb.dma("sync", bc_rows[which][:, :], bcrow[which][:, :],
                           waits=[(ve, prog.chain["vector"].v)] if which == 0 else ())
            with prog.block() as bb:
                brow = prog.sbuf("brow", [1, T], F32)
                brep = prog.sbuf("brep", [128, T], BF16)
                pb2 = PP(prog, "pb2", [128, 512], 2, "vector")
                st = None
                for which in range(2):
                    for n in range(DS):
                        wv = [(pe, prog.chain["tensor"].v)]
                        ldb = bb.dma("sync", brow[:, :], bc_rows[which][n:n + 1, :], waits=wv)
                        bb.wait("tensor", *ldb)
                        if st is not None:
                            bb.wait("vector", *st)
                        for c0 in range(0, T, 512):
                            ps, idx = pb2.acquire(bb)
                            tgt = pb2.produce(bb, lambda e, c0=c0, ps=ps: e.matmul(
                                ps[:, :], csts["cst_onesrow"][:, :], brow[:, c0:c0 + 512],
                                start=True, stop=True))
                            pb2.consume(bb, idx, tgt, [
                                lambda e, c0=c0, ps=ps: e.tensor_copy(
                                    brep[:, c0:c0 + 512], ps[:, :])])
                        st = bb.dma("sync", bc_stage[which, n, :, :], brep[:, :],
                                    waits=[(ve, prog.chain["vector"].v)])

        with prog.scope():
            with prog.block() as bb:
                wdt2, _ = load_w_bf16(bb, prog, w["W_dt"], name="wdt2")
                bdtc = prog.sbuf("bdtc", [128, 16], F32)
                bb.dma("sync", bdtc[:, :], w["bdt_col"][:, :])
                alog = [prog.sbuf(f"alog{i}", [128, 16], F32) for i in range(16)]
                for i in range(16):
                    bb.dma("sync", alog[i][:, :],
                           w["A_log"][i * 128:(i + 1) * 128, :])
                dcol = prog.sbuf("dcol", [128, 16], F32)
                bb.dma("sync", dcol[:, :], w["D_col"][:, :])
            with prog.block() as bb:
                for i in range(16):
                    a1 = bb.op("scalar", lambda e, i=i: e.activation(
                        alog[i][:, :], alog[i][:, :], AF.Exp))
                    bb.op("vector", lambda e, i=i: e.tensor_scalar_mul(
                        alog[i][:, :], alog[i][:, :], -1.0), waits=[(act, a1)])
            xct = prog.sbuf("xct", [128, T], BF16)
            deltaT = prog.sbuf("deltaT", [128, T], F32)
            scrD = prog.sbuf("scrD", [128, T], F32)
            dA = prog.sbuf("dA", [128, T], F32)
            du = prog.sbuf("du", [128, T], F32)
            dBu = prog.sbuf("dBu", [128, T], F32)
            ss = prog.sbuf("ss", [128, T], F32)
            yac = prog.sbuf("yac", [128, T], F32)
            bb_b = [prog.sbuf(f"bbb{i}", [128, T], BF16) for i in range(2)]
            cc_b = [prog.sbuf(f"ccb{i}", [128, T], BF16) for i in range(2)]
            zl = prog.sbuf("zl", [128, T], BF16)
            yout = prog.sbuf("yout", [128, T], BF16)
            pdl = PP(prog, "pdl", [128, 512], 2, "scalar")
            for dt in range(16):
                with prog.block() as bb:
                    ldxc = bb.dma("sync", xct[:, :], xc_stage[dt * 128:(dt + 1) * 128, :])
                    for c0 in range(0, T, 512):
                        ps, idx = pdl.acquire(bb)
                        tgt = pdl.produce(bb, lambda e, dt=dt, c0=c0, ps=ps: e.matmul(
                            ps[:, :], wdt2[0][0:64, dt * 128:(dt + 1) * 128],
                            dtT[:, c0:c0 + 512], start=True, stop=True))
                        pdl.consume(bb, idx, tgt, [
                            lambda e, c0=c0, ps=ps, dt=dt: e.activation(
                                scrD[:, c0:c0 + 512], ps[:, :], AF.Abs,
                                bias=bdtc[:, dt:dt + 1]),
                            lambda e, c0=c0: e.activation(
                                scrD[:, c0:c0 + 512], scrD[:, c0:c0 + 512], AF.Exp, scale=-1.0),
                            lambda e, c0=c0, _b=prog.constf(1.0): e.activation(
                                scrD[:, c0:c0 + 512], scrD[:, c0:c0 + 512], AF.Ln, bias=_b),
                            lambda e, c0=c0, ps=ps, dt=dt: e.activation(
                                deltaT[:, c0:c0 + 512], ps[:, :], AF.Relu,
                                bias=bdtc[:, dt:dt + 1]),
                        ])
                    dv1 = bb.op("vector", lambda e: e.tensor_tensor(
                        deltaT[:, :], deltaT[:, :], scrD[:, :], ALU.add),
                        waits=[(act, prog.chain["scalar"].v)])
                    dv2 = bb.op("vector", lambda e: e.tensor_tensor(
                        du[:, :], deltaT[:, :], xct[:, :], ALU.mult), waits=[ldxc])
                    ldz = bb.dma("sync", zl[:, :], z_stage[dt * 128:(dt + 1) * 128, :])
                    yz = bb.op("vector", lambda e: e.memset(yac[:, :], 0.0))
                    ld_bc = {}
                    s1_prev = 0
                    g2_prev = 0
                    for n in range(DS):
                        bi = n % 2
                        wv = [(pl, ld_bc[n - 2][0])] if n >= 2 else []
                        l1 = bb.dma("sync", bb_b[bi][:, :], bc_stage[0, n, :, :], waits=wv)
                        wv2 = [(ve, ld_bc[n - 2][1])] if n >= 2 else []
                        l2 = bb.dma("sync", cc_b[bi][:, :], bc_stage[1, n, :, :], waits=wv2)
                        a_dA = bb.op("scalar", lambda e, dt=dt, n=n: e.activation(
                            dA[:, :], deltaT[:, :], AF.Exp, scale=alog[dt][:, n:n + 1]),
                            waits=[(ve, dv1)] if n == 0 else [(ve, s1_prev)])
                        g1 = bb.op("gpsimd", lambda e, bi=bi: e.tensor_tensor(
                            dBu[:, :], du[:, :], bb_b[bi][:, :], ALU.mult),
                            waits=[l1, (ve, dv2)] + ([(ve, s1_prev)] if n > 0 else []))
                        s1 = bb.op("vector", lambda e: e.tensor_tensor_scan(
                            ss[:, :], dA[:, :], dBu[:, :], 0.0, ALU.mult, ALU.add),
                            waits=[(act, a_dA), (pl, g1)] + ([(pl, g2_prev)] if n > 0 else []))
                        s1_prev = s1
                        s2 = bb.op("vector", lambda e, bi=bi: e.tensor_tensor(
                            ss[:, :], ss[:, :], cc_b[bi][:, :], ALU.mult), waits=[l2])
                        g2 = bb.op("gpsimd", lambda e: e.tensor_tensor(
                            yac[:, :], yac[:, :], ss[:, :], ALU.add),
                            waits=[(ve, s2)] + ([(ve, yz)] if n == 0 else []))
                        g2_prev = g2
                        ld_bc[n] = (g2, s2)
                    f1 = bb.op("vector", lambda e, dt=dt: e.scalar_tensor_tensor(
                        yac[:, :], xct[:, :], dcol[:, dt:dt + 1], yac[:, :],
                        ALU.mult, ALU.add), waits=[(pl, prog.chain["gpsimd"].v)])
                    f2 = bb.op("scalar", lambda e: e.activation(zl[:, :], zl[:, :], AF.Silu),
                               waits=[ldz])
                    f3 = bb.op("vector", lambda e: e.tensor_tensor(
                        yout[:, :], yac[:, :], zl[:, :], ALU.mult), waits=[(act, f2)])
                    bb.dma("sync", y_stage[dt * 128:(dt + 1) * 128, :], yout[:, :],
                           waits=[(ve, f3)])

        with prog.scope():
            with prog.block() as bb:
                wout, _ = load_w_bf16(bb, prog, w["W_out"], name="wout")
            out_proj_residual(prog, act_stage=y_stage, n_act=DI, wo_tiles=wout,
                              x_hbm=x_hbm, xout_hbm=xout_hbm,
                              extra_add_hbm=pre_stage)

import math as _math
from concourse.bass_utils import run_bass_kernel_spmd

N_GROUPS = 1

_SUBS = [
    ("l0a", "L0aa"), ("l0f1", "L0ff"), ("l0b", "L0ab"), ("l0f2", "L0ff"),
    ("l1a", "L1aa"), ("l1f1", "L1ff"), ("l1b", "L1ab"), ("l1f2", "L1ff"),
    ("l2a", "L2aa"), ("l2f1", "L2ff"), ("l2b", "L2ab"), ("l2f2", "L2ff"),
    ("l3a", "L3aa"), ("l3f1", "L3ff"), ("l3b", "L3ab"), ("l3f2", "L3ff"),
]


def _bucket_np(d):
    NB, MD = 32, 128
    nb = NB // 2
    rel = (d > 0).astype(np.int64) * nb
    rp = np.abs(d)
    mx = nb // 2
    small = rp < mx
    rps = np.maximum(rp, 1)
    large = mx + (np.log(rps / mx) / _math.log(MD / mx) * (nb - mx)).astype(np.int64)
    large = np.minimum(large, nb - 1)
    return rel + np.where(small, rp, large)


def _col(v, nb):
    return np.ascontiguousarray(np.asarray(v, np.float32).reshape(nb, 128).T)


def _row(v):
    return np.asarray(v, np.float32).reshape(1, -1)


def _tp(a):
    return np.ascontiguousarray(np.asarray(a).T)


def _host_weights(I):
    E_ = 1024
    W = {}
    H = 16
    start = 2.0 ** (-2.0 ** (-(_math.log2(H) - 3)))
    slopes = np.array([start * start ** i for i in range(H)], np.float32)
    W["L0aa"] = dict(WqT=_tp(I["L0_attn_a_Wq"]), WkT=_tp(I["L0_attn_a_Wk"]),
                     WvT=_tp(I["L0_attn_a_Wv"]), WpT=_tp(I["L0_attn_a_Wp"]),
                     BqT=_tp(I["L0_attn_a_Bq"]), BkT=_tp(I["L0_attn_a_Bk"]),
                     BvT=_tp(I["L0_attn_a_Bv"]),
                     AqT=_tp(I["L0_attn_a_Aq"]), AkT=_tp(I["L0_attn_a_Ak"]),
                     AvT=_tp(I["L0_attn_a_Av"]), slopes=slopes[None, :])
    d = dict(gate_wT=_tp(I["L0_ffn_gate_w"]), gate_bias=_row(I["L0_ffn_gate_bias"]),
             sw1T=_tp(I["L0_ffn_sw1"]), sw3T=_tp(I["L0_ffn_sw3"]),
             sw2T=_tp(I["L0_ffn_sw2"]))
    for e in range(8):
        d[f"w1T_{e}"] = _tp(I["L0_ffn_w1"][e])
        d[f"w3T_{e}"] = _tp(I["L0_ffn_w3"][e])
        d[f"w2T_{e}"] = _tp(I["L0_ffn_w2"][e])
    W["L0ff"] = d
    W["L0ab"] = dict(Mq=np.ascontiguousarray(I["L0_attn_b_Mq"]),
                     Mk=np.ascontiguousarray(I["L0_attn_b_Mk"]),
                     Mv=np.ascontiguousarray(I["L0_attn_b_Mv"]))
    p_ = np.arange(64)
    W["L1aa"] = dict(W_writeT=_tp(I["L1_attn_a_W_write"]),
                     W_readT=_tp(I["L1_attn_a_W_read"]),
                     decay=np.asarray(I["L1_attn_a_decay"], np.float32).reshape(1, 1),
                     log_alpha=np.asarray(I["L1_attn_a_log_alpha"], np.float32).reshape(1, 1),
                     diffT=np.maximum(p_[None, :] - 1 - p_[:, None], 0).astype(np.float32),
                     causT=(p_[None, :] > p_[:, None]).astype(np.float32),
                     rev64=(63.0 - p_).astype(np.float32)[:, None])
    d = dict(gw1T=_tp(I["L1_ffn_gw1"]), gb1=_row(I["L1_ffn_gb1"]),
             gw2T=_tp(I["L1_ffn_gw2"]), gb2=_row(I["L1_ffn_gb2"]))
    e8 = np.zeros((8, 128), np.float32)
    for e in range(8):
        e8[e, e * 16:(e + 1) * 16] = 1
    d["exp8"] = e8
    for pfx in "qkvo":
        d[pfx + "WmT"] = _tp(I[f"L1_ffn_{pfx}Wm"])
        d[pfx + "VallT"] = np.ascontiguousarray(
            I[f"L1_ffn_{pfx}V"].transpose(2, 0, 1).reshape(E_, 128))
        d[pfx + "S"] = np.ascontiguousarray(
            np.asarray(I[f"L1_ffn_{pfx}S"], np.float32).reshape(128, 1))
        d[pfx + "UrT"] = np.ascontiguousarray(
            I[f"L1_ffn_{pfx}U"].transpose(0, 2, 1).reshape(128, E_))
    d["qb"] = _col(I["L1_ffn_qb"], 8)
    d["kb"] = _col(I["L1_ffn_kb"], 8)
    d["vbrow"] = _row(I["L1_ffn_vb"])
    d["obrow"] = _row(I["L1_ffn_ob"])
    W["L1ff"] = d
    qkvT = _tp(I["L1_attn_b_w_qkv"])
    dd = np.clip(4222 - np.arange(8448), -4095, 4095)
    bk = _bucket_np(dd)
    oht = np.zeros((32, 8448), np.float32)
    oht[bk, np.arange(8448)] = 1.0
    W["L1ab"] = dict(w_qT=np.ascontiguousarray(qkvT[:, :E_]),
                     w_kT=np.ascontiguousarray(qkvT[:, E_:2 * E_]),
                     w_vT=np.ascontiguousarray(qkvT[:, 2 * E_:]),
                     w_projT=_tp(I["L1_attn_b_w_proj"]),
                     b_proj=_row(I["L1_attn_b_b_proj"]),
                     rel_table=np.ascontiguousarray(I["L1_attn_b_rel_bias_table"]),
                     OHT=oht)
    W["L2aa"] = dict(W_q=np.ascontiguousarray(I["L2_attn_a_W_q"]),
                     W_k=np.ascontiguousarray(I["L2_attn_a_W_k"]),
                     W_v=np.ascontiguousarray(I["L2_attn_a_W_v"]),
                     W_o=np.ascontiguousarray(I["L2_attn_a_W_o"]),
                     bq_row=_row(I["L2_attn_a_b_q"]), bk_row=_row(I["L2_attn_a_b_k"]),
                     bo_row=_row(I["L2_attn_a_b_o"]), bv_col=_col(I["L2_attn_a_b_v"], 8))
    W["L2ff"] = dict(w_upT=_tp(I["L2_ffn_w_up"]), w_downT=_tp(I["L2_ffn_w_down"]))
    qw = I["L2_attn_b_qkv_w"]
    qb = I["L2_attn_b_qkv_b"]
    W["L2ab"] = dict(wq=np.ascontiguousarray(qw[:, :E_]),
                     wk=np.ascontiguousarray(qw[:, E_:2 * E_]),
                     wv=np.ascontiguousarray(qw[:, 2 * E_:]),
                     bq_col=_col(qb[:E_], 8), bk_col=_col(qb[E_:2 * E_], 8),
                     bv_row=_row(qb[2 * E_:]),
                     rq_w=np.ascontiguousarray(I["L2_attn_b_rq_w"]),
                     rq_b_col=_col(I["L2_attn_b_rq_b"], 2),
                     rkv_w=np.ascontiguousarray(I["L2_attn_b_rkv_w"]),
                     rkv_b_col=_col(I["L2_attn_b_rkv_b"], 4),
                     rout_w=np.ascontiguousarray(I["L2_attn_b_rout_w"]),
                     rout_b_col=_col(I["L2_attn_b_rout_b"], 8),
                     out_w=np.ascontiguousarray(I["L2_attn_b_out_w"]),
                     out_b_row=_row(I["L2_attn_b_out_b"]),
                     gl=np.asarray(I["L2_attn_b_gl_w"], np.float32),
                     gr=np.asarray(I["L2_attn_b_gr_w"], np.float32))
    cw = np.zeros((128, 64), np.float32)
    conv_w = I["L3_attn_a_conv_w"]
    for ct in range(16):
        for k in range(4):
            cw[:, ct * 4 + k] = conv_w[ct * 128:(ct + 1) * 128, 0, k]
    xp = I["L3_attn_a_W_xproj"]
    W["L3aa"] = dict(W_in=np.ascontiguousarray(I["L3_attn_a_W_in"]),
                     lng_col=_col(I["L3_attn_a_ln_g"], 8),
                     lnb_col=_col(I["L3_attn_a_ln_b"], 8),
                     convw_col=cw, convb_col=_col(I["L3_attn_a_conv_b"], 16),
                     Wx_dt=np.ascontiguousarray(xp[:, :64]),
                     Wx_B=np.ascontiguousarray(xp[:, 64:80]),
                     Wx_C=np.ascontiguousarray(xp[:, 80:96]),
                     W_dt=np.ascontiguousarray(I["L3_attn_a_W_dt"]),
                     bdt_col=_col(I["L3_attn_a_b_dt"], 16),
                     A_log=np.ascontiguousarray(I["L3_attn_a_A_log"]),
                     D_col=_col(I["L3_attn_a_D_param"], 16),
                     W_out=np.ascontiguousarray(I["L3_attn_a_W_out"]))
    d = dict(router_w=np.ascontiguousarray(I["L3_ffn_router_w"]))
    rmsc = np.zeros((128, 16 * 8), np.float32)
    for e in range(14):
        rmsc[:, e * 8:(e + 1) * 8] = I["L3_ffn_rms_w"][e].reshape(8, 128).T
    for s_ in range(2):
        rmsc[:, (14 + s_) * 8:(15 + s_) * 8] = I["L3_ffn_rms_w_s"][s_].reshape(8, 128).T
    d["rms_cols"] = rmsc
    for e in range(14):
        d[f"w1_{e}"] = np.ascontiguousarray(I["L3_ffn_w1"][e])
        d[f"w2_{e}"] = np.ascontiguousarray(I["L3_ffn_w2"][e])
    for s_ in range(2):
        d[f"w1s_{s_}"] = np.ascontiguousarray(I["L3_ffn_w1_s"][s_])
        d[f"w2s_{s_}"] = np.ascontiguousarray(I["L3_ffn_w2_s"][s_])
    W["L3ff"] = d
    W["L3ab"] = dict(Wq=np.ascontiguousarray(I["L3_attn_b_Wq"]),
                     Wl=np.ascontiguousarray(I["L3_attn_b_Wl"]),
                     Wk=np.ascontiguousarray(I["L3_attn_b_Wk"]),
                     Wv=np.ascontiguousarray(I["L3_attn_b_Wv"]),
                     Wo=np.ascontiguousarray(I["L3_attn_b_Wo"]),
                     bq=_col(I["L3_attn_b_bq"], 8), bl=_col(I["L3_attn_b_bl"], 2),
                     bk=_col(I["L3_attn_b_bk"], 8),
                     bv=_row(I["L3_attn_b_bv"]), bo=_row(I["L3_attn_b_bo"]))
    return W


_SUB_FN = {"L0aa": sub_l0_attn_a, "L0ff": sub_l0_ffn, "L0ab": sub_l0_attn_b,
           "L1aa": sub_l1_attn_a, "L1ff": sub_l1_ffn, "L1ab": sub_l1_attn_b,
           "L2aa": sub_l2_attn_a, "L2ff": sub_l2_ffn, "L2ab": sub_l2_attn_b,
           "L3aa": sub_l3_attn_a, "L3ff": sub_l3_ffn, "L3ab": sub_l3_attn_b}


def _build_group(sub_slice, warrs_by_kind):
    p = Prog()
    x_in = p.dram("x_in", [T, E], F32, kind="ExternalInput")
    x_out = p.dram("x_out", [T, E], F32, kind="ExternalOutput")
    xa = p.dram("x_mid_a", [T, E], F32)
    xb = p.dram("x_mid_b", [T, E], F32)
    with p.block() as bb:
        load_consts(bb, p)
    n = len(sub_slice)
    kind_handles = {}
    for i, (uid, kind) in enumerate(sub_slice):
        src = x_in if i == 0 else (xa if i % 2 == 1 else xb)
        dst = x_out if i == n - 1 else (xa if i % 2 == 0 else xb)
        if kind not in kind_handles:
            kind_handles[kind] = {
                k: p.dram(f"{kind}_{k}", list(arr.shape), F32, kind="ExternalInput")
                for k, arr in warrs_by_kind[kind].items()}
        _SUB_FN[kind](p, src, dst, kind_handles[kind], uid)
    with p.block() as bb:
        pass
    p.emit_all()
    return p


_PROG_CACHE = {}


def kernel(**inputs):
    I = {k: np.asarray(v) for k, v in inputs.items()}
    x_full = np.asarray(I["L0_attn_a_x"], np.float32)
    B = x_full.shape[0]
    Wk_ = _host_weights(I)
    csts = host_consts()
    groups = []
    gsz = len(_SUBS) // N_GROUPS
    for g in range(N_GROUPS):
        groups.append(_SUBS[g * gsz:(g + 1) * gsz])

    cur = [np.ascontiguousarray(x_full[b]) for b in range(B)]
    for gi, grp in enumerate(groups):
        key = ("g", gi, N_GROUPS)
        if key not in _PROG_CACHE:
            _PROG_CACHE[key] = _build_group(grp, Wk_)
        p = _PROG_CACHE[key]
        in_maps = []
        for b in range(B):
            m = dict(x_in=cur[b])
            m.update(csts)
            for uid, kind in set(grp):
                for k, arr in Wk_[kind].items():
                    m[f"{kind}_{k}"] = arr
            in_maps.append(m)
        res = run_bass_kernel_spmd(p.nc, in_maps, core_ids=list(range(B)))
        cur = [np.asarray(res.results[b]["x_out"]) for b in range(B)]
    return np.stack(cur, axis=0)


# version 03
# speedup vs baseline: 1.1816x; 1.1816x over previous
from contextlib import ExitStack, contextmanager

import numpy as np
import concourse.bass as bass
import concourse.mybir as mybir

F32 = mybir.dt.float32
BF16 = mybir.dt.bfloat16
AF = mybir.ActivationFunctionType
ALU = mybir.AluOpType
AX = mybir.AxisListType

ENGINES = ("sync", "tensor", "vector", "scalar", "gpsimd")
CHAINED = ("vector", "scalar", "gpsimd")

T = 4096
E = 1024
NTB = T // 128


class SemCtr:
    def __init__(self, h):
        self.h = h
        self.v = 0


class Prog:
    def __init__(self):
        self.nc = bass.Bass("TRN2", target_bir_lowering=False)
        self.es = ExitStack()
        self._scopes = [self.es]
        self.sems = {}
        self._uid = 0
        self.bbs = []
        self._consts = {}
        self._const_inits = []
        self._carena = self.es.enter_context(self.nc.sbuf_tensor("cst_arena", [128, 64], F32))
        self.chain = {e: self.sem(f"ch_{e}") for e in ("tensor", "vector", "scalar", "gpsimd")}
        self.qrr = {q: [self.sem(f"dq_{q}{i}") for i in range(4)]
                    for q in ("sync", "gpsimd", "scalar", "vector")}
        self.qrr_i = {q: 0 for q in self.qrr}

    def uid(self):
        self._uid += 1
        return self._uid

    def sem(self, name) -> SemCtr:
        if name not in self.sems:
            h = self.es.enter_context(self.nc.semaphore(name))
            self.sems[name] = SemCtr(h)
        return self.sems[name]

    def constf(self, value, dtype=F32):
        assert dtype == F32
        key = float(value)
        if key not in self._consts:
            i = len(self._consts)
            assert i < 64, "const arena full"
            self._consts[key] = i
            self._const_inits.append((i, float(value)))
        return self._carena[:, self._consts[key]:self._consts[key] + 1]

    def constf_row(self, value, rows=1):
        self.constf(value)
        i = self._consts[float(value)]
        return self._carena[0:rows, i:i + 1]

    def sbuf(self, name, shape, dtype):
        return self._scopes[-1].enter_context(
            self.nc.sbuf_tensor(f"{name}_u{self.uid()}", shape, dtype))

    def psum(self, name, shape, dtype=F32):
        return self._scopes[-1].enter_context(
            self.nc.psum_tensor(f"{name}_u{self.uid()}", shape, dtype))

    @contextmanager
    def scope(self):
        es = ExitStack()
        self._scopes.append(es)
        try:
            yield
        finally:
            self._scopes.pop()
            es.close()

    def dram(self, name, shape, dtype, kind="Internal"):
        return self.nc.dram_tensor(name, shape, dtype, kind=kind)

    def block(self):
        return BB(self)

    def emit_all(self):
        nc = self.nc
        with nc.Block() as blk:
            inits = list(self._const_inits)
            sems = [s.h for s in self.sems.values()]

            def clear(e, sems=sems):
                for h in sems:
                    e.sem_clear(h)
            blk.sync(clear)

            car = self._carena

            def run(e, inits=inits, car=car):
                for i, val in inits:
                    e.memset(car[:, i:i + 1], val)
            blk.vector(run)
        for bb in self.bbs:
            bb.emit()

    def finish(self):
        self.es.close()


class BB:
    def __init__(self, prog: Prog):
        self.prog = prog
        self.ops = {e: [] for e in ENGINES}
        self.used_queues = set()

    def __enter__(self):
        return self

    def __exit__(self, et, ev, tb):
        if et is None:
            self._quiesce = {q: [(c.h, c.v) for c in self.prog.qrr[q] if c.v > 0]
                             for q in self.used_queues}
            self.prog.bbs.append(self)
        return False

    def emit(self):
        nc = self.prog.nc
        for q in sorted(getattr(self, "_quiesce", {})):
            for h, v in self._quiesce[q]:
                self.ops[q].append(lambda e, h=h, v=v: e.wait_ge(h, v))
        with nc.Block() as blk:
            for eng in ENGINES:
                fns = self.ops[eng]
                if fns:
                    def run(e, fns=fns):
                        for f in fns:
                            f(e)
                    getattr(blk, eng)(run)

    def _waits(self, eng, waits):
        for ctr, tgt in waits:
            if tgt is None or tgt <= 0:
                continue
            self.ops[eng].append(lambda e, h=ctr.h, tgt=tgt: e.wait_ge(h, tgt))

    def op(self, eng, fn, waits=(), mark=True, chain=True):
        c = self.prog.chain[eng]
        self._waits(eng, waits)
        if eng in CHAINED:
            selfwait = c.v if chain else 0
            c.v += 1
            my = c.v

            def f(e, fn=fn, selfwait=selfwait, my=my, h=c.h):
                ins = fn(e)
                if selfwait > 0:
                    ins._wait_ge(h, selfwait)
                ins.then_inc(h, 1)
            self.ops[eng].append(f)
            return my
        else:
            if eng == "tensor" and mark:
                c.v += 1
                my = c.v

                def f(e, fn=fn, h=c.h):
                    fn(e).then_inc(h, 1)
                self.ops[eng].append(f)
                return my
            else:
                self.ops[eng].append(lambda e, fn=fn: fn(e))
                return c.v

    def dma(self, eng, out, in_, waits=(), slot=None, **kw):
        prog = self.prog
        if slot is None:
            slot = prog.qrr_i[eng] % 4
            prog.qrr_i[eng] += 1
        c = prog.qrr[eng][slot]
        self._waits(eng, waits)
        prev = c.v
        c.v += 16
        my = c.v
        self.used_queues.add(eng)

        def f(e, out=out, in_=in_, kw=kw, h=c.h, prev=prev):
            ins = e.dma_start(out, in_, **kw)
            if prev > 0:
                ins._wait_ge(h, prev)
            ins.then_inc(h, 16)
        self.ops[eng].append(f)
        return (c, my)

    def wait(self, eng, ctr: SemCtr, tgt):
        self._waits(eng, [(ctr, tgt)])

    def raw(self, eng, fn):
        self.ops[eng].append(fn)


class PsumPool:

    def __init__(self, prog: Prog, name, shape, n, eng, dtype=F32):
        self.prog = prog
        self.eng = eng
        self.tiles = [prog.psum(f"{name}{i}", shape, dtype) for i in range(n)]
        self.n = n
        self.rel = [0] * n
        self.k = 0

    def acquire(self, bb: BB):
        idx = self.k % self.n
        if self.rel[idx] > 0:
            bb.wait("tensor", self.prog.chain[self.eng], self.rel[idx])
        self.k += 1
        return self.tiles[idx], idx

    def produce(self, bb: BB, fn, waits=()):
        return bb.op("tensor", fn, waits=waits, mark=True)

    def consume(self, bb: BB, idx, pe_tgt, fns, waits=()):
        allw = [(self.prog.chain["tensor"], pe_tgt)] + list(waits)
        last = None
        for i, f in enumerate(fns):
            last = bb.op(self.eng, f, waits=allw if i == 0 else ())
        self.rel[idx] = last
        return last


def ceil_div(a, b):
    return -(-a // b)


def wmax(tuples):
    d = {}
    for ctr, tgt in tuples:
        if ctr in d:
            d[ctr] = max(d[ctr], tgt)
        else:
            d[ctr] = tgt
    return [(c, t) for c, t in d.items()]


def load_w_bf16(bb: BB, prog: Prog, hbm, name="w", queue="gpsimd", tiles=None, waits=()):
    K, M = hbm.shape
    kt = ceil_div(K, 128)
    if tiles is None:
        tiles = [prog.sbuf(f"{name}{i}", [128, M], BF16) for i in range(kt)]
    tgts = []
    for i in range(kt):
        p = min(128, K - i * 128)
        tgts.append(bb.dma(queue, tiles[i][:p, :], hbm[i * 128:i * 128 + p, :],
                           waits=waits if i == 0 else ()))
    return tiles, wmax(tgts)


def ln_to_xnT(bb: BB, prog: Prog, x_hbm, xnT, ident,
              eps=1e-5, pad_left=0,
              xn_f32_hbm=None, xn_bf16_hbm=None, tok_cb=None, second_norm=None,
              n_tb=NTB, x_row_base=0, rinv_out=None, pre_f32_hbm=None):
    u = prog.uid()
    ve = prog.chain["vector"]
    act = prog.chain["scalar"]
    pe = prog.chain["tensor"]
    xt = [prog.sbuf(f"lnx{u}_{i}", [128, E], F32) for i in range(2)]
    sq = prog.sbuf(f"lnsq{u}", [128, E], F32)
    xnf = [prog.sbuf(f"lnxn{u}_{i}", [128, E], F32) for i in range(2)]
    xnb = [prog.sbuf(f"lnxb{u}_{i}", [128, E], BF16) for i in range(2)]
    stat = prog.sbuf(f"lnstat{u}", [128, 8], F32)
    tpA = PsumPool(prog, f"lntpA{u}", [128, 128], 2, "scalar", BF16)
    tpB = PsumPool(prog, f"lntpB{u}", [128, 128], 2, "vector", BF16)

    info = {}
    hist = {}

    for tb in range(n_tb):
        buf = tb % 2
        w = []
        if tb >= 2:
            w.append((ve, hist[tb - 2]["ve"]))
            w.append((act, hist[tb - 2]["sa_sq"]))
        ldt = bb.dma("sync", xt[buf][:, :],
                     x_hbm[x_row_base + tb * 128: x_row_base + (tb + 1) * 128, :], waits=w)

        sa_sq = bb.op("scalar", lambda e, b=buf: e.activation(
            sq[:, :], xt[b][:, :], AF.Square, accum_out=stat[:, 1:2]),
            waits=[ldt])
        bb.op("vector", lambda e, b=buf: e.reduce_sum(stat[:, 0:1], xt[b][:, :], axis=AX.X),
              waits=[ldt])
        bb.op("vector", lambda e: e.tensor_scalar_mul(stat[:, 2:3], stat[:, 0:1], 1.0 / E))
        bb.op("vector", lambda e: e.tensor_scalar_mul(stat[:, 3:4], stat[:, 1:2], 1.0 / E),
              waits=[(act, sa_sq)])
        bb.op("vector", lambda e: e.tensor_tensor(stat[:, 6:7], stat[:, 2:3], stat[:, 2:3], ALU.mult))
        va1 = bb.op("vector", lambda e: e.tensor_tensor(stat[:, 3:4], stat[:, 3:4], stat[:, 6:7], ALU.subtract))
        bb.op("scalar", lambda e, _b=prog.constf(eps): e.activation(
            stat[:, 4:5], stat[:, 3:4], AF.Ln, bias=_b), waits=[(ve, va1)])
        sa2 = bb.op("scalar", lambda e: e.activation(stat[:, 4:5], stat[:, 4:5], AF.Exp, scale=-0.5))
        w = [(act, sa2)]
        if xn_f32_hbm is not None and tb >= 2:
            w.append(hist[tb - 2]["st_f32"])
        va2 = bb.op("vector", lambda e, b=buf: e.tensor_scalar(
            xnf[b][:, :], xt[b][:, :], stat[:, 2:3], stat[:, 4:5], ALU.subtract, ALU.mult),
            waits=w)

        if pre_f32_hbm is not None:
            hist_pre = hist.get(tb - 2, {}).get("st_pre")
            if hist_pre is not None:
                bb.wait("sync", *hist_pre)
            st_pre = bb.dma("sync", pre_f32_hbm[tb * 128:(tb + 1) * 128, :], xnf[buf][:, :],
                            waits=[(ve, va2)])
        else:
            st_pre = None

        if second_norm is not None:
            kind, eps2 = second_norm
            sa3 = bb.op("scalar", lambda e, b=buf: e.activation(
                sq[:, :], xnf[b][:, :], AF.Square, accum_out=stat[:, 5:6]),
                waits=[(ve, va2)])
            if pre_f32_hbm is not None:
                bb.wait("vector", *st_pre)
            bb.op("vector", lambda e: e.tensor_scalar_mul(stat[:, 5:6], stat[:, 5:6], 1.0 / E),
                  waits=[(act, sa3)])
            va3 = bb.op("vector", lambda e: e.tensor_copy(stat[:, 6:7], stat[:, 5:6]))
            if kind == "rms_plain":
                bb.op("scalar", lambda e: e.activation(stat[:, 6:7], stat[:, 6:7], AF.Ln),
                      waits=[(ve, va3)])
                sa4 = bb.op("scalar", lambda e: e.activation(stat[:, 6:7], stat[:, 6:7], AF.Exp, scale=0.5))
                bb.op("vector", lambda e, z=float(eps2): e.tensor_scalar_add(stat[:, 6:7], stat[:, 6:7], z),
                      waits=[(act, sa4)])
                bb.op("vector", lambda e: e.reciprocal(stat[:, 6:7], stat[:, 6:7]))
            elif kind == "rms_eps_inside":
                bb.op("scalar", lambda e, z=prog.constf(eps2): e.activation(
                    stat[:, 7:8], stat[:, 6:7], AF.Ln, bias=z), waits=[(ve, va3)])
                if rinv_out is not None:
                    bb.op("scalar", lambda e, tb=tb: e.activation(
                        rinv_out[:, tb:tb + 1], stat[:, 7:8], AF.Exp, scale=0.5))
                sa4 = bb.op("scalar", lambda e: e.activation(stat[:, 6:7], stat[:, 7:8], AF.Exp, scale=-0.5))
                bb.wait("vector", act, sa4)
            else:
                raise ValueError(kind)
            va2 = bb.op("vector", lambda e, b=buf: e.tensor_scalar_mul(
                xnf[b][:, :], xnf[b][:, :], stat[:, 6:7]))

        st_f32 = None
        if xn_f32_hbm is not None:
            st_f32 = bb.dma("sync", xn_f32_hbm[tb * 128:(tb + 1) * 128, :], xnf[buf][:, :],
                            waits=[(ve, va2)])
        if tok_cb is not None:
            tok_cb(bb, tb, xnf[buf], va2)

        w = [(pe, hist[tb - 2]["pe_tr"])] if tb >= 2 else []
        if xn_bf16_hbm is not None and tb >= 2:
            w.append(hist[tb - 2]["st_bf"])
        nb1 = bb.op("vector", lambda e, b=buf: e.tensor_copy(xnb[b][:, :], xnf[b][:, :]), waits=w)
        st_bf = None
        if xn_bf16_hbm is not None:
            st_bf = bb.dma("sync", xn_bf16_hbm[tb * 128:(tb + 1) * 128, :], xnb[buf][:, :],
                           waits=[(ve, nb1)])

        pe_tr = None
        for et in range(8):
            pool = tpA if et % 2 == 0 else tpB
            eng = "scalar" if et % 2 == 0 else "vector"
            ps, idx = pool.acquire(bb)
            pe_tr = pool.produce(bb, lambda e, b=buf, et=et, ps=ps: e.transpose(
                ps[:, :], xnb[b][:, et * 128:(et + 1) * 128], ident[:, :]),
                waits=[(ve, nb1)] if et == 0 else ())

            def evac(e, et=et, tb=tb, ps=ps, eng=eng):
                dst = xnT[et][:, pad_left + tb * 128: pad_left + (tb + 1) * 128]
                return e.copy(dst, ps[:, :]) if eng == "scalar" else e.tensor_copy(dst, ps[:, :])
            pool.consume(bb, idx, pe_tr, [evac])

        hist[tb] = dict(ve=va2, sa_sq=sa_sq, pe_tr=pe_tr, st_f32=st_f32, st_bf=st_bf,
                        st_pre=st_pre)
        info[tb] = hist[tb]

    info["pe_done"] = prog.chain["tensor"].v
    info["ve_done"] = prog.chain["vector"].v
    info["act_done"] = prog.chain["scalar"].v
    return info


def mm_feature_major(bb: BB, prog: Prog, w_tiles, rhs_tiles, n_rhs_cols,
                     pool: PsumPool, writer, m_blocks=None, rhs_col_base=0,
                     chunk=512, extra_mm=None, waits_first=()):
    kt = len(w_tiles)
    M = w_tiles[0].shape[1]
    if m_blocks is None:
        m_blocks = ceil_div(M, 128)
    first = True
    for mb in range(m_blocks):
        mw = min(128, M - mb * 128)
        for c0 in range(0, n_rhs_cols, chunk):
            c1 = min(c0 + chunk, n_rhs_cols)
            ps, idx = pool.acquire(bb)
            w = list(waits_first) if first else []
            first = False
            for k in range(kt):
                last = (k == kt - 1) and extra_mm is None
                fn = lambda e, k=k, mb=mb, c0=c0, c1=c1, ps=ps, last=last, mw=mw: e.matmul(
                    ps[:mw, :c1 - c0],
                    w_tiles[k][:, mb * 128:mb * 128 + mw],
                    rhs_tiles[k][:, rhs_col_base + c0: rhs_col_base + c1],
                    start=(k == 0), stop=last)
                if last:
                    tgt = pool.produce(bb, fn, waits=w)
                else:
                    bb.op("tensor", fn, waits=w, mark=False)
                    w = []
            if extra_mm is not None:
                tgt = extra_mm(bb, mb, c0, c1, ps)
            writer(bb, mb, c0, c1, ps, idx, tgt)


def mm_token_major(bb: BB, prog: Prog, actT_tiles, w_tiles, F_cols,
                   pool: PsumPool, writer, n_tb=NTB, act_col_base=0, chunk=512,
                   waits_first=(), extra_pairs=None):
    kt = len(actT_tiles)
    first = True
    for tb in range(n_tb):
        ex = extra_pairs(tb) if extra_pairs is not None else []
        for c0 in range(0, F_cols, chunk):
            c1 = min(c0 + chunk, F_cols)
            ps, idx = pool.acquire(bb)
            w = list(waits_first) if first else []
            w += [t for _, _, t in ex if t is not None]
            first = False
            net = len(ex)
            for k in range(kt + net):
                last = (k == kt + net - 1)
                if k < kt:
                    fn = lambda e, k=k, tb=tb, c0=c0, c1=c1, ps=ps, last=last: e.matmul(
                        ps[:, :c1 - c0],
                        actT_tiles[k][:, act_col_base + tb * 128: act_col_base + (tb + 1) * 128],
                        w_tiles[k][:, c0:c1],
                        start=(k == 0), stop=last)
                else:
                    fn = lambda e, k=k, c0=c0, c1=c1, ps=ps, last=last,                         ap=ex[k - kt][0], wt=ex[k - kt][1]: e.matmul(
                        ps[:, :c1 - c0], ap, wt[:, c0:c1], start=(k == 0), stop=last)
                if last:
                    tgt = pool.produce(bb, fn, waits=wmax(w))
                else:
                    bb.op("tensor", fn, waits=wmax(w), mark=False)
                    w = []
            writer(bb, tb, c0, c1, ps, idx, tgt)


def run_sim(prog: Prog, inputs: dict):
    from concourse.bass_interp import CoreSim
    sim = CoreSim(prog.nc, require_finite=False, require_nnan=False)
    for k, v in inputs.items():
        sim.tensor(k)[:] = v
    sim.simulate()
    return sim


def host_consts():
    import ml_dtypes
    c = {}
    c["cst_ident"] = np.eye(128, dtype=ml_dtypes.bfloat16)
    u = np.arange(1536)[None, :] - 640
    p = np.arange(128)[:, None]
    c["cst_tric"] = (p <= u).astype(ml_dtypes.bfloat16)
    c["cst_band"] = ((u - p > 0) & (u - p <= 128)).astype(ml_dtypes.bfloat16)
    c["cst_iotaq"] = np.broadcast_to(np.arange(512, dtype=np.float32)[None, :],
                                     (64, 512)).copy()
    c["cst_latb"] = (64.0 * (np.arange(64, dtype=np.float32)[:, None] + 1))
    c["cst_kpos32"] = (np.arange(128)[:, None] + 128 * np.arange(32)[None, :]).astype(np.float32)
    mA = np.zeros((1, 128), np.float32); mA[0, :64] = 1
    mB = np.zeros((1, 128), np.float32); mB[0, 64:] = 1
    c["cst_maskA"] = mA
    c["cst_maskB"] = mB
    c["cst_onesrow"] = np.ones((1, 128), np.float32)
    c["cst_ones128f"] = np.ones((128, 1), np.float32)
    c["cst_identf"] = np.eye(128, dtype=np.float32)
    c["cst_ones128"] = np.ones((128, 1), dtype=ml_dtypes.bfloat16)
    c["cst_iota128"] = np.arange(128, dtype=np.float32)[:, None]
    return c


def load_consts(bb: BB, prog: Prog):
    shapes = {"cst_ident": ([128, 128], BF16), "cst_tric": ([128, 1536], BF16),
              "cst_kpos32": ([128, 32], F32), "cst_maskA": ([1, 128], F32),
              "cst_maskB": ([1, 128], F32), "cst_ones128": ([128, 1], BF16),
              "cst_iota128": ([128, 1], F32), "cst_onesrow": ([1, 128], F32),
              "cst_ones128f": ([128, 1], F32), "cst_identf": ([128, 128], F32),
              "cst_band": ([128, 1536], BF16), "cst_iotaq": ([64, 512], F32),
              "cst_latb": ([64, 1], F32)}
    out = {}
    for name, (shape, dt) in shapes.items():
        t = prog.es.enter_context(prog.nc.sbuf_tensor(f"{name}_u{prog.uid()}", shape, dt))
        hbm = prog.dram(name, shape, dt, kind="ExternalInput")
        bb.dma("sync", t[:, :], hbm[:, :])
        out[name] = t
    prog.consts_tiles = out
    return out


def attention_heads(prog: Prog, *, q_stage, kT_ap, vaug, oT_stage, H, dh, dv,
                    score_scale, cconst=20.0, causal=False,
                    neg_slope_col=None, wmult_ap=None, mask_ap=None,
                    kb_range=None, denom_eps=0.0, n_tok=T, qchunk=512,
                    head_pre=None):
    assert dv + 1 <= 128
    u = prog.uid()
    ve, act, pe = prog.chain["vector"], prog.chain["scalar"], prog.chain["tensor"]
    nkb_all = n_tok // 128
    nqc = ceil_div(n_tok, qchunk)
    csts = prog.consts_tiles
    negc = prog.constf(-cconst)
    if kb_range is None:
        kbs_of = (lambda q0, qcw: range(ceil_div(q0 + qcw, 128))) if causal else (
            lambda q0, qcw: range(nkb_all))
    else:
        kbs_of = kb_range

    with prog.scope():
        qh = prog.sbuf(f"ah_q{u}", [128, n_tok], BF16)
        oh = prog.sbuf(f"ah_o{u}", [max(dv, 64), n_tok], BF16)
        expT = [prog.sbuf(f"ah_e{u}_{i}", [128, qchunk], BF16) for i in range(3)]
        drow = prog.sbuf(f"ah_d{u}", [1, n_tok], F32)
        bias_h = prog.sbuf(f"ah_b{u}", [128, 32], F32)
        spool = PsumPool(prog, f"ah_s{u}", [128, qchunk], 2, "scalar")
        avpool = PsumPool(prog, f"ah_av{u}", [dv + 1, qchunk], 2, "vector")
        rdpool = PsumPool(prog, f"ah_rd{u}", [dv, qchunk], 2, "vector")

        for h in range(H):
            qb0 = (h * dh) % 128
            with prog.block() as bb:
                ldq = bb.dma("sync", qh[qb0:qb0 + dh, :], q_stage[h * dh:(h + 1) * dh, :])
                if head_pre is not None:
                    head_pre(bb, h)
                if neg_slope_col is not None:
                    bias_ready = bb.op("vector", lambda e, h=h, _s=neg_slope_col(h), _c=negc:
                                       e.tensor_scalar(bias_h[:, :], csts["cst_kpos32"][:, :],
                                                       _s, _c, ALU.mult, ALU.add))
                else:
                    bias_ready = None
                eidx = 0
                ebuf_tgt = {}
                pend_av = None
                last_evac = None

                def flush_av(eidx_for):
                    nonlocal pend_av, last_evac
                    if pend_av is None:
                        return
                    av_tgt, is_last, cinfo = pend_av()
                    ebuf_tgt[eidx_for] = av_tgt
                    pend_av = None
                    if is_last:
                        av_idx_, ps_av_, q0_, qcw_ = cinfo
                        last_evac = avpool.consume(bb, av_idx_, av_tgt, [
                            lambda e, q0=q0_, qcw=qcw_, ps_av=ps_av_: e.tensor_copy(
                                oh[0:dv, q0:q0 + qcw], ps_av[0:dv, :qcw]),
                            lambda e, q0=q0_, qcw=qcw_, ps_av=ps_av_: e.tensor_copy(
                                drow[:, q0:q0 + qcw], ps_av[dv:dv + 1, :qcw]),
                        ])
                for qc in range(nqc):
                    q0 = qc * qchunk
                    qcw = min(qchunk, n_tok - q0)
                    kbs = list(kbs_of(q0, qcw))
                    ps_av, av_idx = avpool.acquire(bb)
                    for j, kb in enumerate(kbs):
                        ps_s, s_idx = spool.acquire(bb)
                        w = [ldq] if (qc == 0 and j == 0) else []
                        s_tgt = spool.produce(bb, lambda e, kb=kb, q0=q0, qcw=qcw, ps_s=ps_s, h=h, qb0=qb0:
                                              e.matmul(ps_s[:, :qcw],
                                                       kT_ap(h)[:, kb * 128:(kb + 1) * 128],
                                                       qh[qb0:qb0 + dh, q0:q0 + qcw],
                                                       start=True, stop=True),
                                              waits=w)
                        eb = expT[eidx % 3]
                        ew = []
                        if eidx >= 3:
                            ew.append((pe, ebuf_tgt[eidx - 3]))
                        if bias_ready is not None:
                            bias_ap = bias_h[:, kb:kb + 1]
                            if eidx == 0:
                                ew.append((ve, bias_ready))
                        else:
                            bias_ap = negc
                        e_tgt = spool.consume(bb, s_idx, s_tgt, [
                            lambda e, ps_s=ps_s, eb=eb, qcw=qcw, bias_ap=bias_ap: e.activation(
                                eb[:, :qcw], ps_s[:, :qcw], AF.Exp, bias=bias_ap,
                                scale=float(score_scale))], waits=ew)
                        av_w = (act, e_tgt)
                        m = mask_ap(h, kb, q0, qcw) if mask_ap is not None else None
                        if m is not None:
                            mv = bb.op("vector", lambda e, eb=eb, qcw=qcw, m=m: e.tensor_tensor(
                                eb[:, :qcw], eb[:, :qcw], m, ALU.mult), waits=[(act, e_tgt)])
                            av_w = (ve, mv)
                        wm = wmult_ap(h, kb, q0, qcw) if wmult_ap is not None else None
                        if wm is not None:
                            mv = bb.op("vector", lambda e, eb=eb, qcw=qcw, wm=wm: e.tensor_tensor(
                                eb[:, :qcw], eb[:, :qcw], wm, ALU.mult),
                                waits=[av_w] if av_w[0] is act else [])
                            av_w = (ve, mv)
                        flush_av(eidx - 1)

                        def mk_av(h=h, kb=kb, eb=eb, qcw=qcw, ps_av=ps_av, j=j,
                                  last=(j == len(kbs) - 1), av_w=av_w,
                                  av_idx=av_idx, q0=q0):
                            av_tgt = bb.op("tensor", lambda e: e.matmul(
                                ps_av[:, :qcw], vaug[kb][:, h * (dv + 1): (h + 1) * (dv + 1)],
                                eb[:, :qcw], start=(j == 0), stop=last),
                                waits=[av_w], mark=True)
                            return av_tgt, last, (av_idx, ps_av, q0, qcw)
                        pend_av = mk_av
                        eidx += 1
                flush_av(eidx - 1)
                bb.op("scalar", lambda e, z=prog.constf_row(float(denom_eps)): e.activation(
                    drow[:, :], drow[:, :], AF.Ln, bias=z), waits=[(ve, last_evac)])
                rcp = bb.op("scalar", lambda e: e.activation(
                    drow[:, :], drow[:, :], AF.Exp, scale=-1.0))
                for qc in range(nqc):
                    q0 = qc * qchunk
                    qcw = min(qchunk, n_tok - q0)
                    ps_rd, rd_idx = rdpool.acquire(bb)
                    rd_tgt = rdpool.produce(bb, lambda e, q0=q0, qcw=qcw, ps_rd=ps_rd: e.matmul(
                        ps_rd[:, :qcw], csts["cst_onesrow"][:, 0:dv], drow[:, q0:q0 + qcw],
                        start=True, stop=True), waits=[(act, rcp)] if qc == 0 else ())
                    rdpool.consume(bb, rd_idx, rd_tgt, [
                        lambda e, q0=q0, qcw=qcw, ps_rd=ps_rd: e.tensor_tensor(
                            oh[0:dv, q0:q0 + qcw], oh[0:dv, q0:q0 + qcw], ps_rd[:, :qcw], ALU.mult)])
                bb.dma("sync", oT_stage[h * dv:(h + 1) * dv, :], oh[0:dv, :],
                       waits=[(ve, prog.chain["vector"].v)])


def out_proj_residual(prog: Prog, *, act_stage, n_act, wo_tiles, x_hbm, xout_hbm,
                      row_bias_rep=None, alpha_col=None, extra_add_hbm=None,
                      n_tok=T, tb_per_blk=4, xout_fn=None, blk_pre=None,
                      extra_pairs=None):
    u = prog.uid()
    ve = prog.chain["vector"]
    EA = n_act // 128
    with prog.scope():
        a_ch = [prog.sbuf(f"op_a{u}_{i}", [128, tb_per_blk * 128], BF16) for i in range(EA)]
        xold = [prog.sbuf(f"op_x{u}_{i}", [128, E], F32) for i in range(tb_per_blk)]
        xex = [prog.sbuf(f"op_e{u}_{i}", [128, E], F32) for i in range(tb_per_blk)] if extra_add_hbm is not None else None
        ytile = [prog.sbuf(f"op_y{u}_{i}", [128, 512], F32) for i in range(4)]
        pool = PsumPool(prog, f"op_p{u}", [128, 512], 2, "vector")

        for blk0 in range(0, n_tok // 128, tb_per_blk):
            with prog.block() as bb:
                lda = []
                for i in range(EA):
                    lda.append(bb.dma("sync", a_ch[i][:, :],
                                      act_stage[i * 128:(i + 1) * 128,
                                                blk0 * 128:(blk0 + tb_per_blk) * 128]))
                ldr = []
                for j in range(tb_per_blk):
                    tb = blk0 + j
                    ldr.append(bb.dma("sync", xold[j][:, :], x_hbm[tb * 128:(tb + 1) * 128, :]))
                    if extra_add_hbm is not None:
                        ldr.append(bb.dma("sync", xex[j][:, :],
                                          extra_add_hbm[tb * 128:(tb + 1) * 128, :]))
                cnt = [0]
                st_hist = {}

                def owriter(bb, tb_loc, c0, c1, ps, idx, tgt):
                    k = cnt[0] % 4
                    tb = blk0 + tb_loc
                    fns = []
                    if xout_fn is not None:
                        xout_fn(bb, pool, tb, tb_loc, c0, c1, ps, idx, tgt, xold, ytile, k, ldr)
                        cnt[0] += 1
                        return
                    if alpha_col is not None:
                        fns.append(lambda e, ps=ps, k=k, c0=c0, c1=c1, tb_loc=tb_loc:
                                   e.scalar_tensor_tensor(ytile[k][:, :c1 - c0], ps[:, :c1 - c0],
                                                          alpha_col, xold[tb_loc][:, c0:c1],
                                                          ALU.mult, ALU.add))
                    elif row_bias_rep is not None:
                        fns.append(lambda e, ps=ps, k=k, c0=c0, c1=c1: e.tensor_tensor(
                            ytile[k][:, :c1 - c0], ps[:, :c1 - c0], row_bias_rep[:, c0:c1], ALU.add))
                        fns.append(lambda e, k=k, c0=c0, c1=c1, tb_loc=tb_loc: e.tensor_tensor(
                            ytile[k][:, :c1 - c0], ytile[k][:, :c1 - c0], xold[tb_loc][:, c0:c1], ALU.add))
                    else:
                        fns.append(lambda e, ps=ps, k=k, c0=c0, c1=c1, tb_loc=tb_loc: e.tensor_tensor(
                            ytile[k][:, :c1 - c0], ps[:, :c1 - c0], xold[tb_loc][:, c0:c1], ALU.add))
                    if extra_add_hbm is not None:
                        fns.append(lambda e, k=k, c0=c0, c1=c1, tb_loc=tb_loc: e.tensor_tensor(
                            ytile[k][:, :c1 - c0], ytile[k][:, :c1 - c0], xex[tb_loc][:, c0:c1], ALU.add))
                    wl_ = wmax(ldr)
                    if cnt[0] >= 4:
                        wl_ = wl_ + [st_hist[cnt[0] - 4]]
                    last = pool.consume(bb, idx, tgt, fns, waits=wl_)
                    st_hist[cnt[0]] = bb.dma("sync", xout_hbm[tb * 128:(tb + 1) * 128, c0:c1],
                           ytile[k][:, :c1 - c0], waits=[(ve, last)])
                    cnt[0] += 1

                if blk_pre is not None:
                    blk_pre(bb, blk0, a_ch, wmax(lda))
                mm_token_major(bb, prog, a_ch, wo_tiles, E, pool, owriter,
                               n_tb=tb_per_blk, waits_first=wmax(lda),
                               extra_pairs=extra_pairs)


def stage_fm_writer(prog, pool, stage_hbm, tmp_tiles, bias_tile=None, scale=None):
    n_tmp = len(tmp_tiles)
    st_hist = {}

    def writer(bb, mb, c0, c1, ps, idx, tgt):
        tt = tmp_tiles[mb % n_tmp]
        w = []
        if c0 == 0 and mb >= n_tmp:
            w.append(st_hist[mb - n_tmp])
        if bias_tile is not None:
            fn = lambda e, mb=mb, c0=c0, c1=c1, ps=ps, tt=tt: e.tensor_scalar(
                tt[:, c0:c1], ps[:, :c1 - c0], bias_tile[:, mb:mb + 1], None, ALU.add)
        elif scale is not None:
            fn = lambda e, mb=mb, c0=c0, c1=c1, ps=ps, tt=tt, s=float(scale): e.tensor_scalar_mul(
                tt[:, c0:c1], ps[:, :c1 - c0], s)
        else:
            fn = lambda e, mb=mb, c0=c0, c1=c1, ps=ps, tt=tt: e.tensor_copy(
                tt[:, c0:c1], ps[:, :c1 - c0])
        last = pool.consume(bb, idx, tgt, fns=[fn], waits=w)
        if c1 == tt.shape[1] or c1 == stage_hbm.shape[1]:
            st_hist[mb] = bb.dma("sync", stage_hbm[mb * 128:(mb + 1) * 128, :c1],
                                 tt[:, :c1], waits=[(prog.chain[pool.eng], last)])
    return writer


def causal_mask_ap(prog, qchunk=512):
    tric = prog.consts_tiles["cst_tric"]

    def f(h, kb, q0, qcw):
        if kb * 128 + 127 <= q0:
            return None
        off = 640 + q0 - kb * 128
        return tric[:, off:off + qcw]
    return f


def mm_feature_major_streamed(bb, prog, w_tiles, xn_stage, n_feat, n_tok,
                              pool, writer, m_blocks=None, chunk=512,
                              extra_rhs=None, extra_w=None, waits_first=(),
                              ld_bufs=2, chunk_cb=None, xb=None, pre_chunk=None):
    u = prog.uid()
    EQ = n_feat // 128
    if m_blocks is None:
        m_blocks = w_tiles[0].shape[1] // 128
    if xb is None:
        xb = [[prog.sbuf(f"fms{u}_{b}_{i}", [128, chunk], BF16) for i in range(EQ)]
              for b in range(ld_bufs)]
    ld_bufs = len(xb)
    first = True
    ld_hist = {}
    for ci, c0 in enumerate(range(0, n_tok, chunk)):
        c1 = min(c0 + chunk, n_tok)
        b = ci % ld_bufs
        w = []
        if ci >= ld_bufs:
            w = [(prog.chain["tensor"], ld_hist[ci - ld_bufs])]
        lds = []
        for i in range(EQ):
            lds.append(bb.dma("sync", xb[b][i][:, :c1 - c0],
                              xn_stage[i * 128:(i + 1) * 128, c0:c1],
                              waits=w if i == 0 else ()))
        if pre_chunk is not None:
            pre_chunk(bb, ci, c0, c1, xb[b], lds)
        ex = extra_rhs(c0, c1) if extra_rhs is not None else []
        for mb in range(m_blocks):
            pl = pool(mb) if callable(pool) else pool
            ps, idx = pl.acquire(bb)
            wf = wmax(list(lds) + [t for _, t in ex if t is not None])
            if first:
                wf += list(waits_first)
                first = False
            kt = len(w_tiles)
            net = len(ex)
            for k in range(kt + net):
                last = (k == kt + net - 1)
                if k < kt:
                    fn = lambda e, k=k, mb=mb, ps=ps, cw=c1 - c0, b=b, last=last: e.matmul(
                        ps[:, :cw], w_tiles[k][:, mb * 128:(mb + 1) * 128],
                        xb[b][k][:, :cw], start=(k == 0), stop=last)
                else:
                    fn = lambda e, k=k, mb=mb, ps=ps, cw=c1 - c0, last=last, ap=ex[k - kt][0]: e.matmul(
                        ps[:, :cw], extra_w[k - kt][:, mb * 128:(mb + 1) * 128],
                        ap, start=(k == 0), stop=last)
                if last:
                    tgt = bb.op("tensor", fn, waits=wf if k == 0 else (), mark=True)
                else:
                    bb.op("tensor", fn, waits=wf if k == 0 else (), mark=False)
                wf = ()
            ld_hist[ci] = tgt
            writer(bb, mb, c0, c1, ps, idx, tgt)
        if chunk_cb is not None:
            t2 = chunk_cb(bb, ci, c0, c1)
            if t2 is not None:
                ld_hist[ci] = t2


def mm_token_major_streamed(bb, prog, xn_stage, n_feat, w_tiles, F_cols,
                            pool, writer, n_tb=NTB, chunk=512, ld_bufs=2,
                            extra_act=None, extra_w=None, waits_first=(),
                            pre_tb=None):
    u = prog.uid()
    EA = n_feat // 128
    ab = [prog.sbuf(f"tms{u}_{b}", [128, EA * 128], BF16) for b in range(ld_bufs)]
    first = True
    ld_hist = {}
    for tb in range(n_tb):
        b = tb % ld_bufs
        w = []
        if tb >= ld_bufs:
            w = [(prog.chain["tensor"], ld_hist[tb - ld_bufs])]
        src = xn_stage.rearrange("(e p) t -> p e t", p=128)[:, :, tb * 128:(tb + 1) * 128]
        ld = bb.dma("sync", ab[b][:, :].rearrange("p (e c) -> p e c", c=128), src, waits=w)
        if pre_tb is not None:
            pre_tb(bb, tb, ab[b], ld)
        ex = extra_act(tb) if extra_act is not None else []
        for c0 in range(0, F_cols, chunk):
            c1 = min(c0 + chunk, F_cols)
            ps, idx = pool.acquire(bb)
            wf = [ld] + [t for _, t in ex if t is not None]
            if first:
                wf += list(waits_first)
                first = False
            kt = EA
            net = len(ex)
            for k in range(kt + net):
                last = (k == kt + net - 1)
                if k < kt:
                    fn = lambda e, k=k, ps=ps, c0=c0, c1=c1, b=b: e.matmul(
                        ps[:, :c1 - c0], ab[b][:, k * 128:(k + 1) * 128],
                        w_tiles[k][:, c0:c1], start=(k == 0), stop=False if k < kt + net - 1 else True)
                else:
                    fn = lambda e, k=k, ps=ps, c0=c0, c1=c1, ap=ex[k - kt][0], last=last: e.matmul(
                        ps[:, :c1 - c0], ap, extra_w[k - kt][:, c0:c1],
                        start=(k == 0), stop=last)
                if last:
                    tgt = bb.op("tensor", fn, waits=wmax(wf) if k == 0 else (), mark=True)
                else:
                    bb.op("tensor", fn, waits=wmax(wf) if k == 0 else (), mark=False)
                wf = []
            ld_hist[tb] = tgt
            writer(bb, tb, c0, c1, ps, idx, tgt)


def stage_fm_chunk_writer(prog, pool, stage_hbm, bias_tile=None, scale=None, n_tmp=4):
    u = prog.uid()
    tmp = [prog.sbuf(f"sfc{u}_{i}", [128, 512], BF16) for i in range(n_tmp)]
    cnt = [0]
    st_hist = {}

    def writer(bb, mb, c0, c1, ps, idx, tgt):
        k = cnt[0] % n_tmp
        w = [st_hist[cnt[0] - n_tmp]] if cnt[0] >= n_tmp else []
        if bias_tile is not None:
            fn = lambda e, mb=mb, c0=c0, c1=c1, ps=ps, k=k: e.tensor_scalar(
                tmp[k][:, :c1 - c0], ps[:, :c1 - c0], bias_tile[:, mb:mb + 1], None, ALU.add)
        elif scale is not None:
            fn = lambda e, c0=c0, c1=c1, ps=ps, k=k, sc=float(scale): e.tensor_scalar_mul(
                tmp[k][:, :c1 - c0], ps[:, :c1 - c0], sc)
        else:
            fn = lambda e, c0=c0, c1=c1, ps=ps, k=k: e.tensor_copy(
                tmp[k][:, :c1 - c0], ps[:, :c1 - c0])
        last = pool.consume(bb, idx, tgt, [fn], waits=w)
        st_hist[cnt[0]] = bb.dma("sync", stage_hbm[mb * 128:(mb + 1) * 128, c0:c1],
                                 tmp[k][:, :c1 - c0], waits=[(prog.chain[pool.eng], last)])
        cnt[0] += 1
    return writer


def add_residual_pass(prog: Prog, x_hbm, y_hbm, xout_hbm, n_tok=T, y_dtype=F32):
    u = prog.uid()
    ve = prog.chain["vector"]
    with prog.scope():
        xa = [prog.sbuf(f"arp_x{u}_{i}", [128, E], F32) for i in range(2)]
        ya = [prog.sbuf(f"arp_y{u}_{i}", [128, E], y_dtype) for i in range(2)]
        with prog.block() as bb:
            hist = {}
            for tb in range(n_tok // 128):
                b = tb % 2
                w = [(ve, hist[tb - 2][0])] if tb >= 2 else []
                if tb >= 2:
                    w.append(hist[tb - 2][1])
                l1 = bb.dma("sync", xa[b][:, :], x_hbm[tb * 128:(tb + 1) * 128, :], waits=w)
                l2 = bb.dma("sync", ya[b][:, :], y_hbm[tb * 128:(tb + 1) * 128, :])
                a = bb.op("vector", lambda e, b=b: e.tensor_add(xa[b][:, :], xa[b][:, :], ya[b][:, :]),
                          waits=wmax([l1, l2]))
                st = bb.dma("sync", xout_hbm[tb * 128:(tb + 1) * 128, :], xa[b][:, :],
                            waits=[(ve, a)])
                hist[tb] = (a, st)


def ffn_expert_pass(bb, prog, ctx, *, xn_stage, w1_tiles, w2_tiles, swiglu,
                    wcomb_col, y_hbm, first_expert, inter=2048, n_tok=T,
                    tc_chunk=512):
    act, ve, pe = prog.chain["scalar"], prog.chain["vector"], prog.chain["tensor"]
    IMB = inter // 128
    pA, pB, ypool = ctx["pA"], ctx["pB"], ctx["ypool"]
    h1sb, hT, ytile = ctx["h1sb"], ctx["hT"], ctx["ytile"]
    h1_tgt = {}
    hT_tgt = {}
    y_done = ctx.setdefault("y_done", [0])
    ycnt = ctx.setdefault("ycnt", [0])
    yst = ctx.setdefault("yst", {})

    def hwriter(bb, mb, c0, c1, ps, idx, tgt):
        w = [(pe, y_done[0])] if (mb == 0 or mb == IMB) and y_done[0] else []
        if swiglu and mb >= IMB:
            m2 = mb - IMB
            hT_tgt[m2] = pB.consume(bb, idx, tgt, [
                lambda e, m2=m2, ps=ps, cw=c1 - c0: e.tensor_tensor(
                    hT[m2][:, :cw], ps[:, :cw], h1sb[m2][:, :cw], ALU.mult)],
                waits=w + [(act, h1_tgt[m2])])
        elif swiglu:
            h1_tgt[mb] = pA.consume(bb, idx, tgt, [
                lambda e, mb=mb, ps=ps, cw=c1 - c0: e.activation(
                    h1sb[mb][:, :cw], ps[:, :cw], AF.Silu)], waits=w)
        else:
            hT_tgt[mb] = pA.consume(bb, idx, tgt, [
                lambda e, mb=mb, ps=ps, cw=c1 - c0: e.activation(
                    hT[mb][:, :cw], ps[:, :cw], AF.Silu)], waits=w)

    def chunk_cb(bb, ci, c0, c1):
        tbs = (c1 - c0) // 128
        hready = [(pB if swiglu else pA).prog.chain[(pB if swiglu else pA).eng],
                  max(hT_tgt.values())] if hT_tgt else None

        def ywriter(bb, tb_loc, y0, y1, ps, idx, tgt):
            tbg = (c0 + tb_loc * 128) // 128
            k = ycnt[0] % 4
            w = [yst[ycnt[0] - 4]] if ycnt[0] >= 4 else []
            col = wcomb_col(tbg) if wcomb_col is not None else None
            if col is not None:
                fn = lambda e, ps=ps, k=k, y0=y0, y1=y1, col=col: e.tensor_scalar_mul(
                    ytile[k][:, :y1 - y0], ps[:, :y1 - y0], col)
            else:
                fn = lambda e, ps=ps, k=k, y0=y0, y1=y1: e.tensor_copy(
                    ytile[k][:, :y1 - y0], ps[:, :y1 - y0])
            last = ypool.consume(bb, idx, tgt, [fn], waits=w)
            yst[ycnt[0]] = bb.dma(
                "gpsimd", y_hbm[tbg * 128:(tbg + 1) * 128, y0:y1], ytile[k][:, :y1 - y0],
                waits=[(ve, last)],
                accum_op=(ALU.bypass if first_expert else ALU.add))
            ycnt[0] += 1

        mm_token_major(bb, prog, hT, w2_tiles, E, ypool, ywriter, n_tb=tbs,
                       waits_first=[hready] if hready else ())
        y_done[0] = prog.chain["tensor"].v
        hT_tgt.clear()
        h1_tgt.clear()
        return y_done[0]

    mm_feature_major_streamed(
        bb, prog, w1_tiles, xn_stage, E, n_tok, (lambda mb: pB if (swiglu and mb >= IMB) else pA),
        hwriter, m_blocks=(2 * IMB if swiglu else IMB), chunk=tc_chunk,
        chunk_cb=chunk_cb, xb=ctx["xb"])


def ffn_make_ctx(prog, inter=2048, tc_chunk=512):
    IMB = inter // 128
    u = prog.uid()
    return dict(
        pA=PsumPool(prog, f"ffA{u}", [128, tc_chunk], 2, "scalar"),
        pB=PsumPool(prog, f"ffB{u}", [128, tc_chunk], 2, "vector"),
        ypool=PsumPool(prog, f"ffY{u}", [128, 512], 2, "vector"),
        h1sb=[prog.sbuf(f"ffh1{u}_{i}", [128, tc_chunk], BF16) for i in range(IMB)],
        hT=[prog.sbuf(f"ffh{u}_{i}", [128, tc_chunk], BF16) for i in range(IMB)],
        ytile=[prog.sbuf(f"ffy{u}_{i}", [128, 512], F32) for i in range(4)],
        xb=[[prog.sbuf(f"ffx{u}_{b}_{i}", [128, tc_chunk], BF16) for i in range(E // 128)]
            for b in range(2)],
    )

PP = PsumPool
"""Per-sublayer program fragments. Each sub_* authors blocks computing
x_out = x + f(LN(x)) for its sublayer; x/x_out are HBM [T, E] f32."""
import numpy as np
import concourse.bass as bass


def bias_fm_writer(prog, pool, outT, bias_tile):
    def writer(bb, mb, c0, c1, ps, idx, tgt):
        pool.consume(bb, idx, tgt, [
            lambda e, mb=mb, c0=c0, c1=c1, ps=ps: e.tensor_scalar(
                outT[mb][:, c0:c1], ps[:, :c1 - c0], bias_tile[:, mb:mb + 1], None, ALU.add)])
    return writer


def copy_fm_writer(prog, pool, outT):
    def writer(bb, mb, c0, c1, ps, idx, tgt):
        pool.consume(bb, idx, tgt, [
            lambda e, mb=mb, c0=c0, c1=c1, ps=ps: e.tensor_copy(outT[mb][:, c0:c1], ps[:, :c1 - c0])])
    return writer


def broadcast_row(bb, prog, row_ap, width, out_tile, pool):
    onesrow = prog.consts_tiles["cst_onesrow"]
    for c0 in range(0, width, 512):
        c1 = min(c0 + 512, width)
        ps, idx = pool.acquire(bb)
        tgt = pool.produce(bb, lambda e, c0=c0, c1=c1, ps=ps: e.matmul(
            ps[:, :c1 - c0], onesrow[:, :], row_ap[:, c0:c1], start=True, stop=True))
        pool.consume(bb, idx, tgt, [
            lambda e, c0=c0, c1=c1, ps=ps: e.tensor_copy(out_tile[:, c0:c1], ps[:, :c1 - c0])])


def vaug_tm_writer(prog, pool, vaug, dv, bias_rep=None):
    def writer(bb, tb, c0, c1, ps, idx, tgt):
        base0 = (c0 // dv) * (dv + 1)
        nh = (c1 - c0) // dv
        dst = vaug[tb][:, base0: base0 + nh * (dv + 1)].rearrange(
            "p (h d) -> p h d", d=dv + 1)[:, :, 0:dv]
        src = ps[:, :c1 - c0].rearrange("p (h d) -> p h d", d=dv)
        if bias_rep is not None:
            fn = lambda e, dst=dst, src=src, c0=c0, c1=c1: e.tensor_tensor(
                dst, src, bias_rep[:, c0:c1].rearrange("p (h d) -> p h d", d=dv), ALU.add)
        else:
            fn = lambda e, dst=dst, src=src: e.tensor_copy(dst, src)
        pool.consume(bb, idx, tgt, [fn])
    return writer


def vaug_memset_ones(bb, prog, vaug, dv):
    for tb in range(len(vaug)):
        bb.op("vector", lambda e, tb=tb: e.memset(
            vaug[tb][:, :].rearrange("p (h d) -> p h d", d=dv + 1)[:, :, dv:dv + 1], 1.0))


def sub_l3_attn_b(prog: Prog, x_hbm, xout_hbm, w, uid):
    H, dh, dv = 16, 64, 64
    csts = prog.consts_tiles
    ident = csts["cst_ident"]
    q_stage = prog.dram(f"stg_q_{uid}", [H * dh, T], BF16)
    o_stage = prog.dram(f"stg_o_{uid}", [H * dv, T], BF16)

    with prog.scope():
        with prog.block() as bb:
            bq = prog.sbuf("bq", [128, 8], F32)
            bl = prog.sbuf("bl", [128, 2], F32)
            bk = prog.sbuf("bk", [128, 8], F32)
            borow = prog.sbuf("borow", [1, E], F32)
            bb.dma("sync", bq[:, :], w["bq"][:, :])
            bb.dma("sync", bl[:, :], w["bl"][:, :])
            bb.dma("sync", bk[:, :], w["bk"][:, :])
            bb.dma("sync", borow[:, :], w["bo"][:, :])
        borep = prog.sbuf("borep", [128, E], BF16)
        _l3b_scope = prog.scope(); _l3b_scope.__enter__()
        latT = [prog.sbuf(f"latT{i}", [128, T], BF16) for i in range(2)]

        with prog.scope():
            xnT = [prog.sbuf(f"xnT{i}", [128, T], BF16) for i in range(8)]
            with prog.block() as bb:
                wq, _ = load_w_bf16(bb, prog, w["Wq"], name="wq")
                wl, _ = load_w_bf16(bb, prog, w["Wl"], name="wl")
                ln_to_xnT(bb, prog, x_hbm, xnT, ident)
            with prog.block() as bb:
                pl = PsumPool(prog, "pl", [128, 512], 2, "scalar")

                def lat_writer(bb, mb, c0, c1, ps, idx, tgt):
                    pl.consume(bb, idx, tgt, [
                        lambda e, mb=mb, c0=c0, c1=c1, ps=ps: e.activation(
                            latT[mb][:, c0:c1], ps[:, :c1 - c0], AF.Identity,
                            bias=bl[:, mb:mb + 1])])
                mm_feature_major(bb, prog, wl, xnT, T, pl, lat_writer, m_blocks=2)
                pq = PsumPool(prog, "pq", [128, 512], 2, "vector")
                qtmp = [prog.sbuf(f"qtmp{i}", [128, T], BF16) for i in range(2)]
                mm_feature_major(bb, prog, wq, xnT, T, pq,
                                 stage_fm_writer(prog, pq, q_stage, qtmp, bias_tile=bq))

        kT = [prog.sbuf(f"kT{i}", [128, T], BF16) for i in range(8)]
        vaug = [prog.sbuf(f"va{i}", [128, H * (dv + 1)], BF16) for i in range(NTB)]
        with prog.scope():
            with prog.block() as bb:
                wk, _ = load_w_bf16(bb, prog, w["Wk"], name="wk")
                wv, _ = load_w_bf16(bb, prog, w["Wv"], name="wv")
                bvrow = prog.sbuf("bvrow", [1, E], F32)
                bb.dma("sync", bvrow[:, :], w["bv"][:, :])
            bvrep = prog.sbuf("bvrep", [128, E], F32)
            with prog.block() as bb:
                bp = PsumPool(prog, "brc", [128, 512], 2, "vector")
                broadcast_row(bb, prog, bvrow, E, bvrep, bp)
                broadcast_row(bb, prog, borow, E, borep, bp)
            with prog.block() as bb:
                pk = PsumPool(prog, "pk", [128, 512], 2, "vector")
                mm_feature_major(bb, prog, wk, latT, T, pk, bias_fm_writer(prog, pk, kT, bk))
                pv = PsumPool(prog, "pv", [128, 512], 2, "vector")
                vaug_memset_ones(bb, prog, vaug, dv)
                mm_token_major(bb, prog, latT, wv, E, pv,
                               vaug_tm_writer(prog, pv, vaug, dv, bias_rep=bvrep))

        attention_heads(prog, q_stage=q_stage,
                        kT_ap=lambda h: kT[h // 2][(h % 2) * 64:(h % 2) * 64 + dh, :],
                        vaug=vaug, oT_stage=o_stage, H=H, dh=dh, dv=dv,
                        score_scale=dh ** -0.5, cconst=20.0)
        _l3b_scope.__exit__(None, None, None)

        with prog.block() as bb:
            wo, _ = load_w_bf16(bb, prog, w["Wo"], name="wo")
        out_proj_residual(prog, act_stage=o_stage, n_act=H * dv, wo_tiles=wo,
                          x_hbm=x_hbm, xout_hbm=xout_hbm, row_bias_rep=borep)


def sub_l0_attn_a(prog: Prog, x_hbm, xout_hbm, w, uid):
    H, dh, dv = 16, 64, 64
    csts = prog.consts_tiles
    ident = csts["cst_ident"]
    q_stage = prog.dram(f"stg_q_{uid}", [H * dh, T], BF16)
    o_stage = prog.dram(f"stg_o_{uid}", [H * dv, T], BF16)
    xn_stage = prog.dram(f"stg_xn_{uid}", [E, T], BF16)

    with prog.scope():
        with prog.block() as bb:
            slopes = prog.sbuf("slopes", [1, 16], F32)
            bb.dma("sync", slopes[:, :], w["slopes"][:, :])
            bqT, _ = load_w_bf16(bb, prog, w["BqT"], name="bqT")
            bkT, _ = load_w_bf16(bb, prog, w["BkT"], name="bkT")
            bvT, _ = load_w_bf16(bb, prog, w["BvT"], name="bvT")
            aqT, _ = load_w_bf16(bb, prog, w["AqT"], name="aqT")
            akT, _ = load_w_bf16(bb, prog, w["AkT"], name="akT")
            avT, _ = load_w_bf16(bb, prog, w["AvT"], name="avT")
        nslopes = prog.sbuf("nslopes", [128, 16], F32)
        with prog.block() as bb:
            bp = PsumPool(prog, "brc", [128, 512], 1, "vector")
            ps, idx = bp.acquire(bb)
            tgt = bp.produce(bb, lambda e, ps=ps: e.matmul(
                ps[:, :16], csts["cst_onesrow"][:, :], slopes[:, :], start=True, stop=True))
            bp.consume(bb, idx, tgt, [
                lambda e, ps=ps: e.tensor_scalar_mul(nslopes[:, :], ps[:, :16], -1.0)])

        t1q = prog.sbuf("t1q", [8, T], BF16)
        t1k = prog.sbuf("t1k", [8, T], BF16)
        t1v = prog.sbuf("t1v", [8, T], BF16)

        with prog.scope():
            xnT = [prog.sbuf(f"xnT{i}", [128, T], BF16) for i in range(8)]
            with prog.block() as bb:
                ln_to_xnT(bb, prog, x_hbm, xnT, ident)
            with prog.block() as bb:
                pt = PsumPool(prog, "pt", [128, 512], 2, "vector")

                def t1_writer(dst):
                    def wr(bb, mb, c0, c1, ps, idx, tgt):
                        pt.consume(bb, idx, tgt, [
                            lambda e, c0=c0, c1=c1, ps=ps: e.tensor_scalar_mul(
                                dst[:, c0:c1], ps[0:8, :c1 - c0], 2.0)])
                    return wr
                mm_feature_major(bb, prog, bqT, xnT, T, pt, t1_writer(t1q), m_blocks=1)
                mm_feature_major(bb, prog, bkT, xnT, T, pt, t1_writer(t1k), m_blocks=1)
                mm_feature_major(bb, prog, bvT, xnT, T, pt, t1_writer(t1v), m_blocks=1)
                for i in range(8):
                    bb.dma("sync", xn_stage[i * 128:(i + 1) * 128, :], xnT[i][:, :])

        with prog.scope():
            kT = [prog.sbuf(f"kT{i}", [128, T], BF16) for i in range(8)]
            with prog.scope():
                with prog.block() as bb:
                    wkT, _ = load_w_bf16(bb, prog, w["WkT"], name="wkT")
                    wqT, _ = load_w_bf16(bb, prog, w["WqT"], name="wqT")
                with prog.block() as bb:
                    pk = PsumPool(prog, "pk", [128, 512], 2, "vector")
                    mm_feature_major_streamed(
                        bb, prog, wkT, xn_stage, E, T, pk, copy_fm_writer(prog, pk, kT),
                        extra_rhs=lambda c0, c1: [(t1k[:, c0:c1], None)],
                        extra_w=[akT[0][0:8, :]])
                with prog.block() as bb:
                    pq = PsumPool(prog, "pq", [128, 512], 2, "vector")
                    mm_feature_major_streamed(
                        bb, prog, wqT, xn_stage, E, T, pq,
                        stage_fm_chunk_writer(prog, pq, q_stage),
                        extra_rhs=lambda c0, c1: [(t1q[:, c0:c1], None)],
                        extra_w=[aqT[0][0:8, :]])

            vaug = [prog.sbuf(f"va{i}", [128, H * (dv + 1)], BF16) for i in range(NTB)]
            with prog.scope():
                with prog.block() as bb:
                    wvT, _ = load_w_bf16(bb, prog, w["WvT"], name="wvT")
                with prog.block() as bb:
                    pv = PsumPool(prog, "pv", [128, 512], 2, "vector")
                    vaug_memset_ones(bb, prog, vaug, dv)
                    mm_token_major_streamed(
                        bb, prog, xn_stage, E, wvT, E, pv,
                        vaug_tm_writer(prog, pv, vaug, dv),
                        extra_act=lambda tb: [(t1v[:, tb * 128:(tb + 1) * 128], None)],
                        extra_w=[avT[0][0:8, :]])

            attention_heads(prog, q_stage=q_stage,
                            kT_ap=lambda h: kT[h // 2][(h % 2) * 64:(h % 2) * 64 + dh, :],
                            vaug=vaug, oT_stage=o_stage, H=H, dh=dh, dv=dv,
                            score_scale=dh ** -0.5, cconst=20.0, causal=True,
                            neg_slope_col=lambda h: nslopes[:, h:h + 1],
                            mask_ap=causal_mask_ap(prog))

        with prog.block() as bb:
            wpT, _ = load_w_bf16(bb, prog, w["WpT"], name="wpT")
        out_proj_residual(prog, act_stage=o_stage, n_act=H * dv, wo_tiles=wpT,
                          x_hbm=x_hbm, xout_hbm=xout_hbm)


def sub_l3_ffn(prog: Prog, x_hbm, xout_hbm, w, uid):
    csts = prog.consts_tiles
    ident = csts["cst_ident"]
    RMS_EPS = 1.1920929e-07
    NE = 14
    xn_stage = prog.dram(f"stg_xn_{uid}", [E, T], BF16)
    y_hbm = prog.dram(f"stg_y_{uid}", [T, E], F32)

    with prog.scope():
        rinv = prog.sbuf("rinv", [128, NTB], F32)
        wcomb = prog.sbuf("wcomb", [128, NTB * NE], F32)
        rms_cols = prog.sbuf("rmsc", [128, 16 * 8], F32)
        with prog.block() as bb:
            bb.dma("sync", rms_cols[:, :], w["rms_cols"][:, :])
            rw, _ = load_w_bf16(bb, prog, w["router_w"], name="rw")

        with prog.scope():
            xnT = [prog.sbuf(f"xnT{i}", [128, T], BF16) for i in range(8)]
            with prog.block() as bb:
                info = ln_to_xnT(bb, prog, x_hbm, xnT, ident,
                                 second_norm=("rms_eps_inside", RMS_EPS), rinv_out=rinv)
                for i in range(8):
                    bb.dma("sync", xn_stage[i * 128:(i + 1) * 128, :], xnT[i][:, :],
                           waits=[(prog.chain["vector"], info["ve_done"]),
                                  (prog.chain["scalar"], info["act_done"])] if i == 0 else ())

        with prog.scope():
            with prog.block() as bb:
                rp = PsumPool(prog, "rp", [128, 32], 2, "vector")
                g = prog.sbuf("rg", [128, NE], F32)
                st = prog.sbuf("rst", [128, 8], F32)
                act = prog.chain["scalar"]
                ve = prog.chain["vector"]

                def rwriter(bb, tb, c0, c1, ps, idx, tgt):
                    rp.consume(bb, idx, tgt, [
                        lambda e, ps=ps, tb=tb: e.tensor_scalar_mul(
                            g[:, :], ps[:, :NE], rinv[:, tb:tb + 1]),
                        lambda e: e.reduce_max(st[:, 0:1], g[:, :], axis=AX.X),
                        lambda e: e.tensor_scalar_mul(st[:, 1:2], st[:, 0:1], -1.0),
                    ])
                    a1 = bb.op("scalar", lambda e: e.activation(
                        g[:, :], g[:, :], AF.Exp, bias=st[:, 1:2]),
                        waits=[(ve, prog.chain["vector"].v)])
                    bb.op("vector", lambda e: e.reduce_sum(st[:, 2:3], g[:, :], axis=AX.X),
                          waits=[(act, a1)])
                    bb.op("vector", lambda e: e.reciprocal(st[:, 2:3], st[:, 2:3]))
                    bb.op("vector", lambda e: e.tensor_scalar_mul(g[:, :], g[:, :], st[:, 2:3]))
                    bb.op("vector", lambda e: e.reduce_max(st[:, 3:4], g[:, :], axis=AX.X))
                    bb.op("vector", lambda e, tb=tb: e.tensor_scalar(
                        wcomb[:, tb * NE:(tb + 1) * NE], g[:, :], st[:, 3:4], None, ALU.is_ge))
                    bb.op("vector", lambda e, tb=tb: e.scalar_tensor_tensor(
                        wcomb[:, tb * NE:(tb + 1) * NE], wcomb[:, tb * NE:(tb + 1) * NE],
                        -1e30, g[:, :], ALU.mult, ALU.add))
                    bb.op("vector", lambda e, tb=tb: e.reduce_max(
                        st[:, 6:7], wcomb[:, tb * NE:(tb + 1) * NE], axis=AX.X))
                    bb.op("vector", lambda e, tb=tb: e.tensor_scalar(
                        wcomb[:, tb * NE:(tb + 1) * NE], g[:, :], st[:, 6:7], None, ALU.is_ge))
                    bb.op("vector", lambda e, tb=tb: e.tensor_tensor(
                        wcomb[:, tb * NE:(tb + 1) * NE], wcomb[:, tb * NE:(tb + 1) * NE],
                        g[:, :], ALU.mult))
                    bb.op("vector", lambda e, tb=tb: e.reduce_sum(
                        st[:, 7:8], wcomb[:, tb * NE:(tb + 1) * NE], axis=AX.X))
                    bb.op("vector", lambda e: e.tensor_scalar_add(st[:, 7:8], st[:, 7:8], 1e-9))
                    bb.op("vector", lambda e: e.reciprocal(st[:, 7:8], st[:, 7:8]))
                    bb.op("vector", lambda e, tb=tb: e.tensor_scalar_mul(
                        wcomb[:, tb * NE:(tb + 1) * NE], wcomb[:, tb * NE:(tb + 1) * NE],
                        st[:, 7:8]))

                mm_token_major_streamed(bb, prog, xn_stage, E, rw, NE, rp, rwriter)

        with prog.scope():
            ctx = ffn_make_ctx(prog, inter=2048)
            w1t = [prog.sbuf(f"ew1_{i}", [128, 2048], BF16) for i in range(8)]
            w2t = [prog.sbuf(f"ew2_{i}", [128, 1024], BF16) for i in range(16)]
            for e in range(16):
                routed = e < NE
                nm = f"w1_{e}" if routed else f"w1s_{e - NE}"
                nm2 = f"w2_{e}" if routed else f"w2s_{e - NE}"
                with prog.block() as bb:
                    _, t1 = load_w_bf16(bb, prog, w[nm], name=f"lw1{e}", tiles=w1t)
                    _, t2 = load_w_bf16(bb, prog, w[nm2], name=f"lw2{e}", tiles=w2t)
                    for k in range(8):
                        bb.op("vector", lambda e_, k=k, e=e: e_.tensor_scalar_mul(
                            w1t[k][:, :], w1t[k][:, :], rms_cols[:, e * 8 + k:e * 8 + k + 1]),
                            waits=wmax(t1 + t2) if k == 0 else ())
                with prog.block() as bb:
                    ffn_expert_pass(bb, prog, ctx, xn_stage=xn_stage,
                                    w1_tiles=w1t, w2_tiles=w2t, swiglu=False,
                                    wcomb_col=(lambda tbg, e=e: wcomb[:, tbg * NE + e: tbg * NE + e + 1]) if routed else None,
                                    y_hbm=y_hbm, first_expert=(e == 0), inter=2048)

        add_residual_pass(prog, x_hbm, y_hbm, xout_hbm)


def sub_l0_ffn(prog: Prog, x_hbm, xout_hbm, w, uid):
    csts = prog.consts_tiles
    ident = csts["cst_ident"]
    NE = 8
    xn_stage = prog.dram(f"stg_xn_{uid}", [E, T], BF16)
    y_hbm = prog.dram(f"stg_y_{uid}", [T, E], F32)

    with prog.scope():
        wcomb = prog.sbuf("wcomb", [128, NTB * NE], F32)
        gbias_rep = prog.sbuf("gbrep", [128, NE], F32)
        with prog.block() as bb:
            gw, _ = load_w_bf16(bb, prog, w["gate_wT"], name="gw")
            gbrow = prog.sbuf("gbrow", [1, NE], F32)
            bb.dma("sync", gbrow[:, :], w["gate_bias"][:, :])
        with prog.block() as bb:
            bp = PsumPool(prog, "brc", [128, 32], 1, "vector")
            ps, idx = bp.acquire(bb)
            tgt = bp.produce(bb, lambda e, ps=ps: e.matmul(
                ps[:, :NE], csts["cst_onesrow"][:, :], gbrow[:, :], start=True, stop=True))
            bp.consume(bb, idx, tgt, [
                lambda e, ps=ps: e.tensor_copy(gbias_rep[:, :], ps[:, :NE])])

        with prog.scope():
            xnT = [prog.sbuf(f"xnT{i}", [128, T], BF16) for i in range(8)]
            with prog.block() as bb:
                info = ln_to_xnT(bb, prog, x_hbm, xnT, ident)
                for i in range(8):
                    bb.dma("sync", xn_stage[i * 128:(i + 1) * 128, :], xnT[i][:, :],
                           waits=[(prog.chain["vector"], info["ve_done"]),
                                  (prog.chain["scalar"], info["act_done"])] if i == 0 else ())

        with prog.scope():
            with prog.block() as bb:
                rp = PsumPool(prog, "rp", [128, 32], 2, "vector")
                g = prog.sbuf("rg", [128, NE], F32)
                st = prog.sbuf("rst", [128, 8], F32)
                act = prog.chain["scalar"]
                ve = prog.chain["vector"]

                def rwriter(bb, tb, c0, c1, ps, idx, tgt):
                    rp.consume(bb, idx, tgt, [
                        lambda e, ps=ps: e.tensor_copy(g[:, :], ps[:, :NE]),
                        lambda e: e.reduce_max(st[:, 0:1], g[:, :], axis=AX.X),
                        lambda e: e.tensor_scalar_mul(st[:, 1:2], st[:, 0:1], -1.0),
                    ])
                    a1 = bb.op("scalar", lambda e: e.activation(
                        g[:, :], g[:, :], AF.Exp, bias=st[:, 1:2]),
                        waits=[(ve, prog.chain["vector"].v)])
                    bb.op("vector", lambda e: e.reduce_sum(st[:, 2:3], g[:, :], axis=AX.X),
                          waits=[(act, a1)])
                    bb.op("vector", lambda e: e.reciprocal(st[:, 2:3], st[:, 2:3]))
                    bb.op("vector", lambda e: e.tensor_scalar_mul(g[:, :], g[:, :], st[:, 2:3]))
                    wsl = lambda tb=tb: wcomb[:, tb * NE:(tb + 1) * NE]
                    bb.op("vector", lambda e, tb=tb: e.tensor_tensor(
                        wsl(tb), g[:, :], gbias_rep[:, :], ALU.add))
                    bb.op("vector", lambda e, tb=tb: e.reduce_max(st[:, 3:4], wsl(tb), axis=AX.X))
                    bb.op("vector", lambda e, tb=tb: e.tensor_scalar(
                        st[:, 4:5], st[:, 3:4], 1.0, None, ALU.mult))
                    bb.op("vector", lambda e, tb=tb: e.tensor_scalar(
                        g2[:, :], wsl(tb), st[:, 3:4], None, ALU.is_ge))
                    bb.op("vector", lambda e, tb=tb: e.scalar_tensor_tensor(
                        g2[:, :], g2[:, :], -1e30, wsl(tb), ALU.mult, ALU.add))
                    bb.op("vector", lambda e: e.reduce_max(st[:, 6:7], g2[:, :], axis=AX.X))
                    bb.op("vector", lambda e, tb=tb: e.tensor_scalar(
                        g2[:, :], wsl(tb), st[:, 6:7], None, ALU.is_ge))
                    bb.op("vector", lambda e, tb=tb: e.tensor_tensor(
                        wsl(tb), g2[:, :], g[:, :], ALU.mult))

                g2 = prog.sbuf("rg2", [128, NE], F32)
                mm_token_major_streamed(bb, prog, xn_stage, E, gw, NE, rp, rwriter)

        with prog.scope():
            ctx = ffn_make_ctx(prog, inter=2048)
            w13t = [prog.sbuf(f"ew13_{i}", [128, 4096], BF16) for i in range(8)]
            w2t = [prog.sbuf(f"ew2_{i}", [128, 1024], BF16) for i in range(16)]
            for e in range(NE + 1):
                routed = e < NE
                n1 = f"w1T_{e}" if routed else "sw1T"
                n3 = f"w3T_{e}" if routed else "sw3T"
                n2 = f"w2T_{e}" if routed else "sw2T"
                with prog.block() as bb:
                    for i in range(8):
                        bb.dma("gpsimd", w13t[i][:, 0:2048], w[n1][i * 128:(i + 1) * 128, :])
                        bb.dma("gpsimd", w13t[i][:, 2048:4096], w[n3][i * 128:(i + 1) * 128, :])
                    load_w_bf16(bb, prog, w[n2], name=f"lw2{e}", tiles=w2t)
                with prog.block() as bb:
                    ffn_expert_pass(bb, prog, ctx, xn_stage=xn_stage,
                                    w1_tiles=w13t, w2_tiles=w2t, swiglu=True,
                                    wcomb_col=(lambda tbg, e=e: wcomb[:, tbg * NE + e: tbg * NE + e + 1]) if routed else None,
                                    y_hbm=y_hbm, first_expert=(e == 0), inter=2048)

        add_residual_pass(prog, x_hbm, y_hbm, xout_hbm)


def ternarize(prog, w_hbm, out_tiles, name):
    K, M = w_hbm.shape
    kt = K // 128
    u = prog.uid()
    act, ve, pe = prog.chain["scalar"], prog.chain["vector"], prog.chain["tensor"]
    csts = prog.consts_tiles
    with prog.scope():
        wf = [prog.sbuf(f"tz_w{u}_{i}", [128, M], F32) for i in range(2)]
        scr = prog.sbuf(f"tz_s{u}", [128, M], F32)
        part = prog.sbuf(f"tz_p{u}", [128, kt], F32)
        thr = prog.sbuf(f"tz_t{u}", [128, 1], F32)
        with prog.block() as bb:
            hist = {}
            for i in range(kt):
                b = i % 2
                w = [(ve, hist[i - 2])] if i >= 2 else []
                ld = bb.dma("sync", wf[b][:, :], w_hbm[i * 128:(i + 1) * 128, :], waits=w)
                hist[i] = bb.op("vector", lambda e, b=b, i=i: e.reduce_sum(
                    part[:, i:i + 1], wf[b][:, :], axis=AX.X, apply_absolute_value=True),
                    waits=[ld])
            bp = PsumPool(prog, f"tzp{u}", [128, 64], 1, "vector")
            ps, idx = bp.acquire(bb)
            tgt = bp.produce(bb, lambda e, ps=ps: e.matmul(
                ps[0:1, :kt], csts["cst_ones128f"][:, :], part[:, :],
                start=True, stop=True), waits=[(ve, hist[kt - 1])])
            bp.consume(bb, idx, tgt, [
                lambda e, ps=ps: e.reduce_sum(thr[0:1, :], ps[0:1, :kt], axis=AX.X),
                lambda e: e.tensor_scalar_mul(thr[0:1, :], thr[0:1, :], 0.5 / (K * M)),
            ])
            ps2, idx2 = bp.acquire(bb)
            tgt2 = bp.produce(bb, lambda e, ps2=ps2: e.matmul(
                ps2[:, 0:1], csts["cst_onesrow"][:, :], thr[0:1, 0:1], start=True, stop=True),
                waits=[(ve, prog.chain["vector"].v)])
            bp.consume(bb, idx2, tgt2, [
                lambda e, ps2=ps2: e.tensor_copy(thr[:, :], ps2[:, 0:1])])
        with prog.block() as bb:
            hist = {}
            for i in range(kt):
                b = i % 2
                w = [(ve, hist[i - 2])] if i >= 2 else []
                ld = bb.dma("sync", wf[b][:, :], w_hbm[i * 128:(i + 1) * 128, :], waits=w)
                wv = [ld] + ([(ve, hist[i - 1])] if i >= 1 else [])
                a1 = bb.op("scalar", lambda e, b=b: e.activation(
                    scr[:, :], wf[b][:, :], AF.Abs), waits=wv)
                m1 = bb.op("vector", lambda e: e.tensor_scalar(
                    scr[:, :], scr[:, :], thr[:, 0:1], None, ALU.is_gt),
                    waits=[(act, a1)])
                a2 = bb.op("scalar", lambda e, b=b: e.sign(wf[b][:, :], wf[b][:, :]),
                           waits=[(ve, m1)])
                hist[i] = bb.op("vector", lambda e, b=b, i=i: e.tensor_tensor(
                    out_tiles[i][:, :], wf[b][:, :], scr[:, :], ALU.mult),
                    waits=[(act, a2)])


def sub_l2_ffn(prog: Prog, x_hbm, xout_hbm, w, uid):
    csts = prog.consts_tiles
    ident = csts["cst_ident"]
    DF = 4096
    xn_stage = prog.dram(f"stg_xn_{uid}", [E, T], BF16)
    h_stage = prog.dram(f"stg_h_{uid}", [DF, T], BF16)
    act, ve, pe = prog.chain["scalar"], prog.chain["vector"], prog.chain["tensor"]

    with prog.scope():
        with prog.scope():
            xnT = [prog.sbuf(f"xnT{i}", [128, T], BF16) for i in range(8)]
            with prog.block() as bb:
                info = ln_to_xnT(bb, prog, x_hbm, xnT, ident,
                                 second_norm=("rms_plain", 1e-8))
                for i in range(8):
                    bb.dma("sync", xn_stage[i * 128:(i + 1) * 128, :], xnT[i][:, :],
                           waits=[(ve, info["ve_done"]), (act, info["act_done"])] if i == 0 else ())

        ssq_row = prog.sbuf("ssq", [1, T], F32)
        rcol = prog.sbuf("rcol", [128, NTB], F32)
        with prog.scope():
            wtup = [prog.sbuf(f"wtu{i}", [128, DF], BF16) for i in range(8)]
            ternarize(prog, w["w_upT"], wtup, "up")
            with prog.block() as bb:
                hp = PsumPool(prog, "hp", [128, 512], 2, "scalar")
                sqp = PsumPool(prog, "sqp", [1, 512], 1, "vector")
                hsb = [prog.sbuf(f"hsb{i}", [128, 512], BF16) for i in range(4)]
                hsq = prog.sbuf("hsq", [128, 512], BF16)
                st_h = {}
                cnt = [0]
                ssq_ps = [None]

                def hwriter(bb, mb, c0, c1, ps, idx, tgt):
                    k = cnt[0] % 4
                    wv = [st_h[cnt[0] - 4]] if cnt[0] >= 4 else []
                    if mb == 0:
                        ssq_ps[0] = sqp.acquire(bb)
                    a1 = hp.consume(bb, idx, tgt, [
                        lambda e, ps=ps, k=k, cw=c1 - c0: e.activation(
                            hsb[k][:, :cw], ps[:, :cw], AF.Gelu)], waits=wv)
                    a2 = bb.op("scalar", lambda e, k=k, cw=c1 - c0: e.activation(
                        hsq[:, :cw], hsb[k][:, :cw], AF.Square))
                    pssq, issq = ssq_ps[0]
                    t2 = bb.op("tensor", lambda e, cw=c1 - c0, mb=mb, pssq=pssq: e.matmul(
                        pssq[0:1, :cw], csts["cst_ones128"][:, :], hsq[:, :cw],
                        start=(mb == 0), stop=(mb == 31)),
                        waits=[(act, a2)], mark=True)
                    st_h[cnt[0]] = bb.dma("sync", h_stage[mb * 128:(mb + 1) * 128, c0:c1],
                                          hsb[k][:, :cw if False else c1 - c0],
                                          waits=[(act, a1)])
                    cnt[0] += 1
                    if mb == 31:
                        sqp.consume(bb, issq, t2, [
                            lambda e, c0=c0, c1=c1, pssq=pssq: e.tensor_copy(
                                ssq_row[:, c0:c1], pssq[0:1, :c1 - c0])])

                mm_feature_major_streamed(bb, prog, wtup, xn_stage, E, T, hp,
                                          hwriter, m_blocks=32)
            with prog.block() as bb:
                r_row = prog.sbuf("rrow", [1, T], F32)
                bb.op("scalar", lambda e: e.activation(r_row[:, :], ssq_row[:, :], AF.Ln,
                                                       scale=1.0 / DF))
                a = bb.op("scalar", lambda e: e.activation(r_row[:, :], r_row[:, :], AF.Exp, scale=0.5))
                v1 = bb.op("vector", lambda e: e.tensor_scalar_add(r_row[:, :], r_row[:, :], 1e-8),
                           waits=[(act, a)])
                bb.op("scalar", lambda e: e.activation(r_row[:, :], r_row[:, :], AF.Ln),
                      waits=[(ve, v1)])
                a2 = bb.op("scalar", lambda e: e.activation(r_row[:, :], r_row[:, :], AF.Exp, scale=-1.0))
                tp = PsumPool(prog, "tpr", [128, NTB], 1, "vector")
                psr, idxr = tp.acquire(bb)
                for c in range(NTB):
                    tt = bb.op("tensor", lambda e, c=c, psr=psr: e.transpose(
                        psr[:, c:c + 1], r_row[0:1, c * 128:(c + 1) * 128],
                        csts["cst_identf"][0:1, 0:1]),
                        waits=[(act, a2)] if c == 0 else (), mark=(c == NTB - 1))
                tp.consume(bb, idxr, tt, [
                    lambda e, psr=psr: e.tensor_copy(rcol[:, :], psr[:, :])])

        with prog.scope():
            wtdn = [prog.sbuf(f"wtd{i}", [128, E], BF16) for i in range(32)]
            ternarize(prog, w["w_downT"], wtdn, "dn")
            with prog.block() as bb:
                yp = PsumPool(prog, "yp", [128, 512], 2, "vector")
                xold = [prog.sbuf(f"xo{i}", [128, E], F32) for i in range(2)]
                ytl = [prog.sbuf(f"ytl{i}", [128, 512], F32) for i in range(4)]
                ld_hist = {}
                st_hist = {}
                cnt = [0]

                def ywriter(bb, tb, c0, c1, ps, idx, tgt):
                    b = tb % 2
                    if c0 == 0:
                        wv = [(ve, ld_hist[tb - 2][1])] if tb >= 2 else []
                        ld = bb.dma("sync", xold[b][:, :], x_hbm[tb * 128:(tb + 1) * 128, :], waits=wv)
                        ld_hist[tb] = (ld, 0)
                    k = cnt[0] % 4
                    wv = [st_hist[cnt[0] - 4]] if cnt[0] >= 4 else []
                    wv.append(ld_hist[tb][0])
                    last = yp.consume(bb, idx, tgt, [
                        lambda e, ps=ps, k=k, c0=c0, c1=c1, b=b, tb=tb: e.scalar_tensor_tensor(
                            ytl[k][:, :c1 - c0], ps[:, :c1 - c0], rcol[:, tb:tb + 1],
                            xold[b][:, c0:c1], ALU.mult, ALU.add)], waits=wmax(wv))
                    ld_hist[tb] = (ld_hist[tb][0], last)
                    st_hist[cnt[0]] = bb.dma("sync", xout_hbm[tb * 128:(tb + 1) * 128, c0:c1],
                                             ytl[k][:, :c1 - c0], waits=[(ve, last)])
                    cnt[0] += 1

                mm_token_major_streamed(bb, prog, h_stage, DF, wtdn, E, yp, ywriter)


def sub_l1_attn_b(prog: Prog, x_hbm, xout_hbm, w, uid):
    _br = broadcast_row
    H, dh, dv = 16, 64, 64
    csts = prog.consts_tiles
    ident = csts["cst_ident"]
    q_stage = prog.dram(f"stg_q_{uid}", [H * dh, T], BF16)
    o_stage = prog.dram(f"stg_o_{uid}", [H * dv, T], BF16)
    xn_stage = prog.dram(f"stg_xn_{uid}", [E, T], BF16)
    rep_hbm = prog.dram(f"stg_rep_{uid}", [H, 128, 8448], F32)
    SW = 8448

    with prog.scope():
        borep = prog.sbuf("borep", [128, E], BF16)
        band_hbm = prog.dram(f"stg_band_{uid}", [16, SW], F32)
        with prog.scope():
            with prog.block() as bb:
                table = prog.sbuf("rtab", [32, 16], F32)
                oht = prog.sbuf("oht", [32, SW], F32)
                borow = prog.sbuf("borow", [1, E], F32)
                bb.dma("sync", table[:, :], w["rel_table"][:, :])
                bb.dma("sync", oht[:, :], w["OHT"][:, :])
                bb.dma("sync", borow[:, :], w["b_proj"][:, :])
            bandT = prog.sbuf("bandT", [16, SW], F32)
            with prog.block() as bb:
                bp = PsumPool(prog, "bp", [128, 512], 2, "scalar")
                for c0 in range(0, SW, 512):
                    c1 = min(c0 + 512, SW)
                    ps, idx = bp.acquire(bb)
                    tgt = bp.produce(bb, lambda e, c0=c0, c1=c1, ps=ps: e.matmul(
                        ps[0:16, :c1 - c0], table[:, :], oht[:, c0:c1], start=True, stop=True))
                    bp.consume(bb, idx, tgt, [
                        lambda e, c0=c0, c1=c1, ps=ps: e.activation(
                            bandT[:, c0:c1], ps[0:16, :c1 - c0], AF.Exp)])
                bp2 = PsumPool(prog, "bp2", [128, 512], 2, "vector")
                _br(bb, prog, borow, E, borep, bp2)
            with prog.block() as bb:
                bb.dma("sync", band_hbm[:, :], bandT[:, :],
                       waits=[(prog.chain["scalar"], prog.chain["scalar"].v)])
        with prog.scope():
            brow = prog.sbuf("brow", [1, SW], F32)
            brep = prog.sbuf("brep", [128, SW], F32)
            with prog.block() as bb:
                rp = PsumPool(prog, "rp", [128, 512], 2, "vector")
                st = None
                for h in range(H):
                    wv = [(prog.chain["tensor"], prog.chain["tensor"].v)]
                    ldb = bb.dma("sync", brow[:, :], band_hbm[h:h + 1, :], waits=wv)
                    bb.wait("tensor", *ldb)
                    if st is not None:
                        bb.wait("vector", *st)
                    _br(bb, prog, brow, SW, brep, rp)
                    st = bb.dma("sync", rep_hbm[h, :, :], brep[:, :],
                                waits=[(prog.chain["vector"], prog.chain["vector"].v)])

        with prog.scope():
            xnT = [prog.sbuf(f"xnT{i}", [128, T], BF16) for i in range(8)]
            with prog.block() as bb:
                info = ln_to_xnT(bb, prog, x_hbm, xnT, ident)
                for i in range(8):
                    bb.dma("sync", xn_stage[i * 128:(i + 1) * 128, :], xnT[i][:, :],
                           waits=[(prog.chain["vector"], info["ve_done"]),
                                  (prog.chain["scalar"], info["act_done"])] if i == 0 else ())

        kT = [prog.sbuf(f"kT{i}", [128, T], BF16) for i in range(8)]
        with prog.scope():
            with prog.scope():
                with prog.block() as bb:
                    wkT, _ = load_w_bf16(bb, prog, w["w_kT"], name="wkT")
                    wqT, _ = load_w_bf16(bb, prog, w["w_qT"], name="wqT")
                with prog.block() as bb:
                    pk = PsumPool(prog, "pk", [128, 512], 2, "vector")
                    mm_feature_major_streamed(bb, prog, wkT, xn_stage, E, T, pk,
                                              copy_fm_writer(prog, pk, kT))
                    pq = PsumPool(prog, "pq", [128, 512], 2, "vector")
                    mm_feature_major_streamed(bb, prog, wqT, xn_stage, E, T, pq,
                                              stage_fm_chunk_writer(prog, pq, q_stage))
            vaug = [prog.sbuf(f"va{i}", [128, H * (dv + 1)], BF16) for i in range(NTB)]
            with prog.scope():
                with prog.block() as bb:
                    wvT, _ = load_w_bf16(bb, prog, w["w_vT"], name="wvT")
                with prog.block() as bb:
                    pv = PsumPool(prog, "pv", [128, 512], 2, "vector")
                    vaug_memset_ones(bb, prog, vaug, dv)
                    mm_token_major_streamed(bb, prog, xn_stage, E, wvT, E, pv,
                                            vaug_tm_writer(prog, pv, vaug, dv))

            wrep = prog.sbuf("wrep", [128, 8192], BF16)
            wrep_ld = [None]

            def head_pre(bb, h):
                wrep_ld[0] = bb.dma("gpsimd", wrep[:, :],
                                    bass.AP(rep_hbm, h * 128 * SW + 127,
                                            [[SW - 1, 128], [1, 8192]]))
                bb.wait("vector", *wrep_ld[0])

            def wmult_ap(h, kb, q0, qcw):
                off = 4095 + q0 - kb * 128
                return wrep[:, off:off + qcw]


            attention_heads(prog, q_stage=q_stage,
                            kT_ap=lambda h: kT[h // 2][(h % 2) * 64:(h % 2) * 64 + dh, :],
                            vaug=vaug, oT_stage=o_stage, H=H, dh=dh, dv=dv,
                            score_scale=dh ** -0.5, cconst=20.0,
                            wmult_ap=wmult_ap, head_pre=head_pre)

        with prog.block() as bb:
            wpT, _ = load_w_bf16(bb, prog, w["w_projT"], name="wpT")
        out_proj_residual(prog, act_stage=o_stage, n_act=H * dv, wo_tiles=wpT,
                          x_hbm=x_hbm, xout_hbm=xout_hbm, row_bias_rep=borep)


def sub_l1_ffn(prog: Prog, x_hbm, xout_hbm, w, uid):
    H, dh, dv = 16, 64, 64
    csts = prog.consts_tiles
    ident = csts["cst_ident"]
    q_stage = prog.dram(f"stg_q_{uid}", [H * dh, T], BF16)
    o_stage = prog.dram(f"stg_o_{uid}", [H * dv, T], BF16)
    xn_stage = prog.dram(f"stg_xn_{uid}", [E, T], BF16)
    act, ve, pe = prog.chain["scalar"], prog.chain["vector"], prog.chain["tensor"]

    with prog.scope():
        gcol = prog.sbuf("gcol", [128, 1], F32)
        pooledT = prog.sbuf("pooledT", [128, 8], F32)
        with prog.scope():
            xnT = [prog.sbuf(f"xnT{i}", [128, T], BF16) for i in range(8)]
            with prog.block() as bb:
                info = ln_to_xnT(bb, prog, x_hbm, xnT, ident)
                for i in range(8):
                    bb.dma("sync", xn_stage[i * 128:(i + 1) * 128, :], xnT[i][:, :],
                           waits=[(ve, info["ve_done"]), (act, info["act_done"])] if i == 0 else ())
                for i in range(8):
                    bb.op("vector", lambda e, i=i: e.reduce_sum(
                        pooledT[:, i:i + 1], xnT[i][:, :], axis=AX.X),
                        waits=[(act, info["act_done"])] if i == 0 else ())
                bb.op("vector", lambda e: e.tensor_scalar_mul(pooledT[:, :], pooledT[:, :], 1.0 / T))

        with prog.scope():
            with prog.block() as bb:
                gw1, _ = load_w_bf16(bb, prog, w["gw1T"], name="gw1")
                gw2, _ = load_w_bf16(bb, prog, w["gw2T"], name="gw2")
                gb1 = prog.sbuf("gb1", [1, 256], F32)
                gb2 = prog.sbuf("gb2", [1, 8], F32)
                exp8 = prog.sbuf("exp8", [8, 128], F32)
                bb.dma("sync", gb1[:, :], w["gb1"][:, :])
                bb.dma("sync", gb2[:, :], w["gb2"][:, :])
                bb.dma("sync", exp8[:, :], w["exp8"][:, :])
            with prog.block() as bb:
                pb = prog.sbuf("poolb", [128, 8], BF16)
                c1v = bb.op("vector", lambda e: e.tensor_copy(pb[:, :], pooledT[:, :]))
                gp = PP(prog, "gp", [128, 512], 2, "vector")
                hrow = prog.sbuf("hrow", [1, 256], F32)
                hT = prog.sbuf("hT", [128, 2], BF16)
                st8 = prog.sbuf("st8", [1, 8], F32)
                grow = prog.sbuf("grow", [1, 8], F32)
                ps, idx = gp.acquire(bb)
                for k in range(8):
                    fn = lambda e, k=k, ps=ps: e.matmul(
                        ps[0:1, 0:256], pb[:, k:k + 1], gw1[k][:, :],
                        start=(k == 0), stop=(k == 7))
                    if k == 7:
                        tgt = gp.produce(bb, fn)
                    else:
                        bb.op("tensor", fn, waits=[(ve, c1v)] if k == 0 else (), mark=False)
                gp.consume(bb, idx, tgt, [
                    lambda e, ps=ps: e.tensor_tensor(hrow[:, :], ps[0:1, 0:256], gb1[:, :], ALU.add),
                    lambda e: e.tensor_relu(hrow[:, :], hrow[:, :]),
                ])
                hrb = prog.sbuf("hrb", [1, 256], BF16)
                hv = bb.op("vector", lambda e: e.tensor_copy(hrb[:, :], hrow[:, :]))
                tpp = PP(prog, "tpp", [128, 2], 2, "vector", dtype=BF16)
                tppf = PP(prog, "tppf", [128, 2], 1, "vector")
                for c in range(2):
                    ps2, idx2 = tpp.acquire(bb)
                    tt = bb.op("tensor", lambda e, c=c, ps2=ps2: e.transpose(
                        ps2[:, 0:1], hrb[0:1, c * 128:(c + 1) * 128], ident[0:1, 0:1]),
                        waits=[(ve, hv)] if c == 0 else (), mark=True)
                    tpp.consume(bb, idx2, tt, [
                        lambda e, ps2=ps2, c=c: e.tensor_copy(hT[:, c:c + 1], ps2[:, 0:1])])
                ps3, idx3 = gp.acquire(bb)
                for k in range(2):
                    fn = lambda e, k=k, ps3=ps3: e.matmul(
                        ps3[0:1, 0:8], hT[:, k:k + 1], gw2[k][:, :],
                        start=(k == 0), stop=(k == 1))
                    if k == 1:
                        tgt3 = gp.produce(bb, fn, waits=[(ve, prog.chain["vector"].v)])
                    else:
                        bb.op("tensor", fn, waits=[(ve, prog.chain["vector"].v)], mark=False)
                gp.consume(bb, idx3, tgt3, [
                    lambda e, ps3=ps3: e.tensor_tensor(grow[:, :], ps3[0:1, 0:8], gb2[:, :], ALU.add),
                    lambda e: e.reduce_max(st8[:, 0:1], grow[:, :], axis=AX.X),
                    lambda e: e.tensor_scalar(st8[:, 1:2], grow[:, :], st8[:, 0:1], None, ALU.is_ge) if False else e.tensor_scalar_mul(st8[:, 1:2], st8[:, 0:1], -1.0),
                ])
                m8 = prog.sbuf("m8", [1, 8], F32)
                m8b = prog.sbuf("m8b", [1, 8], F32)
                bb.op("vector", lambda e: e.tensor_scalar(
                    m8[:, :], grow[:, :], st8[:, 0:1], None, ALU.is_ge))
                bb.op("vector", lambda e: e.scalar_tensor_tensor(
                    m8b[:, :], m8[:, :], -1e30, grow[:, :], ALU.mult, ALU.add))
                bb.op("vector", lambda e: e.reduce_max(st8[:, 2:3], m8b[:, :], axis=AX.X))
                a1 = bb.op("scalar", lambda e: e.activation(
                    st8[:, 3:4], st8[:, 2:3], AF.Exp, bias=st8[:, 1:2]),
                    waits=[(ve, prog.chain["vector"].v)])
                bb.op("vector", lambda e: e.tensor_scalar_add(st8[:, 4:5], st8[:, 3:4], 1.0),
                      waits=[(act, a1)])
                bb.op("vector", lambda e: e.reciprocal(st8[:, 4:5], st8[:, 4:5]))
                bb.op("vector", lambda e: e.tensor_tensor(st8[:, 5:6], st8[:, 3:4], st8[:, 4:5], ALU.mult))
                bb.op("vector", lambda e: e.tensor_scalar(
                    m8b[:, :], m8b[:, :], st8[:, 2:3], None, ALU.is_ge))
                bb.op("vector", lambda e: e.tensor_scalar_mul(m8[:, :], m8[:, :], st8[:, 4:5]))
                bb.op("vector", lambda e: e.tensor_scalar(
                    m8b[:, :], m8b[:, :], st8[:, 5:6], None, ALU.mult))
                gv = bb.op("vector", lambda e: e.tensor_tensor(grow[:, :], m8[:, :], m8b[:, :], ALU.add))
                g8 = prog.sbuf("g8", [8, 1], F32)
                ps4, idx4 = tppf.acquire(bb)
                t4 = bb.op("tensor", lambda e, ps4=ps4: e.transpose(
                    ps4[0:8, 0:1], grow[0:1, 0:8], csts["cst_identf"][0:1, 0:1]),
                    waits=[(ve, gv)], mark=True)
                tppf.consume(bb, idx4, t4, [
                    lambda e, ps4=ps4: e.tensor_copy(g8[:, :], ps4[0:8, 0:1])])
                ps5, idx5 = gp.acquire(bb)
                t5 = gp.produce(bb, lambda e, ps5=ps5: e.matmul(
                    ps5[:, 0:1], exp8[:, :], g8[:, :], start=True, stop=True),
                    waits=[(ve, prog.chain["vector"].v)])
                gp.consume(bb, idx5, t5, [
                    lambda e, ps5=ps5: e.tensor_copy(gcol[:, :], ps5[:, 0:1])])

        kT = [prog.sbuf(f"kT{i}", [128, T], BF16) for i in range(8)]

        def svd_fm(bb, prog, p, writer, pool):
            wm = wdict[p + "Wm"]
            vall = wdict[p + "Vall"]
            urt = wdict[p + "UrT"]
            scol = wdict[p + "S"]
            xvs = prog.sbuf(f"xvs{p}", [128, 512], BF16)
            xvp = PP(prog, f"xvp{p}", [128, 512], 1, "vector")
            xv_tgt = [0]

            def pre_chunk(bb, ci, c0, c1, xbt, lds):
                ps, idx = xvp.acquire(bb)
                for k in range(8):
                    fn = lambda e, k=k, ps=ps, cw=c1 - c0: e.matmul(
                        ps[:, :cw], vall[k][:, :], xbt[k][:, :cw],
                        start=(k == 0), stop=(k == 7))
                    if k == 7:
                        tgt = xvp.produce(bb, fn)
                    else:
                        bb.op("tensor", fn, waits=wmax(lds) if k == 0 else (), mark=False)
                xv_tgt[0] = xvp.consume(bb, idx, tgt, [
                    lambda e, ps=ps, cw=c1 - c0: e.tensor_scalar(
                        xvs[:, :cw], ps[:, :cw], scol[:, 0:1], gcol[:, 0:1], ALU.mult, ALU.mult)],
                    waits=[(prog.chain["tensor"], prog.chain["tensor"].v)])

            mm_feature_major_streamed(
                bb, prog, wm, xn_stage, E, T, pool, writer, chunk=512,
                extra_rhs=lambda c0, c1: [(xvs[:, :c1 - c0], (ve, xv_tgt[0]))],
                extra_w=[urt[0]], pre_chunk=pre_chunk)

        wdict = {}
        with prog.scope():
            with prog.block() as bb:
                for p in ("q", "k"):
                    wdict[p + "Wm"], _ = load_w_bf16(bb, prog, w[p + "WmT"], name=p + "wm")
                    wdict[p + "Vall"], _ = load_w_bf16(bb, prog, w[p + "VallT"], name=p + "va")
                    wdict[p + "UrT"], _ = load_w_bf16(bb, prog, w[p + "UrT"], name=p + "ur")
                    wdict[p + "S"] = prog.sbuf(p + "S", [128, 1], F32)
                    bb.dma("sync", wdict[p + "S"][:, :], w[p + "S"][:, :])
                    wdict[p + "b"] = prog.sbuf(p + "b", [128, 8], F32)
                    bb.dma("sync", wdict[p + "b"][:, :], w[p + "b"][:, :])
            with prog.block() as bb:
                pk = PP(prog, "pk", [128, 512], 2, "vector")
                svd_fm(bb, prog, "k", bias_fm_writer(prog, pk, kT, wdict["kb"]), pk)
                pq = PP(prog, "pq", [128, 512], 2, "vector")
                svd_fm(bb, prog, "q", stage_fm_chunk_writer(prog, pq, q_stage,
                                                            bias_tile=wdict["qb"]), pq)

        vaug = [prog.sbuf(f"va{i}", [128, H * (dv + 1)], BF16) for i in range(NTB)]
        with prog.scope():
            with prog.block() as bb:
                for p in ("v",):
                    wdict[p + "Wm"], _ = load_w_bf16(bb, prog, w[p + "WmT"], name=p + "wm")
                    wdict[p + "Vall"], _ = load_w_bf16(bb, prog, w[p + "VallT"], name=p + "va")
                    wdict[p + "UrT"], _ = load_w_bf16(bb, prog, w[p + "UrT"], name=p + "ur")
                    wdict[p + "S"] = prog.sbuf(p + "S", [128, 1], F32)
                    bb.dma("sync", wdict[p + "S"][:, :], w[p + "S"][:, :])
                vbrow = prog.sbuf("vbrow", [1, E], F32)
                bb.dma("sync", vbrow[:, :], w["vbrow"][:, :])
            vbrep = prog.sbuf("vbrep", [128, E], F32)
            with prog.block() as bb:
                bp = PP(prog, "bp", [128, 512], 2, "vector")
                broadcast_row(bb, prog, vbrow, E, vbrep, bp)
            with prog.block() as bb:
                pv = PP(prog, "pv", [128, 512], 2, "vector")
                vaug_memset_ones(bb, prog, vaug, dv)
                xvsv = prog.sbuf("xvsv", [128, 128], BF16)
                xvp = PP(prog, "xvpv", [128, 128], 1, "vector")
                xw = [0]

                def extra_act(tb):
                    return [(xvsv[:, :], (ve, xw[0]))]

                def pre_tb(bb, tb, ab_tile, ld):
                    ps, idx = xvp.acquire(bb)
                    for k in range(8):
                        fn = lambda e, k=k, ps=ps: e.matmul(
                            ps[:, :], wdict["vVall"][k][:, :], ab_tile[:, k * 128:(k + 1) * 128],
                            start=(k == 0), stop=(k == 7))
                        if k == 7:
                            tgt = xvp.produce(bb, fn)
                        else:
                            bb.op("tensor", fn, waits=[ld] if k == 0 else (), mark=False)
                    xw[0] = xvp.consume(bb, idx, tgt, [
                        lambda e, ps=ps: e.tensor_scalar(
                            xvsv[:, :], ps[:, :], wdict["vS"][:, 0:1], gcol[:, 0:1],
                            ALU.mult, ALU.mult)],
                        waits=[(prog.chain["tensor"], prog.chain["tensor"].v)])

                mm_token_major_streamed(bb, prog, xn_stage, E, wdict["vWm"], E, pv,
                                        vaug_tm_writer(prog, pv, vaug, dv, bias_rep=vbrep),
                                        extra_act=extra_act, extra_w=[wdict["vUrT"][0]],
                                        pre_tb=pre_tb)

        attention_heads(prog, q_stage=q_stage,
                        kT_ap=lambda h: kT[h // 2][(h % 2) * 64:(h % 2) * 64 + dh, :],
                        vaug=vaug, oT_stage=o_stage, H=H, dh=dh, dv=dv,
                        score_scale=0.125, cconst=20.0)

        with prog.scope():
            with prog.block() as bb:
                for p in ("o",):
                    wdict[p + "Wm"], _ = load_w_bf16(bb, prog, w[p + "WmT"], name=p + "wm")
                    wdict[p + "Vall"], _ = load_w_bf16(bb, prog, w[p + "VallT"], name=p + "va")
                    wdict[p + "UrT"], _ = load_w_bf16(bb, prog, w[p + "UrT"], name=p + "ur")
                    wdict[p + "S"] = prog.sbuf(p + "S", [128, 1], F32)
                    bb.dma("sync", wdict[p + "S"][:, :], w[p + "S"][:, :])
                obrow = prog.sbuf("obrow", [1, E], F32)
                bb.dma("sync", obrow[:, :], w["obrow"][:, :])
            obrep = prog.sbuf("obrep", [128, E], F32)
            with prog.block() as bb:
                bp2 = PP(prog, "bp2", [128, 512], 2, "vector")
                broadcast_row(bb, prog, obrow, E, obrep, bp2)

            xvso = prog.sbuf("xvso", [128, 512], BF16)
            xvpo = PP(prog, "xvpo", [128, 512], 1, "vector")
            xwo = [0]

            def blk_pre(bb, blk0, a_ch, ldw):
                ps, idx = xvpo.acquire(bb)
                for k in range(8):
                    fn = lambda e, k=k, ps=ps: e.matmul(
                        ps[:, :], a_ch[k][:, :], wdict["oVallT_rhs"][k][:, :] if False else None,
                        start=(k == 0), stop=(k == 7))
                for k in range(8):
                    fn = lambda e, k=k, ps=ps: e.matmul(
                        ps[:, :], wdict["oVall"][k][:, :], a_ch[k][:, :],
                        start=(k == 0), stop=(k == 7))
                    if k == 7:
                        tgt = xvpo.produce(bb, fn)
                    else:
                        bb.op("tensor", fn, waits=ldw if k == 0 else (), mark=False)
                xwo[0] = xvpo.consume(bb, idx, tgt, [
                    lambda e, ps=ps: e.tensor_scalar(
                        xvso[:, :], ps[:, :], wdict["oS"][:, 0:1], gcol[:, 0:1],
                        ALU.mult, ALU.mult)],
                    waits=[(prog.chain["tensor"], prog.chain["tensor"].v)])

            def extra_pairs(tb_loc):
                return [(xvso[:, tb_loc * 128:(tb_loc + 1) * 128],
                         wdict["oUrT"][0], (ve, xwo[0]))]

            out_proj_residual(prog, act_stage=o_stage, n_act=H * dv,
                              wo_tiles=wdict["oWm"], x_hbm=x_hbm, xout_hbm=xout_hbm,
                              row_bias_rep=obrep, blk_pre=blk_pre, extra_pairs=extra_pairs)


def sub_l2_attn_a(prog: Prog, x_hbm, xout_hbm, w, uid):
    csts = prog.consts_tiles
    ident = csts["cst_ident"]
    identf = csts["cst_identf"]
    NEH = 16
    s_qk = float(1.0 / np.sqrt(E))
    q_tok = prog.dram(f"stg_q_{uid}", [T, E], BF16)
    k_tok = prog.dram(f"stg_k_{uid}", [T, E], BF16)
    xn_stage = prog.dram(f"stg_xn_{uid}", [E, T], BF16)
    rv_stage = prog.dram(f"stg_rv_{uid}", [E, T], BF16)
    R_hbm = prog.dram(f"stg_R_{uid}", [16, T], F32)
    act, ve, pe = prog.chain["scalar"], prog.chain["vector"], prog.chain["tensor"]

    with prog.scope():
        with prog.block() as bb:
            bqrow = prog.sbuf("bqrow", [1, E], F32)
            bkrow = prog.sbuf("bkrow", [1, E], F32)
            borow = prog.sbuf("borow", [1, E], F32)
            bvcol = prog.sbuf("bvcol", [128, 8], F32)
            bb.dma("sync", bqrow[:, :], w["bq_row"][:, :])
            bb.dma("sync", bkrow[:, :], w["bk_row"][:, :])
            bb.dma("sync", borow[:, :], w["bo_row"][:, :])
            bb.dma("sync", bvcol[:, :], w["bv_col"][:, :])
        borep = prog.sbuf("borep", [128, E], BF16)
        with prog.block() as bb:
            bp = PP(prog, "brc", [128, 512], 1, "vector")
            broadcast_row(bb, prog, borow, E, borep, bp)

        with prog.scope():
            xnT = [prog.sbuf(f"xnT{i}", [128, T], BF16) for i in range(8)]
            bqrep = prog.sbuf("bqrep", [128, E], F32)
            bkrep = prog.sbuf("bkrep", [128, E], F32)
            with prog.block() as bb:
                bp2 = PP(prog, "brc2", [128, 512], 1, "vector")
                broadcast_row(bb, prog, bqrow, E, bqrep, bp2)
                broadcast_row(bb, prog, bkrow, E, bkrep, bp2)
                bb.op("vector", lambda e: e.tensor_scalar_mul(bqrep[:, :], bqrep[:, :], s_qk))
                bb.op("vector", lambda e: e.tensor_scalar_mul(bkrep[:, :], bkrep[:, :], s_qk))
            with prog.block() as bb:
                wq, _ = load_w_bf16(bb, prog, w["W_q"], name="wq")
                wk, _ = load_w_bf16(bb, prog, w["W_k"], name="wk")
                ln_to_xnT(bb, prog, x_hbm, xnT, ident)
            with prog.block() as bb:
                pq = PP(prog, "pq", [128, 512], 2, "vector")
                qt = [prog.sbuf(f"qt{i}", [128, 512], BF16) for i in range(4)]
                cnt = [0]
                sth = {}

                def tok_stage_writer(stage, brep):
                    def wr(bb, tb, c0, c1, ps, idx, tgt):
                        k = cnt[0] % 4
                        wv = [sth[cnt[0] - 4]] if cnt[0] >= 4 else []
                        last = pq.consume(bb, idx, tgt, [
                            lambda e, ps=ps, k=k, c0=c0, c1=c1, brep=brep: e.scalar_tensor_tensor(
                                qt[k][:, :c1 - c0], ps[:, :c1 - c0], s_qk, brep[:, c0:c1],
                                ALU.mult, ALU.add)], waits=wv)
                        sth[cnt[0]] = bb.dma("sync", stage[tb * 128:(tb + 1) * 128, c0:c1],
                                             qt[k][:, :c1 - c0], waits=[(ve, last)])
                        cnt[0] += 1
                    return wr
                mm_token_major(bb, prog, xnT, wq, E, pq, tok_stage_writer(q_tok, bqrep))
                mm_token_major(bb, prog, xnT, wk, E, pq, tok_stage_writer(k_tok, bkrep))
                for i in range(8):
                    bb.dma("sync", xn_stage[i * 128:(i + 1) * 128, :], xnT[i][:, :])

        vT = [prog.sbuf(f"vT{i}", [128, T], BF16) for i in range(8)]
        with prog.scope():
            with prog.block() as bb:
                wv_, _ = load_w_bf16(bb, prog, w["W_v"], name="wv")
            with prog.block() as bb:
                pv = PP(prog, "pv", [128, 512], 2, "vector")
                mm_feature_major_streamed(bb, prog, wv_, xn_stage, E, T, pv,
                                          bias_fm_writer(prog, pv, vT, bvcol))

        q_flat = q_tok.rearrange("t (f g) -> (t f) g", g=64)
        k_flat = k_tok.rearrange("t (f g) -> (t f) g", g=64)
        with prog.scope():
            qe = prog.sbuf("qe", [128, 32 * 64], BF16)
            ke = prog.sbuf("ke", [128, 32 * 64], BF16)
            qeT = prog.sbuf("qeT", [64, T], BF16)
            keT = prog.sbuf("keT", [64, T], BF16)
            prodqk = prog.sbuf("prodqk", [64, T], BF16)
            diag_row = prog.sbuf("diagr", [1, T], F32)
            r_row = prog.sbuf("rrow", [1, T], F32)
            colp = prog.sbuf("colp", [128, 8], F32)
            cs_col = prog.sbuf("cscol", [128, 32], F32)
            dg_col = prog.sbuf("dgcol", [128, 32], F32)
            scr = prog.sbuf("scr", [128, 512], F32)
            rg = prog.sbuf("rg", [1, 256], F32)
            tp = PP(prog, "tpe", [64, 512], 2, "vector", dtype=BF16)
            dp = PP(prog, "dpe", [1, 512], 1, "vector")
            sp = PP(prog, "spe", [128, 512], 2, "scalar")
            tcp = PP(prog, "tcp", [128, 32], 1, "vector")
            trp = PP(prog, "trp", [1, 512], 1, "vector")
            for e_ in range(NEH):
                with prog.block() as bb:
                    ld1 = bb.dma("sync", qe[:, :].rearrange("p (b d) -> p b d", d=64),
                                 q_flat.rearrange("(e b p) d -> e p b d", e=16, p=128)[e_])
                    ld2 = bb.dma("sync", ke[:, :].rearrange("p (b d) -> p b d", d=64),
                                 k_flat.rearrange("(e b p) d -> e p b d", e=16, p=128)[e_])
                    for which, (tile, dst, ldw) in enumerate(
                            ((qe, qeT, ld1), (ke, keT, ld2))):
                        for cc in range(8):
                            ps, idx = tp.acquire(bb)
                            for j in range(4):
                                blk = cc * 4 + j
                                tt = bb.op("tensor", lambda e, tile=tile, blk=blk, j=j, ps=ps:
                                           e.transpose(ps[:, j * 128:(j + 1) * 128],
                                                       tile[:, blk * 64:(blk + 1) * 64],
                                                       ident[:, :]),
                                           waits=[ldw] if (cc == 0 and j == 0) else (),
                                           mark=(j == 3))
                            tp.consume(bb, idx, tt, [
                                lambda e, dst=dst, cc=cc, ps=ps: e.tensor_copy(
                                    dst[:, cc * 512:(cc + 1) * 512], ps[:, :])])
                    dv = bb.op("vector", lambda e: e.tensor_tensor(
                        prodqk[:, :], qeT[:, :], keT[:, :], ALU.mult))
                    for cc in range(8):
                        ps, idx = dp.acquire(bb)
                        tgt = dp.produce(bb, lambda e, cc=cc, ps=ps: e.matmul(
                            ps[0:1, :], csts["cst_ones128"][0:64, :],
                            prodqk[:, cc * 512:(cc + 1) * 512], start=True, stop=True),
                            waits=[(ve, dv)] if cc == 0 else ())
                        dp.consume(bb, idx, tgt, [
                            lambda e, cc=cc, ps=ps: e.tensor_copy(
                                diag_row[:, cc * 512:(cc + 1) * 512], ps[0:1, :])])
                    rs_prev = 0
                    for bblk in range(32):
                        for ac in range(8):
                            ps, idx = sp.acquire(bb)
                            tgt = sp.produce(bb, lambda e, bblk=bblk, ac=ac, ps=ps: e.matmul(
                                ps[:, :], keT[:, bblk * 128:(bblk + 1) * 128],
                                qeT[:, ac * 512:(ac + 1) * 512], start=True, stop=True))
                            sp.consume(bb, idx, tgt, [
                                lambda e, ac=ac, ps=ps: e.activation(
                                    scr[:, :], ps[:, :], AF.Exp, accum_out=colp[:, ac:ac + 1])],
                                waits=[(ve, rs_prev)] if (ac == 0 and rs_prev) else ())
                        rs_prev = bb.op("vector", lambda e, bblk=bblk: e.reduce_sum(
                            cs_col[:, bblk:bblk + 1], colp[:, :], axis=AX.X),
                            waits=[(act, prog.chain["scalar"].v)])
                    psd, idxd = tcp.acquire(bb)
                    for cc in range(32):
                        td = bb.op("tensor", lambda e, cc=cc, psd=psd: e.transpose(
                            psd[:, cc:cc + 1], diag_row[0:1, cc * 128:(cc + 1) * 128],
                            identf[0:1, 0:1]),
                            waits=[(ve, prog.chain["vector"].v)] if cc == 0 else (),
                            mark=(cc == 31))
                    tcp.consume(bb, idxd, td, [
                        lambda e, psd=psd: e.tensor_copy(dg_col[:, :], psd[:, :])])
                    ae = bb.op("scalar", lambda e: e.activation(dg_col[:, :], dg_col[:, :], AF.Exp),
                               waits=[(ve, prog.chain["vector"].v)])
                    rv1 = bb.op("vector", lambda e: e.reciprocal(cs_col[:, :], cs_col[:, :]),
                                waits=[(act, ae)])
                    rv2 = bb.op("vector", lambda e: e.tensor_tensor(
                        dg_col[:, :], dg_col[:, :], cs_col[:, :], ALU.mult))
                    for cc in range(8):
                        ps, idx = trp.acquire(bb)
                        for j in range(4):
                            blk = cc * 4 + j
                            tr = bb.op("tensor", lambda e, blk=blk, j=j, ps=ps: e.transpose(
                                ps[0:1, j * 128:(j + 1) * 128], dg_col[:, blk:blk + 1],
                                identf[:, :]),
                                waits=[(ve, rv2)] if (cc == 0 and j == 0) else (),
                                mark=(j == 3))
                        trp.consume(bb, idx, tr, [
                            lambda e, cc=cc, ps=ps: e.tensor_copy(
                                r_row[:, cc * 512:(cc + 1) * 512], ps[0:1, :])])
                    st_rg = None
                    for g in range(16):
                        cv = bb.op("vector", lambda e, g=g: e.tensor_copy(
                            rg[:, :], r_row.rearrange("a (u g) -> a g u", g=16)[:, g, :]),
                            waits=[st_rg] if st_rg else ())
                        st_rg = bb.dma("sync", R_hbm[g:g + 1, e_ * 256:(e_ + 1) * 256], rg[:, :],
                                       waits=[(ve, cv)])

        with prog.scope():
            r2a = prog.sbuf("r2a", [1, T], F32)
            r2b = prog.sbuf("r2b", [1, T], F32)
            with prog.block() as bb:
                rp = PP(prog, "rvp", [128, 512], 2, "vector")
                for kidx in range(8):
                    l1 = bb.dma("sync", r2a[:, :], R_hbm[2 * kidx:2 * kidx + 1, :],
                                waits=[(pe, prog.chain["tensor"].v)])
                    l2 = bb.dma("sync", r2b[:, :], R_hbm[2 * kidx + 1:2 * kidx + 2, :])
                    for c0 in range(0, T, 512):
                        ps, idx = rp.acquire(bb)
                        bb.op("tensor", lambda e, c0=c0, ps=ps: e.matmul(
                            ps[:, :], csts["cst_maskA"][:, :], r2a[:, c0:c0 + 512],
                            start=True, stop=False),
                            waits=wmax([l1, l2]) if c0 == 0 else (), mark=False)
                        tgt = rp.produce(bb, lambda e, c0=c0, ps=ps: e.matmul(
                            ps[:, :], csts["cst_maskB"][:, :], r2b[:, c0:c0 + 512],
                            start=False, stop=True))
                        rp.consume(bb, idx, tgt, [
                            lambda e, kidx=kidx, c0=c0, ps=ps: e.tensor_tensor(
                                vT[kidx][:, c0:c0 + 512], vT[kidx][:, c0:c0 + 512],
                                ps[:, :], ALU.mult)])
                    bb.dma("sync", rv_stage[kidx * 128:(kidx + 1) * 128, :], vT[kidx][:, :],
                           waits=[(ve, prog.chain["vector"].v)])

        with prog.block() as bb:
            wo, _ = load_w_bf16(bb, prog, w["W_o"], name="wo")
        out_proj_residual(prog, act_stage=rv_stage, n_act=E, wo_tiles=wo,
                          x_hbm=x_hbm, xout_hbm=xout_hbm, row_bias_rep=borep)


def sub_l0_attn_b(prog: Prog, x_hbm, xout_hbm, w, uid):
    csts = prog.consts_tiles
    ident = csts["cst_ident"]
    NEH, dh, dv, LP = 8, 64, 128, 4096
    SC = 0.125
    CC = 4.0
    q_tok = prog.dram(f"stg_q_{uid}", [T, 512], BF16)
    k_tok = prog.dram(f"stg_k_{uid}", [T, 512], BF16)
    v_tok = prog.dram(f"stg_v_{uid}", [T, E], BF16)
    e_stage = prog.dram(f"stg_e_{uid}", [LP, LP], BF16)
    o_tok = prog.dram(f"stg_o_{uid}", [T, E], BF16)
    act, ve, pe = prog.chain["scalar"], prog.chain["vector"], prog.chain["tensor"]

    with prog.scope():
        with prog.scope():
            xnT = [prog.sbuf(f"xnT{i}", [128, T], BF16) for i in range(8)]
            with prog.block() as bb:
                mq, _ = load_w_bf16(bb, prog, w["Mq"], name="mq")
                mk, _ = load_w_bf16(bb, prog, w["Mk"], name="mk")
                mv, _ = load_w_bf16(bb, prog, w["Mv"], name="mv")
                ln_to_xnT(bb, prog, x_hbm, xnT, ident)
            with prog.block() as bb:
                pq = PP(prog, "pq", [128, 512], 2, "vector")
                qt = [prog.sbuf(f"qt{i}", [128, 512], BF16) for i in range(4)]
                cnt = [0]
                sth = {}

                def stg_writer(stage):
                    def wr(bb, tb, c0, c1, ps, idx, tgt):
                        k = cnt[0] % 4
                        wv = [sth[cnt[0] - 4]] if cnt[0] >= 4 else []
                        last = pq.consume(bb, idx, tgt, [
                            lambda e, ps=ps, k=k, c0=c0, c1=c1: e.tensor_copy(
                                qt[k][:, :c1 - c0], ps[:, :c1 - c0])], waits=wv)
                        sth[cnt[0]] = bb.dma("sync", stage[tb * 128:(tb + 1) * 128, c0:c1],
                                             qt[k][:, :c1 - c0], waits=[(ve, last)])
                        cnt[0] += 1
                    return wr
                mm_token_major(bb, prog, xnT, mq, 512, pq, stg_writer(q_tok))
                mm_token_major(bb, prog, xnT, mk, 512, pq, stg_writer(k_tok))
                mm_token_major(bb, prog, xnT, mv, E, pq, stg_writer(v_tok))

        q_flat = q_tok.rearrange("t (a g) -> (t a) g", g=64)
        k_flat = k_tok.rearrange("t (a g) -> (t a) g", g=64)
        v_flat = v_tok.rearrange("t (a g) -> (t a) g", g=128)
        o_flat = o_tok.rearrange("t (a g) -> (t a) g", g=128)

        with prog.scope():
            qe = prog.sbuf("qe", [128, 32 * 64], BF16)
            ke = prog.sbuf("ke", [128, 32 * 64], BF16)
            qeT = prog.sbuf("qeT", [64, LP], BF16)
            keT = prog.sbuf("keT", [64, LP], BF16)
            colp = prog.sbuf("colp", [128, 8], F32)
            cs_col = prog.sbuf("cscol", [128, 32], F32)
            scr = prog.sbuf("scr", [128, 512], F32)
            etile = [prog.sbuf(f"et{i}", [128, 512], BF16) for i in range(3)]
            vs = prog.sbuf("vs", [128, 32 * 128], BF16)
            otT = prog.sbuf("otT", [128, LP], BF16)
            tp = PP(prog, "tpe", [64, 512], 2, "vector", dtype=BF16)
            sp = PP(prog, "spe", [128, 512], 2, "scalar")
            ap_ = PP(prog, "ape", [128, 512], 2, "vector")
            otp = PP(prog, "otp", [128, 128], 2, "vector", dtype=BF16)
            for e_ in range(NEH):
                with prog.block() as bb:
                    ld1 = bb.dma("sync", qe[:, :].rearrange("p (b d) -> p b d", d=64),
                                 q_flat.rearrange("(e b p) d -> e p b d", e=NEH, p=128)[e_])
                    ld2 = bb.dma("sync", ke[:, :].rearrange("p (b d) -> p b d", d=64),
                                 k_flat.rearrange("(e b p) d -> e p b d", e=NEH, p=128)[e_])
                    for which, (tile, dst, ldw) in enumerate(((qe, qeT, ld1), (ke, keT, ld2))):
                        for cc in range(8):
                            ps, idx = tp.acquire(bb)
                            for j in range(4):
                                blk = cc * 4 + j
                                tt = bb.op("tensor", lambda e, tile=tile, blk=blk, j=j, ps=ps:
                                           e.transpose(ps[:, j * 128:(j + 1) * 128],
                                                       tile[:, blk * 64:(blk + 1) * 64],
                                                       ident[:, :]),
                                           waits=[ldw] if (cc == 0 and j == 0) else (),
                                           mark=(j == 3))
                            tp.consume(bb, idx, tt, [
                                lambda e, dst=dst, cc=cc, ps=ps: e.tensor_copy(
                                    dst[:, cc * 512:(cc + 1) * 512], ps[:, :])])
                    rs_prev = 0
                    ecnt = 0
                    esth = {}
                    for jb in range(32):
                        for ic in range(8):
                            ps, idx = sp.acquire(bb)
                            tgt = sp.produce(bb, lambda e, jb=jb, ic=ic, ps=ps: e.matmul(
                                ps[:, :], keT[:, jb * 128:(jb + 1) * 128],
                                qeT[:, ic * 512:(ic + 1) * 512], start=True, stop=True))
                            eb = etile[ecnt % 3]
                            wv = [(ve, rs_prev)] if (ic == 0 and rs_prev) else []
                            if ecnt >= 3:
                                wv.append(esth[ecnt - 3])
                            a1 = sp.consume(bb, idx, tgt, [
                                lambda e, ps=ps, eb=eb, ic=ic, _b=prog.constf(-CC): e.activation(
                                    eb[:, :], ps[:, :], AF.Exp, bias=_b,
                                    scale=SC, accum_out=colp[:, ic:ic + 1])], waits=wv)
                            esth[ecnt] = bb.dma("sync",
                                                e_stage[jb * 128:(jb + 1) * 128,
                                                        ic * 512:(ic + 1) * 512],
                                                eb[:, :], waits=[(act, a1)])
                            ecnt += 1
                        rs_prev = bb.op("vector", lambda e, jb=jb: e.reduce_sum(
                            cs_col[:, jb:jb + 1], colp[:, :], axis=AX.X),
                            waits=[(act, prog.chain["scalar"].v)])
                    bb.op("vector", lambda e: e.reciprocal(cs_col[:, :], cs_col[:, :]))
                    ldv = bb.dma("sync", vs[:, :].rearrange("p (b d) -> p b d", d=128),
                                 v_flat.rearrange("(e b p) d -> e p b d", e=NEH, p=128)[e_])
                    sv = bb.op("vector", lambda e: e.tensor_copy(vs[:, :], vs[:, :]) if False
                               else e.engine_nop(), waits=[ldv])
                    for jb in range(32):
                        sv = bb.op("vector", lambda e, jb=jb: e.tensor_scalar_mul(
                            vs[:, jb * 128:(jb + 1) * 128], vs[:, jb * 128:(jb + 1) * 128],
                            cs_col[:, jb:jb + 1]))
                with prog.block() as bb:
                    ecnt = 0
                    ldh = {}
                    for ic in range(8):
                        ps, idx = ap_.acquire(bb)
                        for jb in range(32):
                            eb = etile[ecnt % 3]
                            wv = []
                            if ecnt >= 3:
                                wv = [(pe, ldh[ecnt - 3])]
                            ldi = bb.dma("sync", eb[:, :],
                                         e_stage[jb * 128:(jb + 1) * 128,
                                                 ic * 512:(ic + 1) * 512], waits=wv)
                            mmt = bb.op("tensor", lambda e, jb=jb, eb=eb, ps=ps, last=(jb == 31):
                                        e.matmul(ps[:, :], vs[:, jb * 128:(jb + 1) * 128],
                                                 eb[:, :], start=(jb == 0), stop=last),
                                        waits=[ldi], mark=True)
                            ldh[ecnt] = mmt
                            ecnt += 1
                        ap_.consume(bb, idx, mmt, [
                            lambda e, ic=ic, ps=ps: e.tensor_copy(
                                otT[:, ic * 512:(ic + 1) * 512], ps[:, :])])
                    ost = None
                    for blk in range(32):
                        ps, idx = otp.acquire(bb)
                        tt = otp.produce(bb, lambda e, blk=blk, ps=ps: e.transpose(
                            ps[:, :], otT[:, blk * 128:(blk + 1) * 128], ident[:, :]),
                            waits=[(ve, prog.chain["vector"].v)] if blk == 0 else ())
                        ob = etile[blk % 3]
                        last = otp.consume(bb, idx, tt, [
                            lambda e, ps=ps, ob=ob: e.tensor_copy(ob[:, 0:128], ps[:, :])],
                            waits=[ost] if ost else ())
                        ost = bb.dma("sync",
                                     o_flat.rearrange("(e b p) d -> e p b d", e=NEH, p=128)[e_][:, blk, :],
                                     ob[:, 0:128], waits=[(ve, last)])
        add_residual_pass(prog, x_hbm, o_tok, xout_hbm, y_dtype=BF16)


def sub_l2_attn_b(prog: Prog, x_hbm, xout_hbm, w, uid):
    H, dh, dv = 16, 64, 64
    LH, LHD, NLAT = 4, 64, 64
    csts = prog.consts_tiles
    ident = csts["cst_ident"]
    q_stage = prog.dram(f"stg_q_{uid}", [H * dh, T], BF16)
    o_stage = prog.dram(f"stg_o_{uid}", [H * dv, T], BF16)
    xn_stage = prog.dram(f"stg_xn_{uid}", [E, T], BF16)
    rem_stage = prog.dram(f"stg_rem_{uid}", [E, T], BF16)
    mix_stage = prog.dram(f"stg_mix_{uid}", [E, T], BF16)
    act, ve, pe = prog.chain["scalar"], prog.chain["vector"], prog.chain["tensor"]

    with prog.scope():
        pooledT = [prog.sbuf(f"plT{i}", [128, NLAT], F32) for i in range(8)]
        rqT = [prog.sbuf(f"rqT{i}", [128, T], BF16) for i in range(2)]
        rkvT = [prog.sbuf(f"rkvT{i}", [128, NLAT], F32) for i in range(4)]
        with prog.scope():
            xnT = [prog.sbuf(f"xnT{i}", [128, T], BF16) for i in range(8)]
            with prog.block() as bb:
                rqw, _ = load_w_bf16(bb, prog, w["rq_w"], name="rqw")
                rqb = prog.sbuf("rqb", [128, 2], F32)
                bb.dma("sync", rqb[:, :], w["rq_b_col"][:, :])
                info = ln_to_xnT(bb, prog, x_hbm, xnT, ident)
                lnw = [(ve, info["ve_done"]), (act, info["act_done"])]
                for i in range(8):
                    bb.dma("sync", xn_stage[i * 128:(i + 1) * 128, :], xnT[i][:, :],
                           waits=lnw if i == 0 else ())
            with prog.block() as bb:
                for i in range(8):
                    for c in range(NLAT):
                        bb.op("vector", lambda e, i=i, c=c: e.reduce_sum(
                            pooledT[i][:, c:c + 1], xnT[i][:, c * 64:(c + 1) * 64], axis=AX.X))
                    bb.op("vector", lambda e, i=i: e.tensor_scalar_mul(
                        pooledT[i][:, :], pooledT[i][:, :], 1.0 / 64))
                pq = PP(prog, "prq", [128, 512], 2, "vector")
                mm_feature_major(bb, prog, rqw, xnT, T, pq,
                                 bias_fm_writer(prog, pq, rqT, rqb), m_blocks=2)

        with prog.scope():
            with prog.block() as bb:
                rkvw, rkvw_t = load_w_bf16(bb, prog, w["rkv_w"], name="rkvw")
                rkvb = prog.sbuf("rkvb", [128, 4], F32)
                rkvb_ld = bb.dma("sync", rkvb[:, :], w["rkv_b_col"][:, :])
                plb = [prog.sbuf(f"plb{i}", [128, NLAT], BF16) for i in range(8)]
                for i in range(8):
                    bb.op("vector", lambda e, i=i: e.tensor_copy(plb[i][:, :], pooledT[i][:, :]))
                prk = PP(prog, "prk", [128, 64], 2, "vector")
                for mb in range(4):
                    ps, idx = prk.acquire(bb)
                    for k in range(8):
                        fn = lambda e, k=k, mb=mb, ps=ps: e.matmul(
                            ps[:, :NLAT], rkvw[k][:, mb * 128:(mb + 1) * 128],
                            plb[k][:, :], start=(k == 0), stop=(k == 7))
                        if k == 7:
                            tgt = prk.produce(bb, fn)
                        else:
                            bb.op("tensor", fn,
                                  waits=([(ve, prog.chain["vector"].v)] + rkvw_t
                                         if (mb == 0 and k == 0) else ()),
                                  mark=False)
                    prk.consume(bb, idx, tgt, [
                        lambda e, mb=mb, ps=ps: e.tensor_scalar(
                            rkvT[mb][:, :], ps[:, :NLAT], rkvb[:, mb:mb + 1], None, ALU.add)],
                        waits=[rkvb_ld] if mb == 0 else ())

        kT = [prog.sbuf(f"kT{i}", [128, T], BF16) for i in range(8)]
        with prog.scope():
            with prog.scope():
                with prog.block() as bb:
                    wkT, _ = load_w_bf16(bb, prog, w["wk"], name="wk")
                    wqT, _ = load_w_bf16(bb, prog, w["wq"], name="wq")
                    bkc = prog.sbuf("bkc", [128, 8], F32)
                    bqc = prog.sbuf("bqc", [128, 8], F32)
                    bb.dma("sync", bkc[:, :], w["bk_col"][:, :])
                    bb.dma("sync", bqc[:, :], w["bq_col"][:, :])
                with prog.block() as bb:
                    pk = PP(prog, "pk", [128, 512], 2, "vector")
                    mm_feature_major_streamed(bb, prog, wkT, xn_stage, E, T, pk,
                                              bias_fm_writer(prog, pk, kT, bkc))
                    pq2 = PP(prog, "pq2", [128, 512], 2, "vector")
                    mm_feature_major_streamed(bb, prog, wqT, xn_stage, E, T, pq2,
                                              stage_fm_chunk_writer(prog, pq2, q_stage,
                                                                    bias_tile=bqc))
            vaug = [prog.sbuf(f"va{i}", [128, H * (dv + 1)], BF16) for i in range(NTB)]
            with prog.scope():
                with prog.block() as bb:
                    wvT, _ = load_w_bf16(bb, prog, w["wv"], name="wv")
                    bvrow = prog.sbuf("bvrow", [1, E], F32)
                    bb.dma("sync", bvrow[:, :], w["bv_row"][:, :])
                bvrep = prog.sbuf("bvrep", [128, E], F32)
                with prog.block() as bb:
                    bp = PP(prog, "brc", [128, 512], 1, "vector")
                    broadcast_row(bb, prog, bvrow, E, bvrep, bp)
                with prog.block() as bb:
                    pv = PP(prog, "pv", [128, 512], 2, "vector")
                    vaug_memset_ones(bb, prog, vaug, dv)
                    mm_token_major_streamed(bb, prog, xn_stage, E, wvT, E, pv,
                                            vaug_tm_writer(prog, pv, vaug, dv, bias_rep=bvrep))

            band = csts["cst_band"]

            def kb_range(q0, qcw):
                lo = max(0, (q0 - 128) // 128)
                hi = min(T // 128, ceil_div(q0 + qcw, 128))
                return range(lo, hi)

            def mask_ap(h, kb, q0, qcw):
                off = 640 + q0 - kb * 128
                return band[:, off:off + qcw]

            attention_heads(prog, q_stage=q_stage,
                            kT_ap=lambda h: kT[h // 2][(h % 2) * 64:(h % 2) * 64 + dh, :],
                            vaug=vaug, oT_stage=o_stage, H=H, dh=dh, dv=dv,
                            score_scale=0.125, cconst=10.0, kb_range=kb_range,
                            mask_ap=mask_ap, denom_eps=1e-30)

        latoutT = [prog.sbuf(f"loT{i}", [128, T], BF16) for i in range(2)]
        with prog.scope():
            with prog.block() as bb:
                rva = prog.sbuf("rva", [64, LH * (LHD + 1)], BF16)
                rkb16 = [prog.sbuf(f"rkb16{i}", [128, NLAT], BF16) for i in range(2)]
                rvb16 = [prog.sbuf(f"rvb16{i}", [128, NLAT], BF16) for i in range(2)]
                for i in range(2):
                    bb.op("vector", lambda e, i=i: e.tensor_copy(
                        rkb16[i][:, :], rkvT[i][:, :]))
                    bb.op("vector", lambda e, i=i: e.tensor_copy(
                        rvb16[i][:, :], rkvT[2 + i][:, :]))
                bb.op("vector", lambda e: e.memset(
                    rva[:, :].rearrange("p (h d) -> p h d", d=LHD + 1)[:, :, LHD:LHD + 1], 1.0))
                tpv = PP(prog, "tpv", [64, 128], 2, "vector", dtype=BF16)
                for i in range(2):
                    for half in range(2):
                        ps, idx = tpv.acquire(bb)
                        tt = tpv.produce(bb, lambda e, i=i, half=half, ps=ps: e.transpose(
                            ps[:, 0:64], rvb16[i][half * 64:(half + 1) * 64, :],
                            ident[half * 64:(half + 1) * 64, half * 64:(half + 1) * 64]),
                            waits=[(ve, prog.chain["vector"].v)] if (i == 0 and half == 0) else ())
                        lh = i * 2 + half
                        tpv.consume(bb, idx, tt, [
                            lambda e, lh=lh, ps=ps: e.tensor_copy(
                                rva[:, :].rearrange("p (h d) -> p h d", d=LHD + 1)[:, lh, 0:LHD],
                                ps[:, 0:64])])
                lsp = PP(prog, "lsp", [64, 512], 2, "scalar")
                lav = PP(prog, "lav", [LHD + 1, 512], 2, "vector")
                lrd = PP(prog, "lrd", [64, 512], 1, "vector")
                lexp = prog.sbuf("lexp", [64, 512], BF16)
                lmask = prog.sbuf("lmask", [64, 512], BF16)
                ldrow = prog.sbuf("ldrow", [1, T], F32)
                for lh in range(LH):
                    for qc in range(8):
                        q0 = qc * 512
                        mv = bb.op("vector", lambda e, q0=q0: e.tensor_scalar(
                            lmask[:, :], csts["cst_iotaq"][:, :], float(q0),
                            csts["cst_latb"][0:64, 0:1], ALU.add, ALU.is_ge))
                        ti, r0 = lh // 2, (lh % 2) * 64
                        ps, idx = lsp.acquire(bb)
                        tgt = lsp.produce(bb, lambda e, ti=ti, r0=r0, q0=q0, ps=ps: e.matmul(
                            ps[:, :], rkb16[ti][r0:r0 + 64, :],
                            rqT[ti][r0:r0 + 64, q0:q0 + 512], start=True, stop=True))
                        a1 = lsp.consume(bb, idx, tgt, [
                            lambda e, ps=ps, _b=prog.constf_row(-10.0, rows=64): e.activation(
                                lexp[:, :], ps[:, :], AF.Exp, bias=_b, scale=0.125)])
                        m2 = bb.op("vector", lambda e: e.tensor_tensor(
                            lexp[:, :], lexp[:, :], lmask[:, :], ALU.mult),
                            waits=[(act, a1)])
                        ps2, idx2 = lav.acquire(bb)
                        t2 = bb.op("tensor", lambda e, lh=lh, ps2=ps2: e.matmul(
                            ps2[:, :], rva[:, lh * (LHD + 1):(lh + 1) * (LHD + 1)],
                            lexp[:, :], start=True, stop=True),
                            waits=[(ve, m2)], mark=True)
                        lav.consume(bb, idx2, t2, [
                            lambda e, lh=lh, q0=q0, ps2=ps2: e.tensor_copy(
                                latoutT[lh // 2][(lh % 2) * 64:(lh % 2) * 64 + LHD, q0:q0 + 512],
                                ps2[0:LHD, :]),
                            lambda e, q0=q0, ps2=ps2: e.tensor_copy(
                                ldrow[:, q0:q0 + 512], ps2[LHD:LHD + 1, :]),
                        ])
                    le = bb.op("scalar", lambda e, _b=prog.constf_row(1e-30): e.activation(
                        ldrow[:, :], ldrow[:, :], AF.Ln, bias=_b),
                        waits=[(ve, prog.chain["vector"].v)])
                    le = bb.op("scalar", lambda e: e.activation(
                        ldrow[:, :], ldrow[:, :], AF.Exp, scale=-1.0))
                    for qc in range(8):
                        q0 = qc * 512
                        ps3, idx3 = lrd.acquire(bb)
                        t3 = lrd.produce(bb, lambda e, q0=q0, ps3=ps3: e.matmul(
                            ps3[:, :], csts["cst_onesrow"][:, 0:64], ldrow[:, q0:q0 + 512],
                            start=True, stop=True), waits=[(act, le)] if qc == 0 else ())
                        lrd.consume(bb, idx3, t3, [
                            lambda e, lh=lh, q0=q0, ps3=ps3: e.tensor_tensor(
                                latoutT[lh // 2][(lh % 2) * 64:(lh % 2) * 64 + LHD, q0:q0 + 512],
                                latoutT[lh // 2][(lh % 2) * 64:(lh % 2) * 64 + LHD, q0:q0 + 512],
                                ps3[:, :], ALU.mult)])

        with prog.scope():
            with prog.block() as bb:
                rro, _ = load_w_bf16(bb, prog, w["rout_w"], name="rro")
                rob = prog.sbuf("rob", [128, 8], F32)
                bb.dma("sync", rob[:, :], w["rout_b_col"][:, :])
            with prog.block() as bb:
                pr = PP(prog, "pr", [128, 512], 2, "vector")
                mm_feature_major(bb, prog, rro, latoutT, T, pr,
                                 stage_fm_chunk_writer(prog, pr, rem_stage, bias_tile=rob))

        with prog.scope():
            with prog.block() as bb:
                glc = prog.sbuf("glc", [64, 1], BF16)
                grc = prog.sbuf("grc", [64, 1], BF16)
                glf = prog.sbuf("glf", [64, 1], F32)
                grf = prog.sbuf("grf", [64, 1], F32)
                lg1 = bb.dma("sync", glf[:, :], w["gl"][:, :])
                lg2 = bb.dma("sync", grf[:, :], w["gr"][:, :])
                bb.op("vector", lambda e: e.tensor_copy(glc[:, :], glf[:, :]),
                      waits=wmax([lg1, lg2]))
                bb.op("vector", lambda e: e.tensor_copy(grc[:, :], grf[:, :]))
            loc = prog.sbuf("loc", [64, T], BF16)
            rem = prog.sbuf("rem", [64, T], BF16)
            mix = prog.sbuf("mix", [64, T], BF16)
            grow = prog.sbuf("grow_g", [1, T], F32)
            gp = PP(prog, "gp", [1, 512], 2, "vector")
            gb = PP(prog, "gb", [64, 512], 2, "vector")
            for h in range(H):
                with prog.block() as bb:
                    l1 = bb.dma("sync", loc[:, :], o_stage[h * 64:(h + 1) * 64, :])
                    l2 = bb.dma("sync", rem[:, :], rem_stage[h * 64:(h + 1) * 64, :])
                    for qc in range(8):
                        q0 = qc * 512
                        ps, idx = gp.acquire(bb)
                        bb.op("tensor", lambda e, q0=q0, ps=ps: e.matmul(
                            ps[0:1, :], glc[:, :], loc[:, q0:q0 + 512], start=True, stop=False),
                            waits=wmax([l1, l2]) if qc == 0 else (), mark=False)
                        tg = gp.produce(bb, lambda e, q0=q0, ps=ps: e.matmul(
                            ps[0:1, :], grc[:, :], rem[:, q0:q0 + 512], start=False, stop=True))
                        gp.consume(bb, idx, tg, [
                            lambda e, q0=q0, ps=ps: e.tensor_copy(grow[:, q0:q0 + 512], ps[0:1, :])])
                    sg = bb.op("scalar", lambda e: e.activation(
                        grow[:, :], grow[:, :], AF.Sigmoid),
                        waits=[(ve, prog.chain["vector"].v)])
                    for qc in range(8):
                        q0 = qc * 512
                        ps2, idx2 = gb.acquire(bb)
                        t2 = gb.produce(bb, lambda e, q0=q0, ps2=ps2: e.matmul(
                            ps2[:, :], csts["cst_onesrow"][:, 0:64], grow[:, q0:q0 + 512],
                            start=True, stop=True), waits=[(act, sg)] if qc == 0 else ())
                        gb.consume(bb, idx2, t2, [
                            lambda e, q0=q0: e.tensor_tensor(
                                mix[:, q0:q0 + 512], loc[:, q0:q0 + 512],
                                rem[:, q0:q0 + 512], ALU.subtract),
                            lambda e, q0=q0, ps2=ps2: e.tensor_tensor(
                                mix[:, q0:q0 + 512], mix[:, q0:q0 + 512], ps2[:, :], ALU.mult),
                            lambda e, q0=q0: e.tensor_tensor(
                                mix[:, q0:q0 + 512], mix[:, q0:q0 + 512],
                                rem[:, q0:q0 + 512], ALU.add),
                        ])
                    bb.dma("sync", mix_stage[h * 64:(h + 1) * 64, :], mix[:, :],
                           waits=[(ve, prog.chain["vector"].v)])

        with prog.scope():
            with prog.block() as bb:
                wo, _ = load_w_bf16(bb, prog, w["out_w"], name="wo")
                obrow = prog.sbuf("obrow", [1, E], F32)
                bb.dma("sync", obrow[:, :], w["out_b_row"][:, :])
            obrep = prog.sbuf("obrep", [128, E], BF16)
            with prog.block() as bb:
                bp = PP(prog, "brc3", [128, 512], 1, "vector")
                broadcast_row(bb, prog, obrow, E, obrep, bp)
            out_proj_residual(prog, act_stage=mix_stage, n_act=E, wo_tiles=wo,
                              x_hbm=x_hbm, xout_hbm=xout_hbm, row_bias_rep=obrep)


def sub_l1_attn_a(prog: Prog, x_hbm, xout_hbm, w, uid):
    C = 64
    NC = T // C
    csts = prog.consts_tiles
    ident = csts["cst_ident"]
    xn_f32_hbm = prog.dram(f"stg_xnf_{uid}", [T, E], F32)
    xn_bf_hbm = prog.dram(f"stg_xnb_{uid}", [T, E], BF16)
    v_hbm = prog.dram(f"stg_v_{uid}", [T, E], BF16)
    reads_hbm = prog.dram(f"stg_r_{uid}", [T, E], BF16)
    act, ve, pe = prog.chain["scalar"], prog.chain["vector"], prog.chain["tensor"]

    with prog.scope():
        lgcol = prog.sbuf("lgcol", [128, 1], F32)
        deccol = prog.sbuf("deccol", [64, 1], F32)
        gwcol = prog.sbuf("gwcol", [64, 1], F32)
        gccol = prog.sbuf("gccol", [128, 1], F32)
        alcol = prog.sbuf("alcol", [128, 1], F32)
        mtb = prog.sbuf("mtb", [64, 64], BF16)
        with prog.scope():
         with prog.block() as bb:
            dsc = prog.sbuf("dsc", [1, 2], F32)
            l1 = bb.dma("sync", dsc[:, 0:1], w["decay"][:, :])
            l2 = bb.dma("sync", dsc[:, 1:2], w["log_alpha"][:, :])
            diffT = prog.sbuf("diffT", [64, 64], F32)
            causT = prog.sbuf("causT", [64, 64], F32)
            l3 = bb.dma("sync", diffT[:, :], w["diffT"][:, :])
            l4 = bb.dma("sync", causT[:, :], w["causT"][:, :])
            rev64 = prog.sbuf("rev64", [64, 1], F32)
            l5 = bb.dma("sync", rev64[:, :], w["rev64"][:, :])
            a1 = bb.op("scalar", lambda e: e.activation(dsc[:, 0:1], dsc[:, 0:1], AF.Sigmoid),
                       waits=wmax([l1, l2, l3, l4, l5]))
            a2 = bb.op("scalar", lambda e: e.activation(dsc[:, 0:1], dsc[:, 0:1], AF.Ln))
            bp = PP(prog, "l1abp", [128, 64], 2, "vector")
            ps, idx = bp.acquire(bb)
            t1 = bp.produce(bb, lambda e, ps=ps: e.matmul(
                ps[:, 0:1], csts["cst_onesrow"][:, :], dsc[:, 0:1],
                start=True, stop=True), waits=[(act, a2)])
            bp.consume(bb, idx, t1, [
                lambda e, ps=ps: e.tensor_copy(lgcol[:, :], ps[:, 0:1])])
            a3 = bb.op("scalar", lambda e: e.activation(dsc[:, 1:2], dsc[:, 1:2], AF.Exp))
            ps2, idx2 = bp.acquire(bb)
            t2 = bp.produce(bb, lambda e, ps2=ps2: e.matmul(
                ps2[:, 0:1], csts["cst_onesrow"][:, :], dsc[:, 1:2],
                start=True, stop=True), waits=[(act, a3)])
            bp.consume(bb, idx2, t2, [
                lambda e, ps2=ps2: e.tensor_copy(alcol[:, :], ps2[:, 0:1])])
            a4 = bb.op("scalar", lambda e: e.activation(
                mtb[:, :], diffT[:, :], AF.Exp, scale=lgcol[0:64, 0:1]),
                waits=[(ve, prog.chain["vector"].v)])
            bb.op("vector", lambda e: e.tensor_tensor(mtb[:, :], mtb[:, :], causT[:, :], ALU.mult),
                  waits=[(act, a4)])
            bb.op("scalar", lambda e: e.activation(
                deccol[:, :], csts["cst_iota128"][0:64, :], AF.Exp, scale=lgcol[0:64, 0:1]))
            bb.op("scalar", lambda e: e.activation(
                gwcol[:, :], rev64[:, :], AF.Exp, scale=lgcol[0:64, 0:1]))
            a5 = bb.op("vector", lambda e: e.tensor_scalar_mul(gccol[:, :], lgcol[:, :], 64.0))
            bb.op("scalar", lambda e: e.activation(gccol[:, :], gccol[:, :], AF.Exp),
                  waits=[(ve, a5)])

        xnT = [prog.sbuf(f"xnTp{i}", [128, T + 4], BF16) for i in range(8)]
        with prog.scope():
            with prog.block() as bb:
                for i in range(8):
                    bb.op("vector", lambda e, i=i: e.memset(xnT[i][:, 0:1], 0.0))
                ln_to_xnT(bb, prog, x_hbm, xnT, ident, pad_left=1,
                          xn_f32_hbm=xn_f32_hbm, xn_bf16_hbm=xn_bf_hbm)

        with prog.scope():
            with prog.block() as bb:
                wwT, _ = load_w_bf16(bb, prog, w["W_writeT"], name="wwT")
            with prog.block() as bb:
                pv = PP(prog, "pv", [128, 512], 2, "vector")
                vt = [prog.sbuf(f"vt{i}", [128, 512], BF16) for i in range(4)]
                cnt = [0]
                sth = {}

                def v_writer(bb, tb, c0, c1, ps, idx, tgt):
                    k = cnt[0] % 4
                    wv = [sth[cnt[0] - 4]] if cnt[0] >= 4 else []
                    last = pv.consume(bb, idx, tgt, [
                        lambda e, ps=ps, k=k, c0=c0, c1=c1: e.tensor_copy(
                            vt[k][:, :c1 - c0], ps[:, :c1 - c0])], waits=wv)
                    sth[cnt[0]] = bb.dma("sync", v_hbm[tb * 128:(tb + 1) * 128, c0:c1],
                                         vt[k][:, :c1 - c0], waits=[(ve, last)])
                    cnt[0] += 1
                mm_token_major(bb, prog, xnT, wwT, E, pv, v_writer, act_col_base=1)

        WT = [prog.sbuf(f"WTs{i}", [128, E], BF16) for i in range(8)]
        with prog.scope():
            wk_tok = [prog.sbuf(f"wkt{i}", [64, E], BF16) for i in range(2)]
            v_tok = [prog.sbuf(f"vtk{i}", [64, E], BF16) for i in range(2)]
            vgw = prog.sbuf("vgw", [64, E], BF16)
            smt = prog.sbuf("smt", [64, 64], BF16)
            intra_sb = prog.sbuf("intra", [64, E], F32)
            reads_sb = prog.sbuf("reads", [64, E], BF16)
            stp = PP(prog, "stp", [64, 64], 1, "vector")
            i1p = PP(prog, "i1p", [64, E], 1, "vector")
            i2p = PP(prog, "i2p", [64, E], 1, "vector")
            upp = PP(prog, "upp", [128, E], 1, "vector")
            with prog.block() as bb:
                for i in range(8):
                    bb.op("vector", lambda e, i=i: e.memset(WT[i][:, :], 0.0))
            ld_hist = {}
            st_hist = {}
            for c in range(NC):
                with prog.block() as bb:
                    buf = c % 2
                    if c == 0:
                        mz = bb.op("vector", lambda e: e.memset(wk_tok[0][0:1, :], 0.0))
                        lw = bb.dma("sync", wk_tok[0][1:64, :], xn_bf_hbm[0:63, :])
                    else:
                        lw = bb.dma("sync", wk_tok[buf][:, :],
                                    xn_bf_hbm[c * C - 1:(c + 1) * C - 1, :])
                    lv = bb.dma("sync", v_tok[buf][:, :], v_hbm[c * C:(c + 1) * C, :])
                    gv = bb.op("vector", lambda e, buf=buf: e.tensor_scalar_mul(
                        vgw[:, :], v_tok[buf][:, :], gwcol[:, 0:1]), waits=[lv])
                    ps_st, i_st = stp.acquire(bb)
                    for et in range(8):
                        fn = lambda e, et=et, c=c, ps_st=ps_st: e.matmul(
                            ps_st[:, :], xnT[et][:, c * C:(c + 1) * C],
                            xnT[et][:, c * C + 1:(c + 1) * C + 1],
                            start=(et == 0), stop=(et == 7))
                        if et == 7:
                            t_st = stp.produce(bb, fn)
                        else:
                            bb.op("tensor", fn, mark=False)
                    stp.consume(bb, i_st, t_st, [
                        lambda e, ps_st=ps_st: e.tensor_tensor(
                            smt[:, :], ps_st[:, :], mtb[:, :], ALU.mult)])
                    ps_i1, i_i1 = i1p.acquire(bb)
                    for et in range(8):
                        for fc in range(2):
                            fn = lambda e, et=et, fc=fc, c=c, ps_i1=ps_i1: e.matmul(
                                ps_i1[:, fc * 512:(fc + 1) * 512],
                                xnT[et][:, c * C + 1:(c + 1) * C + 1],
                                WT[et][:, fc * 512:(fc + 1) * 512],
                                start=(et == 0), stop=(et == 7))
                            if et == 7 and fc == 1:
                                t_i1 = i1p.produce(bb, fn)
                            else:
                                bb.op("tensor", fn, mark=False)
                    ps_i2, i_i2 = i2p.acquire(bb)
                    for fc in range(2):
                        fn = lambda e, fc=fc, buf=buf, ps_i2=ps_i2: e.matmul(
                            ps_i2[:, fc * 512:(fc + 1) * 512], smt[:, :],
                            v_tok[buf][:, fc * 512:(fc + 1) * 512],
                            start=True, stop=True)
                        tloc = bb.op("tensor", fn,
                                     waits=[(ve, prog.chain["vector"].v)] if fc == 0 else (),
                                     mark=(fc == 1))
                    t_i2 = tloc
                    i2p.consume(bb, i_i2, t_i2, [
                        lambda e, ps_i2=ps_i2: e.tensor_copy(intra_sb[:, :], ps_i2[:, :])])
                    rd = i1p.consume(bb, i_i1, t_i1, [
                        lambda e, ps_i1=ps_i1: e.scalar_tensor_tensor(
                            reads_sb[:, :], ps_i1[:, :], deccol[:, 0:1], intra_sb[:, :],
                            ALU.mult, ALU.add)])
                    st_hist[c] = bb.dma("sync", reads_hbm[c * C:(c + 1) * C, :],
                                        reads_sb[:, :], waits=[(ve, rd)])
                    for et in range(8):
                        ps_up, i_up = upp.acquire(bb)
                        for fc in range(2):
                            fn = lambda e, et=et, fc=fc, buf=buf, ps_up=ps_up: e.matmul(
                                ps_up[:, fc * 512:(fc + 1) * 512],
                                wk_tok[buf][:, et * 128:(et + 1) * 128],
                                vgw[:, fc * 512:(fc + 1) * 512], start=True, stop=True)
                            t_up = bb.op("tensor", fn,
                                         waits=[(ve, gv), lw] if (et == 0 and fc == 0) else (),
                                         mark=(fc == 1))
                        upp.consume(bb, i_up, t_up, [
                            lambda e, et=et, ps_up=ps_up: e.scalar_tensor_tensor(
                                WT[et][:, :], WT[et][:, :], gccol[:, 0:1],
                                ps_up[:, :E], ALU.mult, ALU.add)])

        with prog.scope():
            with prog.block() as bb:
                wrT, _ = load_w_bf16(bb, prog, w["W_readT"], name="wrT")
            rT = [prog.sbuf(f"rT{i}", [128, 128], BF16) for i in range(8)]
            rtk = [prog.sbuf(f"rtk{i}", [128, E], BF16) for i in range(2)]
            xo = [prog.sbuf(f"xo{i}", [128, E], F32) for i in range(2)]
            zo = [prog.sbuf(f"zo{i}", [128, E], F32) for i in range(2)]
            yt = [prog.sbuf(f"yt{i}", [128, 512], F32) for i in range(4)]
            with prog.block() as bb:
                fp = PP(prog, "fp", [128, 512], 2, "vector")
                tpr = PP(prog, "tpr", [128, 128], 2, "vector", dtype=BF16)
                cnt = [0]
                sth = {}
                for tb in range(NTB):
                    buf = tb % 2
                    wv = [(ve, ld_hist.get(tb - 2, 0))] if tb >= 2 else []
                    lr = bb.dma("sync", rtk[buf][:, :], reads_hbm[tb * 128:(tb + 1) * 128, :],
                                waits=wv)
                    lx = bb.dma("sync", xo[buf][:, :], x_hbm[tb * 128:(tb + 1) * 128, :])
                    lz = bb.dma("sync", zo[buf][:, :], xn_f32_hbm[tb * 128:(tb + 1) * 128, :])
                    for et in range(8):
                        ps, idx = tpr.acquire(bb)
                        tt = tpr.produce(bb, lambda e, et=et, buf=buf, ps=ps: e.transpose(
                            ps[:, :], rtk[buf][:, et * 128:(et + 1) * 128], ident[:, :]),
                            waits=[lr] if et == 0 else ())
                        tpr.consume(bb, idx, tt, [
                            lambda e, et=et, ps=ps: e.tensor_copy(rT[et][:, :], ps[:, :])])
                    trdone = prog.chain["vector"].v
                    for fc in range(2):
                        ps, idx = fp.acquire(bb)
                        for et in range(8):
                            fn = lambda e, et=et, fc=fc, ps=ps: e.matmul(
                                ps[:, :], rT[et][:, :], wrT[et][:, fc * 512:(fc + 1) * 512],
                                start=(et == 0), stop=(et == 7))
                            if et == 7:
                                tg = fp.produce(bb, fn)
                            else:
                                bb.op("tensor", fn,
                                      waits=[(ve, trdone)] if et == 0 else (), mark=False)
                        k = cnt[0] % 4
                        wv2 = [sth[cnt[0] - 4]] if cnt[0] >= 4 else []
                        wv2 += wmax([lx, lz])
                        last = fp.consume(bb, idx, tg, [
                            lambda e, ps=ps, k=k, fc=fc, buf=buf: e.scalar_tensor_tensor(
                                yt[k][:, :], ps[:, :], alcol[:, 0:1],
                                zo[buf][:, fc * 512:(fc + 1) * 512], ALU.mult, ALU.add),
                            lambda e, k=k, fc=fc, buf=buf: e.tensor_tensor(
                                yt[k][:, :], yt[k][:, :], xo[buf][:, fc * 512:(fc + 1) * 512],
                                ALU.add)], waits=wv2)
                        ld_hist[tb] = last
                        sth[cnt[0]] = bb.dma("sync",
                                             xout_hbm[tb * 128:(tb + 1) * 128,
                                                      fc * 512:(fc + 1) * 512],
                                             yt[k][:, :], waits=[(ve, last)])
                        cnt[0] += 1


def sub_l3_attn_a(prog: Prog, x_hbm, xout_hbm, w, uid):
    DI = 2048
    DS, DTR = 16, 64
    csts = prog.consts_tiles
    ident = csts["cst_ident"]
    xn_stage = prog.dram(f"stg_xn_{uid}", [E, T], BF16)
    pre_stage = prog.dram(f"stg_pre_{uid}", [T, E], F32)
    z_stage = prog.dram(f"stg_z_{uid}", [DI, T], BF16)
    y_stage = prog.dram(f"stg_y_{uid}", [DI, T], BF16)
    bc_stage = prog.dram(f"stg_bc_{uid}", [2, DS, 128, T], BF16)
    bc_rows = [prog.dram(f"stg_bcr{i}_{uid}", [16, T], F32) for i in range(2)]
    act, ve, pe, pl = (prog.chain["scalar"], prog.chain["vector"],
                       prog.chain["tensor"], prog.chain["gpsimd"])

    with prog.scope():
        with prog.scope():
            xnT = [prog.sbuf(f"xnT{i}", [128, T], BF16) for i in range(8)]
            with prog.block() as bb:
                info = ln_to_xnT(bb, prog, x_hbm, xnT, ident,
                                 second_norm=("rms_eps_inside", 1e-5),
                                 pre_f32_hbm=pre_stage)
                for i in range(8):
                    bb.dma("sync", xn_stage[i * 128:(i + 1) * 128, :], xnT[i][:, :],
                           waits=[(ve, info["ve_done"]), (act, info["act_done"])] if i == 0 else ())

        xi_stage = prog.dram(f"stg_xi_{uid}", [DI, T], BF16)
        xc_stage = prog.dram(f"stg_xc_{uid}", [DI, T], BF16)
        with prog.scope():
          with prog.scope():
            with prog.block() as bb:
                win, win_t = load_w_bf16(bb, prog, w["W_in"], name="win")
                gcol = prog.sbuf("gcol", [128, 8], F32)
                bcolb = prog.sbuf("bcolb", [128, 8], BF16)
                lg1 = bb.dma("sync", gcol[:, :], w["lng_col"][:, :])
                lg2 = bb.dma("gpsimd", bcolb[:, :], w["lnb_col"][:, :])
                bias_xz = prog.sbuf("bias_xz", [128, 32], F32)
                bxp = PP(prog, "bxp", [128, 32], 2, "vector")
                for fb in range(32):
                    ps, idx = bxp.acquire(bb)
                    for k in range(8):
                        fn = lambda e, k=k, fb=fb, ps=ps: e.matmul(
                            ps[:, 0:1], win[k][:, fb * 128:(fb + 1) * 128],
                            bcolb[:, k:k + 1], start=(k == 0), stop=(k == 7))
                        if k == 7:
                            tgt = bxp.produce(bb, fn)
                        else:
                            bb.op("tensor", fn,
                                  waits=(win_t + wmax([lg1, lg2])) if (fb == 0 and k == 0) else (),
                                  mark=False)
                    bxp.consume(bb, idx, tgt, [
                        lambda e, fb=fb, ps=ps: e.tensor_copy(bias_xz[:, fb:fb + 1], ps[:, 0:1])])
                for k in range(8):
                    bb.op("vector", lambda e, k=k: e.tensor_scalar_mul(
                        win[k][:, :], win[k][:, :], gcol[:, k:k + 1]),
                        waits=[(pe, prog.chain["tensor"].v)] if k == 0 else ())
            with prog.block() as bb:
                pxz = PP(prog, "pxz", [128, 512], 2, "vector")
                zt = [prog.sbuf(f"zt{i}", [128, 512], BF16) for i in range(4)]
                zcnt = [0]
                zsth = {}

                def xz_writer(bb, mb, c0, c1, ps, idx, tgt):
                    stage = xi_stage if mb < 16 else z_stage
                    mb2 = mb if mb < 16 else mb - 16
                    k = zcnt[0] % 4
                    wv = [zsth[zcnt[0] - 4]] if zcnt[0] >= 4 else []
                    last = pxz.consume(bb, idx, tgt, [
                        lambda e, mb=mb, c0=c0, c1=c1, ps=ps, k=k: e.tensor_scalar(
                            zt[k][:, :c1 - c0], ps[:, :c1 - c0],
                            bias_xz[:, mb:mb + 1], None, ALU.add)], waits=wv)
                    zsth[zcnt[0]] = bb.dma("sync",
                                           stage[mb2 * 128:(mb2 + 1) * 128, c0:c1],
                                           zt[k][:, :c1 - c0], waits=[(ve, last)])
                    zcnt[0] += 1

                mm_feature_major_streamed(bb, prog, win, xn_stage, E, T, pxz,
                                          xz_writer, m_blocks=32)

          with prog.scope():
            with prog.block() as bb:
                ccol = prog.sbuf("ccol", [128, 64], F32)
                cbcol = prog.sbuf("cbcol", [128, 16], F32)
                lc1 = bb.dma("sync", ccol[:, :], w["convw_col"][:, :])
                lc2 = bb.dma("sync", cbcol[:, :], w["convb_col"][:, :])
                acc = prog.sbuf("cacc", [128, T], F32)
                xib = [prog.sbuf(f"xib{i}", [128, 3 + T], BF16) for i in range(2)]
                xcb = [prog.sbuf(f"xcb{i}", [128, T], BF16) for i in range(2)]
                for i in range(2):
                    bb.op("vector", lambda e, i=i: e.memset(xib[i][:, 0:3], 0.0))
                sil_prev = 0
                sth = {}
                for ct in range(16):
                    bi = ct % 2
                    wv = [(ve, sth[ct - 2][0])] if ct >= 2 else []
                    ld = bb.dma("sync", xib[bi][:, 3:3 + T],
                                xi_stage[ct * 128:(ct + 1) * 128, :], waits=wv)
                    wv0 = (wmax([lc1, lc2]) if ct == 0 else [(act, sil_prev)]) + [ld]
                    v0 = bb.op("vector", lambda e, ct=ct, bi=bi: e.tensor_scalar_mul(
                        acc[:, :], xib[bi][:, 0:T], ccol[:, ct * 4:ct * 4 + 1]),
                        waits=wv0)
                    lastv = v0
                    for k in range(1, 4):
                        lastv = bb.op("vector", lambda e, ct=ct, k=k, bi=bi: e.scalar_tensor_tensor(
                            acc[:, :], xib[bi][:, k:k + T], ccol[:, ct * 4 + k:ct * 4 + k + 1],
                            acc[:, :], ALU.mult, ALU.add))
                    wv2 = [(prog.qrr["sync"][0], 0)]
                    if ct >= 2:
                        wv2 = [sth[ct - 2][1]]
                    sil_prev = bb.op("scalar", lambda e, ct=ct, bi=bi: e.activation(
                        xcb[bi][:, :], acc[:, :], AF.Silu, bias=cbcol[:, ct:ct + 1]),
                        waits=[(ve, lastv)] + (wv2 if ct >= 2 else []))
                    stt = bb.dma("sync", xc_stage[ct * 128:(ct + 1) * 128, :], xcb[bi][:, :],
                                 waits=[(act, sil_prev)])
                    sth[ct] = (lastv, stt)

        dtT = prog.sbuf("dtT", [64, T], BF16)
        with prog.scope():
            with prog.block() as bb:
                wdt_, _ = load_w_bf16(bb, prog, w["Wx_dt"], name="wxdt")
                wb_, _ = load_w_bf16(bb, prog, w["Wx_B"], name="wxb")
                wc_, _ = load_w_bf16(bb, prog, w["Wx_C"], name="wxc")
            with prog.block() as bb:
                pd = PP(prog, "pd", [64, 512], 2, "vector")
                pbc = PP(prog, "pbc", [16, 512], 2, "vector")
                bcrow = [prog.sbuf(f"bcrow{i}", [16, T], F32) for i in range(2)]
                xcc = [[prog.sbuf(f"xcc{b}_{i}", [128, 512], BF16) for i in range(16)]
                       for b in range(2)]
                ldh = {}
                for ci, c0 in enumerate(range(0, T, 512)):
                    b = ci % 2
                    wv = [(pe, ldh[ci - 2])] if ci >= 2 else []
                    lds = []
                    for k in range(16):
                        lds.append(bb.dma("sync", xcc[b][k][:, :],
                                          xc_stage[k * 128:(k + 1) * 128, c0:c0 + 512],
                                          waits=wv if k == 0 else ()))
                    ps, idx = pd.acquire(bb)
                    for k in range(16):
                        fn = lambda e, k=k, c0=c0, ps=ps, b=b: e.matmul(
                            ps[:, :], wdt_[k][:, :], xcc[b][k][:, :],
                            start=(k == 0), stop=(k == 15))
                        if k == 15:
                            tgt = pd.produce(bb, fn)
                        else:
                            bb.op("tensor", fn, waits=wmax(lds) if k == 0 else (), mark=False)
                    pd.consume(bb, idx, tgt, [
                        lambda e, c0=c0, ps=ps: e.tensor_copy(dtT[:, c0:c0 + 512], ps[:, :])])
                    for which, wt_ in enumerate((wb_, wc_)):
                        ps2, idx2 = pbc.acquire(bb)
                        for k in range(16):
                            fn = lambda e, k=k, ps2=ps2, wt_=wt_, b=b: e.matmul(
                                ps2[0:16, :], wt_[k][:, :], xcc[b][k][:, :],
                                start=(k == 0), stop=(k == 15))
                            if k == 15:
                                tgt2 = pbc.produce(bb, fn)
                            else:
                                bb.op("tensor", fn, mark=False)
                        pbc.consume(bb, idx2, tgt2, [
                            lambda e, which=which, c0=c0, ps2=ps2: e.tensor_copy(
                                bcrow[which][:, c0:c0 + 512], ps2[0:16, :])])
                    ldh[ci] = prog.chain["tensor"].v
                for which in range(2):
                    bb.dma("sync", bc_rows[which][:, :], bcrow[which][:, :],
                           waits=[(ve, prog.chain["vector"].v)] if which == 0 else ())
            with prog.block() as bb:
                brow = prog.sbuf("brow", [1, T], F32)
                brep = prog.sbuf("brep", [128, T], BF16)
                pb2 = PP(prog, "pb2", [128, 512], 2, "vector")
                st = None
                for which in range(2):
                    for n in range(DS):
                        wv = [(pe, prog.chain["tensor"].v)]
                        ldb = bb.dma("sync", brow[:, :], bc_rows[which][n:n + 1, :], waits=wv)
                        bb.wait("tensor", *ldb)
                        if st is not None:
                            bb.wait("vector", *st)
                        for c0 in range(0, T, 512):
                            ps, idx = pb2.acquire(bb)
                            tgt = pb2.produce(bb, lambda e, c0=c0, ps=ps: e.matmul(
                                ps[:, :], csts["cst_onesrow"][:, :], brow[:, c0:c0 + 512],
                                start=True, stop=True))
                            pb2.consume(bb, idx, tgt, [
                                lambda e, c0=c0, ps=ps: e.tensor_copy(
                                    brep[:, c0:c0 + 512], ps[:, :])])
                        st = bb.dma("sync", bc_stage[which, n, :, :], brep[:, :],
                                    waits=[(ve, prog.chain["vector"].v)])

        with prog.scope():
            with prog.block() as bb:
                wdt2, _ = load_w_bf16(bb, prog, w["W_dt"], name="wdt2")
                bdtc = prog.sbuf("bdtc", [128, 16], F32)
                bb.dma("sync", bdtc[:, :], w["bdt_col"][:, :])
                alog = [prog.sbuf(f"alog{i}", [128, 16], F32) for i in range(16)]
                for i in range(16):
                    bb.dma("sync", alog[i][:, :],
                           w["A_log"][i * 128:(i + 1) * 128, :])
                dcol = prog.sbuf("dcol", [128, 16], F32)
                bb.dma("sync", dcol[:, :], w["D_col"][:, :])
            with prog.block() as bb:
                for i in range(16):
                    a1 = bb.op("scalar", lambda e, i=i: e.activation(
                        alog[i][:, :], alog[i][:, :], AF.Exp))
                    bb.op("vector", lambda e, i=i: e.tensor_scalar_mul(
                        alog[i][:, :], alog[i][:, :], -1.0), waits=[(act, a1)])
            xct = prog.sbuf("xct", [128, T], BF16)
            deltaT = prog.sbuf("deltaT", [128, T], F32)
            scrD = prog.sbuf("scrD", [128, T], F32)
            dA = prog.sbuf("dA", [128, T], F32)
            du = prog.sbuf("du", [128, T], F32)
            dBu = prog.sbuf("dBu", [128, T], F32)
            ss = prog.sbuf("ss", [128, T], F32)
            yac = prog.sbuf("yac", [128, T], F32)
            bb_b = [prog.sbuf(f"bbb{i}", [128, T], BF16) for i in range(2)]
            cc_b = [prog.sbuf(f"ccb{i}", [128, T], BF16) for i in range(2)]
            zl = prog.sbuf("zl", [128, T], BF16)
            yout = prog.sbuf("yout", [128, T], BF16)
            pdl = PP(prog, "pdl", [128, 512], 2, "scalar")
            for dt in range(16):
                with prog.block() as bb:
                    ldxc = bb.dma("sync", xct[:, :], xc_stage[dt * 128:(dt + 1) * 128, :])
                    for c0 in range(0, T, 512):
                        ps, idx = pdl.acquire(bb)
                        tgt = pdl.produce(bb, lambda e, dt=dt, c0=c0, ps=ps: e.matmul(
                            ps[:, :], wdt2[0][0:64, dt * 128:(dt + 1) * 128],
                            dtT[:, c0:c0 + 512], start=True, stop=True))
                        pdl.consume(bb, idx, tgt, [
                            lambda e, c0=c0, ps=ps, dt=dt: e.activation(
                                scrD[:, c0:c0 + 512], ps[:, :], AF.Abs,
                                bias=bdtc[:, dt:dt + 1]),
                            lambda e, c0=c0: e.activation(
                                scrD[:, c0:c0 + 512], scrD[:, c0:c0 + 512], AF.Exp, scale=-1.0),
                            lambda e, c0=c0, _b=prog.constf(1.0): e.activation(
                                scrD[:, c0:c0 + 512], scrD[:, c0:c0 + 512], AF.Ln, bias=_b),
                            lambda e, c0=c0, ps=ps, dt=dt: e.activation(
                                deltaT[:, c0:c0 + 512], ps[:, :], AF.Relu,
                                bias=bdtc[:, dt:dt + 1]),
                        ])
                    dv1 = bb.op("vector", lambda e: e.tensor_tensor(
                        deltaT[:, :], deltaT[:, :], scrD[:, :], ALU.add),
                        waits=[(act, prog.chain["scalar"].v)])
                    dv2 = bb.op("vector", lambda e: e.tensor_tensor(
                        du[:, :], deltaT[:, :], xct[:, :], ALU.mult), waits=[ldxc])
                    ldz = bb.dma("sync", zl[:, :], z_stage[dt * 128:(dt + 1) * 128, :])
                    yz = bb.op("vector", lambda e: e.memset(yac[:, :], 0.0))
                    ld_bc = {}
                    s1_prev = 0
                    g2_prev = 0
                    for n in range(DS):
                        bi = n % 2
                        wv = [(pl, ld_bc[n - 2][0])] if n >= 2 else []
                        l1 = bb.dma("sync", bb_b[bi][:, :], bc_stage[0, n, :, :], waits=wv)
                        wv2 = [(ve, ld_bc[n - 2][1])] if n >= 2 else []
                        l2 = bb.dma("sync", cc_b[bi][:, :], bc_stage[1, n, :, :], waits=wv2)
                        a_dA = bb.op("scalar", lambda e, dt=dt, n=n: e.activation(
                            dA[:, :], deltaT[:, :], AF.Exp, scale=alog[dt][:, n:n + 1]),
                            waits=[(ve, dv1)] if n == 0 else [(ve, s1_prev)])
                        g1 = bb.op("gpsimd", lambda e, bi=bi: e.tensor_tensor(
                            dBu[:, :], du[:, :], bb_b[bi][:, :], ALU.mult),
                            waits=[l1, (ve, dv2)] + ([(ve, s1_prev)] if n > 0 else []))
                        s1 = bb.op("vector", lambda e: e.tensor_tensor_scan(
                            ss[:, :], dA[:, :], dBu[:, :], 0.0, ALU.mult, ALU.add),
                            waits=[(act, a_dA), (pl, g1)] + ([(pl, g2_prev)] if n > 0 else []))
                        s1_prev = s1
                        s2 = bb.op("vector", lambda e, bi=bi: e.tensor_tensor(
                            ss[:, :], ss[:, :], cc_b[bi][:, :], ALU.mult), waits=[l2])
                        g2 = bb.op("gpsimd", lambda e: e.tensor_tensor(
                            yac[:, :], yac[:, :], ss[:, :], ALU.add),
                            waits=[(ve, s2)] + ([(ve, yz)] if n == 0 else []))
                        g2_prev = g2
                        ld_bc[n] = (g2, s2)
                    f1 = bb.op("vector", lambda e, dt=dt: e.scalar_tensor_tensor(
                        yac[:, :], xct[:, :], dcol[:, dt:dt + 1], yac[:, :],
                        ALU.mult, ALU.add), waits=[(pl, prog.chain["gpsimd"].v)])
                    f2 = bb.op("scalar", lambda e: e.activation(zl[:, :], zl[:, :], AF.Silu),
                               waits=[ldz])
                    f3 = bb.op("vector", lambda e: e.tensor_tensor(
                        yout[:, :], yac[:, :], zl[:, :], ALU.mult), waits=[(act, f2)])
                    bb.dma("sync", y_stage[dt * 128:(dt + 1) * 128, :], yout[:, :],
                           waits=[(ve, f3)])

        with prog.scope():
            with prog.block() as bb:
                wout, _ = load_w_bf16(bb, prog, w["W_out"], name="wout")
            out_proj_residual(prog, act_stage=y_stage, n_act=DI, wo_tiles=wout,
                              x_hbm=x_hbm, xout_hbm=xout_hbm,
                              extra_add_hbm=pre_stage)

import math as _math
from concourse.bass_utils import run_bass_kernel_spmd

N_GROUPS = 1

_SUBS = [
    ("l0a", "L0aa"), ("l0f1", "L0ff"), ("l0b", "L0ab"), ("l0f2", "L0ff"),
    ("l1a", "L1aa"), ("l1f1", "L1ff"), ("l1b", "L1ab"), ("l1f2", "L1ff"),
    ("l2a", "L2aa"), ("l2f1", "L2ff"), ("l2b", "L2ab"), ("l2f2", "L2ff"),
    ("l3a", "L3aa"), ("l3f1", "L3ff"), ("l3b", "L3ab"), ("l3f2", "L3ff"),
]


def _bucket_np(d):
    NB, MD = 32, 128
    nb = NB // 2
    rel = (d > 0).astype(np.int64) * nb
    rp = np.abs(d)
    mx = nb // 2
    small = rp < mx
    rps = np.maximum(rp, 1)
    large = mx + (np.log(rps / mx) / _math.log(MD / mx) * (nb - mx)).astype(np.int64)
    large = np.minimum(large, nb - 1)
    return rel + np.where(small, rp, large)


def _col(v, nb):
    return np.ascontiguousarray(np.asarray(v, np.float32).reshape(nb, 128).T)


def _row(v):
    return np.asarray(v, np.float32).reshape(1, -1)


def _tp(a):
    return np.ascontiguousarray(np.asarray(a).T)


def _host_weights(I):
    E_ = 1024
    W = {}
    H = 16
    start = 2.0 ** (-2.0 ** (-(_math.log2(H) - 3)))
    slopes = np.array([start * start ** i for i in range(H)], np.float32)
    W["L0aa"] = dict(WqT=_tp(I["L0_attn_a_Wq"]), WkT=_tp(I["L0_attn_a_Wk"]),
                     WvT=_tp(I["L0_attn_a_Wv"]), WpT=_tp(I["L0_attn_a_Wp"]),
                     BqT=_tp(I["L0_attn_a_Bq"]), BkT=_tp(I["L0_attn_a_Bk"]),
                     BvT=_tp(I["L0_attn_a_Bv"]),
                     AqT=_tp(I["L0_attn_a_Aq"]), AkT=_tp(I["L0_attn_a_Ak"]),
                     AvT=_tp(I["L0_attn_a_Av"]), slopes=slopes[None, :])
    d = dict(gate_wT=_tp(I["L0_ffn_gate_w"]), gate_bias=_row(I["L0_ffn_gate_bias"]),
             sw1T=_tp(I["L0_ffn_sw1"]), sw3T=_tp(I["L0_ffn_sw3"]),
             sw2T=_tp(I["L0_ffn_sw2"]))
    for e in range(8):
        d[f"w1T_{e}"] = _tp(I["L0_ffn_w1"][e])
        d[f"w3T_{e}"] = _tp(I["L0_ffn_w3"][e])
        d[f"w2T_{e}"] = _tp(I["L0_ffn_w2"][e])
    W["L0ff"] = d
    W["L0ab"] = dict(Mq=np.ascontiguousarray(I["L0_attn_b_Mq"]),
                     Mk=np.ascontiguousarray(I["L0_attn_b_Mk"]),
                     Mv=np.ascontiguousarray(I["L0_attn_b_Mv"]))
    p_ = np.arange(64)
    W["L1aa"] = dict(W_writeT=_tp(I["L1_attn_a_W_write"]),
                     W_readT=_tp(I["L1_attn_a_W_read"]),
                     decay=np.asarray(I["L1_attn_a_decay"], np.float32).reshape(1, 1),
                     log_alpha=np.asarray(I["L1_attn_a_log_alpha"], np.float32).reshape(1, 1),
                     diffT=np.maximum(p_[None, :] - 1 - p_[:, None], 0).astype(np.float32),
                     causT=(p_[None, :] > p_[:, None]).astype(np.float32),
                     rev64=(63.0 - p_).astype(np.float32)[:, None])
    d = dict(gw1T=_tp(I["L1_ffn_gw1"]), gb1=_row(I["L1_ffn_gb1"]),
             gw2T=_tp(I["L1_ffn_gw2"]), gb2=_row(I["L1_ffn_gb2"]))
    e8 = np.zeros((8, 128), np.float32)
    for e in range(8):
        e8[e, e * 16:(e + 1) * 16] = 1
    d["exp8"] = e8
    for pfx in "qkvo":
        d[pfx + "WmT"] = _tp(I[f"L1_ffn_{pfx}Wm"])
        d[pfx + "VallT"] = np.ascontiguousarray(
            I[f"L1_ffn_{pfx}V"].transpose(2, 0, 1).reshape(E_, 128))
        d[pfx + "S"] = np.ascontiguousarray(
            np.asarray(I[f"L1_ffn_{pfx}S"], np.float32).reshape(128, 1))
        d[pfx + "UrT"] = np.ascontiguousarray(
            I[f"L1_ffn_{pfx}U"].transpose(0, 2, 1).reshape(128, E_))
    d["qb"] = _col(I["L1_ffn_qb"], 8)
    d["kb"] = _col(I["L1_ffn_kb"], 8)
    d["vbrow"] = _row(I["L1_ffn_vb"])
    d["obrow"] = _row(I["L1_ffn_ob"])
    W["L1ff"] = d
    qkvT = _tp(I["L1_attn_b_w_qkv"])
    dd = np.clip(4222 - np.arange(8448), -4095, 4095)
    bk = _bucket_np(dd)
    oht = np.zeros((32, 8448), np.float32)
    oht[bk, np.arange(8448)] = 1.0
    W["L1ab"] = dict(w_qT=np.ascontiguousarray(qkvT[:, :E_]),
                     w_kT=np.ascontiguousarray(qkvT[:, E_:2 * E_]),
                     w_vT=np.ascontiguousarray(qkvT[:, 2 * E_:]),
                     w_projT=_tp(I["L1_attn_b_w_proj"]),
                     b_proj=_row(I["L1_attn_b_b_proj"]),
                     rel_table=np.ascontiguousarray(I["L1_attn_b_rel_bias_table"]),
                     OHT=oht)
    W["L2aa"] = dict(W_q=np.ascontiguousarray(I["L2_attn_a_W_q"]),
                     W_k=np.ascontiguousarray(I["L2_attn_a_W_k"]),
                     W_v=np.ascontiguousarray(I["L2_attn_a_W_v"]),
                     W_o=np.ascontiguousarray(I["L2_attn_a_W_o"]),
                     bq_row=_row(I["L2_attn_a_b_q"]), bk_row=_row(I["L2_attn_a_b_k"]),
                     bo_row=_row(I["L2_attn_a_b_o"]), bv_col=_col(I["L2_attn_a_b_v"], 8))
    W["L2ff"] = dict(w_upT=_tp(I["L2_ffn_w_up"]), w_downT=_tp(I["L2_ffn_w_down"]))
    qw = I["L2_attn_b_qkv_w"]
    qb = I["L2_attn_b_qkv_b"]
    W["L2ab"] = dict(wq=np.ascontiguousarray(qw[:, :E_]),
                     wk=np.ascontiguousarray(qw[:, E_:2 * E_]),
                     wv=np.ascontiguousarray(qw[:, 2 * E_:]),
                     bq_col=_col(qb[:E_], 8), bk_col=_col(qb[E_:2 * E_], 8),
                     bv_row=_row(qb[2 * E_:]),
                     rq_w=np.ascontiguousarray(I["L2_attn_b_rq_w"]),
                     rq_b_col=_col(I["L2_attn_b_rq_b"], 2),
                     rkv_w=np.ascontiguousarray(I["L2_attn_b_rkv_w"]),
                     rkv_b_col=_col(I["L2_attn_b_rkv_b"], 4),
                     rout_w=np.ascontiguousarray(I["L2_attn_b_rout_w"]),
                     rout_b_col=_col(I["L2_attn_b_rout_b"], 8),
                     out_w=np.ascontiguousarray(I["L2_attn_b_out_w"]),
                     out_b_row=_row(I["L2_attn_b_out_b"]),
                     gl=np.asarray(I["L2_attn_b_gl_w"], np.float32),
                     gr=np.asarray(I["L2_attn_b_gr_w"], np.float32))
    cw = np.zeros((128, 64), np.float32)
    conv_w = I["L3_attn_a_conv_w"]
    for ct in range(16):
        for k in range(4):
            cw[:, ct * 4 + k] = conv_w[ct * 128:(ct + 1) * 128, 0, k]
    xp = I["L3_attn_a_W_xproj"]
    W["L3aa"] = dict(W_in=np.ascontiguousarray(I["L3_attn_a_W_in"]),
                     lng_col=_col(I["L3_attn_a_ln_g"], 8),
                     lnb_col=_col(I["L3_attn_a_ln_b"], 8),
                     convw_col=cw, convb_col=_col(I["L3_attn_a_conv_b"], 16),
                     Wx_dt=np.ascontiguousarray(xp[:, :64]),
                     Wx_B=np.ascontiguousarray(xp[:, 64:80]),
                     Wx_C=np.ascontiguousarray(xp[:, 80:96]),
                     W_dt=np.ascontiguousarray(I["L3_attn_a_W_dt"]),
                     bdt_col=_col(I["L3_attn_a_b_dt"], 16),
                     A_log=np.ascontiguousarray(I["L3_attn_a_A_log"]),
                     D_col=_col(I["L3_attn_a_D_param"], 16),
                     W_out=np.ascontiguousarray(I["L3_attn_a_W_out"]))
    d = dict(router_w=np.ascontiguousarray(I["L3_ffn_router_w"]))
    rmsc = np.zeros((128, 16 * 8), np.float32)
    for e in range(14):
        rmsc[:, e * 8:(e + 1) * 8] = I["L3_ffn_rms_w"][e].reshape(8, 128).T
    for s_ in range(2):
        rmsc[:, (14 + s_) * 8:(15 + s_) * 8] = I["L3_ffn_rms_w_s"][s_].reshape(8, 128).T
    d["rms_cols"] = rmsc
    for e in range(14):
        d[f"w1_{e}"] = np.ascontiguousarray(I["L3_ffn_w1"][e])
        d[f"w2_{e}"] = np.ascontiguousarray(I["L3_ffn_w2"][e])
    for s_ in range(2):
        d[f"w1s_{s_}"] = np.ascontiguousarray(I["L3_ffn_w1_s"][s_])
        d[f"w2s_{s_}"] = np.ascontiguousarray(I["L3_ffn_w2_s"][s_])
    W["L3ff"] = d
    W["L3ab"] = dict(Wq=np.ascontiguousarray(I["L3_attn_b_Wq"]),
                     Wl=np.ascontiguousarray(I["L3_attn_b_Wl"]),
                     Wk=np.ascontiguousarray(I["L3_attn_b_Wk"]),
                     Wv=np.ascontiguousarray(I["L3_attn_b_Wv"]),
                     Wo=np.ascontiguousarray(I["L3_attn_b_Wo"]),
                     bq=_col(I["L3_attn_b_bq"], 8), bl=_col(I["L3_attn_b_bl"], 2),
                     bk=_col(I["L3_attn_b_bk"], 8),
                     bv=_row(I["L3_attn_b_bv"]), bo=_row(I["L3_attn_b_bo"]))
    return W


_SUB_FN = {"L0aa": sub_l0_attn_a, "L0ff": sub_l0_ffn, "L0ab": sub_l0_attn_b,
           "L1aa": sub_l1_attn_a, "L1ff": sub_l1_ffn, "L1ab": sub_l1_attn_b,
           "L2aa": sub_l2_attn_a, "L2ff": sub_l2_ffn, "L2ab": sub_l2_attn_b,
           "L3aa": sub_l3_attn_a, "L3ff": sub_l3_ffn, "L3ab": sub_l3_attn_b}


def _build_group(sub_slice, warrs_by_kind):
    p = Prog()
    x_in = p.dram("x_in", [T, E], F32, kind="ExternalInput")
    x_out = p.dram("x_out", [T, E], F32, kind="ExternalOutput")
    xa = p.dram("x_mid_a", [T, E], F32)
    xb = p.dram("x_mid_b", [T, E], F32)
    with p.block() as bb:
        load_consts(bb, p)
    n = len(sub_slice)
    kind_handles = {}
    for i, (uid, kind) in enumerate(sub_slice):
        src = x_in if i == 0 else (xa if i % 2 == 1 else xb)
        dst = x_out if i == n - 1 else (xa if i % 2 == 0 else xb)
        if kind not in kind_handles:
            kind_handles[kind] = {
                k: p.dram(f"{kind}_{k}", list(arr.shape), F32, kind="ExternalInput")
                for k, arr in warrs_by_kind[kind].items()}
        _SUB_FN[kind](p, src, dst, kind_handles[kind], uid)
    with p.block() as bb:
        pass
    p.emit_all()
    return p


_PROG_CACHE = {}


def kernel(**inputs):
    I = {k: np.asarray(v) for k, v in inputs.items()}
    x_full = np.asarray(I["L0_attn_a_x"], np.float32)
    B = x_full.shape[0]
    Wk_ = _host_weights(I)
    csts = host_consts()
    groups = []
    gsz = len(_SUBS) // N_GROUPS
    for g in range(N_GROUPS):
        groups.append(_SUBS[g * gsz:(g + 1) * gsz])

    cur = [np.ascontiguousarray(x_full[b]) for b in range(B)]
    for gi, grp in enumerate(groups):
        key = ("g", gi, N_GROUPS)
        if key not in _PROG_CACHE:
            _PROG_CACHE[key] = _build_group(grp, Wk_)
        p = _PROG_CACHE[key]
        in_maps = []
        for b in range(B):
            m = dict(x_in=cur[b])
            m.update(csts)
            for uid, kind in set(grp):
                for k, arr in Wk_[kind].items():
                    m[f"{kind}_{k}"] = arr
            in_maps.append(m)
        res = run_bass_kernel_spmd(p.nc, in_maps, core_ids=list(range(B)))
        cur = [np.asarray(res.results[b]["x_out"]) for b in range(B)]
    return np.stack(cur, axis=0)
